# version 12
# speedup vs baseline: 1.2971x; 1.0686x over previous
import numpy as np
import ml_dtypes
from contextlib import ExitStack
import concourse.bass as bass
import concourse.mybir as mybir
from concourse.bass_utils import run_bass_kernel_spmd

F32 = mybir.dt.float32
BF16 = mybir.dt.bfloat16
AF = mybir.ActivationFunctionType
ALU = mybir.AluOpType
AX = mybir.AxisListType
BF = ml_dtypes.bfloat16

ENGINES = ("tensor", "scalar", "vector", "gpsimd", "sync")

T, D, NCH, H, DH = 2048, 1024, 8, 16, 64
QB, NQB, NT, NSEQ = 512, 4, 16, 2
VW = H * 65


class Sem:
    def __init__(self, handle):
        self.h = handle
        self.n = 0


class Prog:
    def __init__(self):
        self.q = {e: [] for e in ENGINES}
        self._lastwait = {e: {} for e in ENGINES}

    def emit(self, eng, fn, waits=(), inc=None, inc_by=1):
        lw = self._lastwait[eng]
        wl = []
        for s, v in waits:
            if v is None or v <= 0:
                continue
            if lw.get(id(s), -1) >= v:
                continue
            wl.append((s, v))
            lw[id(s)] = v
        tok = None
        if inc is not None:
            assert fn is not None, "cannot inc a semaphore without an instruction"
            inc.n += inc_by
            tok = inc.n
        self.q[eng].append((fn, wl, inc, inc_by))
        return tok

    def barrier_all(self, sems):
        snap = [(s, s.n) for s in sems if s.n > 0]
        for e in ENGINES:
            for s, v in snap:
                self.emit(e, None, waits=[(s, v)])

    def build(self, block):
        def runner(engq):
            def _(eng):
                for fn, wl, inc, inc_by in engq:
                    for s, v in wl:
                        eng.wait_ge(s.h, v)
                    if fn is None:
                        continue
                    instr = fn(eng)
                    if inc is not None:
                        instr.then_inc(inc.h, inc_by)
            return _
        block.tensor(runner(self.q["tensor"]))
        block.scalar(runner(self.q["scalar"]))
        block.vector(runner(self.q["vector"]))
        block.gpsimd(runner(self.q["gpsimd"]))
        block.sync(runner(self.q["sync"]))


def consts_a():
    c = {}
    c["c_128x32"] = np.ones((128, 32), dtype=BF)
    c["c_128x32f"] = np.ones((128, 32), dtype=np.float32)
    c["c_1x32"] = np.ones((1, 32), dtype=BF)
    c["c_32x128"] = np.ones((32, 128), dtype=BF)
    c["c_1x64"] = np.ones((1, 64), dtype=BF)
    c["c_1x128"] = np.ones((1, 128), dtype=BF)
    k = np.arange(128)[:, None]
    u = np.arange(QB + 384)[None, :]
    c["tri01"] = (u >= k + 384).astype(BF)
    c["c_32x128d"] = (np.ones((32, 128)) / 32.0).astype(BF)
    sel = np.zeros((16, NCH * 128), dtype=BF)
    for hc in range(NCH):
        sel[2 * hc, hc * 128:hc * 128 + 64] = 1
        sel[2 * hc + 1, hc * 128 + 64:(hc + 1) * 128] = 1
    c["sel16"] = sel
    return c


CONST_SHAPES_A = {
    "c_128x32": ([128, 32], BF16), "c_128x32f": ([128, 32], F32),
    "c_1x32": ([1, 32], BF16), "c_32x128": ([32, 128], BF16),
    "c_1x64": ([1, 64], BF16), "c_1x128": ([1, 128], BF16),
    "tri01": ([128, QB + 384], BF16),
    "c_32x128d": ([32, 128], BF16), "sel16": ([16, NCH * 128], BF16),
}


def emit_a(nc, pr, sems, ctx, dram):
    T, NQB, NT = (globals()["T"], globals()["NQB"], globals()["NT"])
    xin = dram["a_xin"]
    wq, wk, wv = dram["a_wq"], dram["a_wk"], dram["a_wv"]
    bq, bk, bv = dram["a_bq"], dram["a_bk"], dram["a_bv"]
    gcol, gbrow = dram["a_gcol"], dram["a_gbrow"]
    cin = {k: dram[k] for k in CONST_SHAPES_A}
    xa = dram["xab"]
    if True:
        sb = lambda name, shape, dt=BF16: ctx.enter_context(nc.sbuf_tensor("a_" + name, shape, dt))
        ps = lambda name, shape, dt=F32: ctx.enter_context(nc.psum_tensor("a_" + name, shape, dt))
        wq_sb = sb("wq_sb", [128, NCH, D])
        wk_sb = sb("wk_sb", [128, NCH, D])
        wv_sb = sb("wv_sb", [128, NCH, D])
        bq_sb = sb("bq_sb", [128, NCH], F32)
        bk_sb = sb("bk_sb", [128, NCH], F32)
        bv_row = sb("bv_row", [1, D])
        bvrep = sb("bvrep", [32, D])
        gcol_sb = sb("gcol_sb", [128, NCH], F32)
        gb_sb = sb("gb_sb", [2, D])
        cs = {k: sb(k + "_s", sh, dt) for k, (sh, dt) in CONST_SHAPES_A.items()}
        neg50 = sb("neg50", [128, 1], F32)
        epsA = sb("epsA", [32, 1], F32)
        epsB = sb("epsB", [32, 1], F32)

        x_tm = sb("x_tm", [128, 2, D])
        x_blk = sb("x_blk", [128, 2, NCH, QB])
        q_blk = sb("q_blk", [128, NCH, QB])
        k_fm = sb("k_fm", [128, NCH, T])
        v_tm = sb("v_tm", [128, NT * VW])
        attn_blk = sb("attn_blk", [128, NCH, QB])
        praw = sb("praw", [128, 2, QB])
        pbuf = sb("pbuf", [128, 2, QB])
        r_blk = sb("r_blk", [128, NCH, QB])
        t1_blk = attn_blk
        x1_blk = sb("x1_blk", [128, NCH, QB])
        r2f = sb("r2f", [128, QB])
        msum_col = sb("msum_col", [128, NT], F32)
        m_colv = sb("m_colv", [128, NT], F32)
        m_colv2 = sb("m_colv2", [128, NT], F32)
        m_raw = sb("m_raw", [1, T])
        m_row = m_raw

        rdm_row = sb("rdm_row", [1, QB])
        den_tmp = sb("den_tmp", [1, 2, QB], F32)
        den16 = sb("den16", [16, QB], F32)
        rdm16b = sb("rdm16b", [16, QB])
        mrow16 = sb("mrow16", [16, QB])
        row_mu = sb("row_mu", [32, QB], F32)
        row_mu2 = sb("row_mu2", [32, QB], F32)
        row_s2 = sb("row_s2", [32, QB], F32)
        row_rstd = sb("row_rstd", [32, QB], F32)
        row_murstd = sb("row_murstd", [32, QB], F32)
        nurows = sb("nurows", [2, QB])
        mu_bf = sb("mu_bf", [32, QB])
        rstd_bf = sb("rstd_bf", [32, QB])

        p_s = [ps("p_s0", [128, QB]), ps("p_s1", [128, QB])]
        p_o = ps("p_o", [65, QB])
        p_v = ps("p_v", [128, D])
        p_rs = ps("p_rs", [128, QB])
        p_sum = ps("p_sum", [32, QB])
        p_sq = ps("p_sq", [32, QB])
        p_nu = [p_v[:, 0:QB], p_v[:, QB:D]]

        pe, sc, ve, dgp, dgs = (sems[k] for k in ("pe", "sc", "ve", "dgp", "dgs"))
        dgx = [sems["dgx0"], sems["dgx1"]]
        dst_ = [sems["ds0"], sems["ds1"]]
        allsems = list(sems.values())
        E = pr.emit

        for dst, src_ in [(wq_sb, wq), (wk_sb, wk), (wv_sb, wv)]:
            E("gpsimd", lambda g, d=dst, s_=src_: g.dma_start(
                out=d[:], in_=s_.rearrange("c p d -> p c d")), inc=dgp, inc_by=16)
        for dst, src_ in [(bq_sb, bq), (bk_sb, bk), (gcol_sb, gcol),
                          (bv_row, bv), (gb_sb, gbrow)] + \
                         [(cs[k], cin[k]) for k in CONST_SHAPES_A]:
            E("gpsimd", lambda g, d=dst, s_=src_: g.dma_start(out=d[:], in_=s_),
              inc=dgp, inc_by=16)
        LDG = dgp.n
        LDS = LDG
        t_neg50 = E("vector", lambda v: v.memset(neg50[:], -50.0), inc=ve)
        E("vector", lambda v: v.memset(epsA[:], 1e-8), inc=ve)
        t_epsm = E("vector", lambda v: v.memset(epsB[:], 1e-5), inc=ve)
        E("vector", lambda v: v.memset(nurows[:], -1.0), inc=ve)
        t_vones = E("vector", lambda v: v.memset(
            v_tm[:].rearrange("p (t h e) -> p t h e", t=NT, h=H)[:, :, :, 64:65], 1.0),
            inc=ve)
        t_bvev = 0
        for half in range(2):
            w = [(dgp, LDG)] + ([(ve, t_bvev)] if half else [])
            tmm = E("tensor", lambda t, h=half: t.matmul(
                p_rs[0:32, :], cs["c_1x32"][:], bv_row[:, h * QB:(h + 1) * QB],
                start=True, stop=True), waits=w, inc=pe)
            t_bvev = E("vector", lambda v, h=half: v.tensor_scalar(
                bvrep[:, h * QB:(h + 1) * QB], p_rs[0:32, :], 1.0 / 32.0, None, ALU.mult),
                waits=[(pe, tmm)], inc=ve)

        def load_xblk(s, b, slot, t_prev_cons, do_msum):
            for j in range(4):
                i = b * 4 + j
                tsl = j % 2
                w = [t_prev_cons, (dst_[tsl], t_tr_prev[tsl]), (ve, t_ms_prev[tsl])]
                tl = E("gpsimd", lambda g, i=i, tsl=tsl, s=s: g.dma_start(
                    out=x_tm[:, tsl, :], in_=xin[s, i * 128:(i + 1) * 128, :]),
                    waits=w, inc=dgx[tsl], inc_by=16)
                t_tr_prev[tsl] = E("sync", lambda sy, j=j, tsl=tsl, slot=slot: sy.dma_start(
                    out=x_blk[:, slot, :, j * 128:(j + 1) * 128],
                    in_=x_tm[:, tsl, :].rearrange("p (c f) -> p c f", c=NCH),
                    transpose=True), waits=[(dgx[tsl], tl), t_prev_cons],
                    inc=dst_[tsl], inc_by=16)
                if do_msum:
                    t_ms_prev[tsl] = E("vector", lambda v, i=i, tsl=tsl: v.reduce_sum(
                        msum_col[:, i:i + 1], x_tm[:, tsl, :], axis=AX.X),
                        waits=[(dgx[tsl], tl)], inc=ve)
            return [(dst_[0], t_tr_prev[0]), (dst_[1], t_tr_prev[1])]

        for s in range(NSEQ):
            pr.barrier_all(allsems)
            t_tr_prev = [0, 0]
            t_ms_prev = [0, 0]
            t_mrow_ev = 0
            t_bank_free = [0, 0]
            t_vev = 0
            t_psum_free0 = 0
            t_xblk_cons = [(pe, 0), (pe, 0)]
            t_xblk_rdy = [0, 0]
            for b in range(NQB):
                slot = b % 2
                wrdy = load_xblk(s, b, slot, t_xblk_cons[slot], do_msum=True)
                mm = None
                for c in range(NCH):
                    mm = E("tensor", lambda t, c=c, slot=slot: t.matmul(
                        p_sum[:], cs["c_128x32"][:], x_blk[:, slot, c, :],
                        start=(c == 0), stop=(c == NCH - 1)),
                        waits=wrdy + [(dgp, LDG), (ve, t_mrow_ev)], inc=pe)
                t_mrow_ev = E("vector", lambda v, b=b: v.tensor_copy(
                    m_raw[0:1, b * QB:(b + 1) * QB], p_sum[0:1, :]),
                    waits=[(pe, mm)], inc=ve)
                for c in range(NCH):
                    bi = c % 2
                    mm = None
                    for kc in range(NCH):
                        mm = E("tensor", lambda t, c=c, kc=kc, bi=bi, slot=slot: t.matmul(
                            p_s[bi][:], wk_sb[:, kc, c * 128:(c + 1) * 128],
                            x_blk[:, slot, kc, :],
                            start=(kc == 0), stop=(kc == NCH - 1)),
                            waits=[(sc, t_bank_free[bi])] + wrdy, inc=pe)
                    t_bank_free[bi] = E("scalar", lambda sce, c=c, b=b, bi=bi: sce.activation(
                        k_fm[:, c, b * QB:(b + 1) * QB], p_s[bi][:], AF.Relu,
                        bias=bk_sb[:, c:c + 1], scale=1.0),
                        waits=[(pe, mm), (dgp, LDS)], inc=sc)
                for j in range(4):
                    i = b * 4 + j
                    mm = None
                    for kc in range(NCH):
                        for half in range(2):
                            mm = E("tensor", lambda t, j=j, kc=kc, hf=half, slot=slot: t.matmul(
                                p_v[:, hf * QB:(hf + 1) * QB],
                                x_blk[:, slot, kc, j * 128:(j + 1) * 128],
                                wv_sb[:, kc, hf * QB:(hf + 1) * QB],
                                start=(kc == 0), stop=False),
                                waits=[(sc, t_vev), (ve, t_bvev)], inc=pe)
                    for half in range(2):
                        mm = E("tensor", lambda t, hf=half: t.matmul(
                            p_v[:, hf * QB:(hf + 1) * QB], cs["c_32x128"][:],
                            bvrep[:, hf * QB:(hf + 1) * QB], start=False, stop=True), inc=pe)
                    t_vev = E("scalar", lambda sce, i=i: sce.activation(
                        v_tm[:, i * VW:(i + 1) * VW].rearrange(
                            "p (h e) -> p h e", h=H)[:, :, 0:64],
                        p_v[:].rearrange("p (h e) -> p h e", h=H), AF.Relu),
                        waits=[(pe, mm), (ve, t_vones)], inc=sc)
                t_xblk_cons[slot] = (pe, mm)
            t_psum_free0 = t_mrow_ev
            tmp = E("scalar", lambda c: c.activation(m_colv2[:], msum_col[:], AF.Sign),
                    waits=[(ve, t_ms_prev[0]), (ve, t_ms_prev[1])], inc=sc)
            t_mcol = E("scalar", lambda c: c.activation(m_colv[:], m_colv2[:], AF.Abs),
                       waits=[(sc, tmp)], inc=sc)
            t_mb = E("vector", lambda v: v.tensor_scalar(
                m_colv2[:], m_colv[:], 100.0, -150.0, ALU.mult, ALU.add),
                waits=[(sc, t_mcol)], inc=ve)
            t_mrow_s = 0
            for b in range(NQB):
                tmp = E("scalar", lambda c, b=b: c.activation(
                    rdm_row[:], m_raw[0:1, b * QB:(b + 1) * QB], AF.Sign),
                    waits=[(ve, t_mrow_ev), (sc, t_mrow_s)], inc=sc)
                t_mrow_s = E("scalar", lambda c, b=b: c.activation(
                    m_raw[0:1, b * QB:(b + 1) * QB], rdm_row[:], AF.Abs),
                    waits=[(sc, tmp)], inc=sc)
            t_mrow = t_mrow_s

            t_attn_cons = 0
            t_praw_cons = [(ve, 0), (ve, 0)]
            t_pbuf_cons = [0, 0]
            t_oev = 0
            t_prs_free = (ve, t_bvev)
            t_den16_free = (ve, 0)
            t_dtmp = [0, 0]
            t_dprev = 0
            t_rdm16_free = (ve, 0)
            t_m16_free = (ve, 0)
            t_nu_free = [0, 0]
            t_x1_store = 0
            t_nurows_cons = 0
            t_mubf_cons = 0
            t_rstdbf_cons = 0
            t_r_cons_pe = (pe, 0)
            t_r_cons_sc = (sc, 0)
            t_psum_free = (ve, t_psum_free0)
            t_psq_free = (ve, 0)
            t_rows_free = (ve, 0)
            t_qev = 0
            t_q_cons = 0
            t_am = 0
            for qb in range(NQB):
                slot = qb % 2
                wrdy = load_xblk(s, qb, slot, t_xblk_cons[slot], do_msum=False)
                pm = E("tensor", lambda t, qb=qb: t.matmul(
                    p_rs[0:64, :], cs["c_1x64"][:],
                    m_raw[0:1, qb * QB:(qb + 1) * QB], start=True, stop=True),
                    waits=[(sc, t_mrow), t_prs_free], inc=pe)
                t_m16 = E("vector", lambda v: v.tensor_copy(
                    mrow16[0:16, :], p_rs[0:16, :]),
                    waits=[(pe, pm), t_m16_free], inc=ve)
                t_prs_free = (ve, t_m16)
                for c in range(NCH):
                    bi = c % 2
                    mm = None
                    for kc in range(NCH):
                        mm = E("tensor", lambda t, c=c, kc=kc, bi=bi, slot=slot: t.matmul(
                            p_s[bi][:], wq_sb[:, kc, c * 128:(c + 1) * 128],
                            x_blk[:, slot, kc, :],
                            start=(kc == 0), stop=(kc == NCH - 1)),
                            waits=[(sc, t_bank_free[bi]), (pe, t_q_cons)] + wrdy, inc=pe)
                    t_bank_free[bi] = E("scalar", lambda sce, c=c, bi=bi: sce.activation(
                        q_blk[:, c, :], p_s[bi][:], AF.Relu,
                        bias=bq_sb[:, c:c + 1], scale=1.0),
                        waits=[(pe, mm), (dgp, LDS)], inc=sc)
                    t_qev = t_bank_free[bi]
                nkt = 4 * qb + 4
                w_po = [(ve, t_oev)]
                for h in range(H):
                    hb = (h % 2) * 64
                    hc = h // 2
                    t_omm = None
                    for kt in range(nkt):
                        bi = kt % 2
                        smm = E("tensor", lambda t, hb=hb, hc=hc, kt=kt, bi=bi: t.matmul(
                            p_s[bi][:], k_fm[hb:hb + 64, hc, kt * 128:(kt + 1) * 128],
                            q_blk[hb:hb + 64, hc, :],
                            start=True, stop=True),
                            waits=[(sc, max(t_bank_free[bi], t_qev))], inc=pe)
                        t_q_cons = smm
                        texp = E("scalar", lambda sce, bi=bi, kt=kt: sce.activation(
                            praw[:, bi, :], p_s[bi][:], AF.Exp,
                            bias=m_colv2[:, kt:kt + 1], scale=0.125),
                            waits=[(pe, smm), t_praw_cons[bi], (ve, t_mb)], inc=sc)
                        t_bank_free[bi] = texp
                        off = kt - 4 * qb
                        if off >= 0:
                            ttri = E("vector", lambda v, bi=bi, off=off: v.tensor_tensor(
                                pbuf[:, bi, :], praw[:, bi, :],
                                cs["tri01"][:, 384 - off * 128:384 - off * 128 + QB],
                                ALU.mult),
                                waits=[(sc, texp), (pe, t_pbuf_cons[bi])], inc=ve)
                            t_praw_cons[bi] = (ve, ttri)
                            rhs = pbuf[:, bi, :]
                            w_omm = [(ve, ttri)]
                        else:
                            rhs = praw[:, bi, :]
                            w_omm = [(sc, texp)]
                        t_omm = E("tensor", lambda t, h=h, kt=kt, rhs=rhs,
                                  first=(kt == 0), last=(kt == nkt - 1): t.matmul(
                            p_o[:], v_tm[:, kt * VW + h * 65: kt * VW + h * 65 + 65],
                            rhs, start=first, stop=last),
                            waits=w_omm + w_po + [(sc, t_vev)], inc=pe)
                        w_po = []
                        if off >= 0:
                            t_pbuf_cons[bi] = t_omm
                        else:
                            t_praw_cons[bi] = (pe, t_omm)
                    d0a = E("vector", lambda v, h=h: v.tensor_copy(
                        den_tmp[0:1, h % 2, :], p_o[64:65, :]),
                        waits=[(pe, t_omm), (dgp, t_dtmp[h % 2])], inc=ve)
                    t_dtmp[h % 2] = E("gpsimd", lambda g, h=h: g.dma_start(
                        out=den16[h:h + 1, :], in_=den_tmp[0:1, h % 2, :]),
                        waits=[(ve, d0a), t_den16_free, (dgp, t_dprev)],
                        inc=dgp, inc_by=16)
                    t_dprev = t_dtmp[h % 2]
                    t_oev = E("vector", lambda v, hb=hb, hc=hc: v.tensor_copy(
                        attn_blk[hb:hb + 64, hc, :], p_o[0:64, :]),
                        waits=[(ve, max(d0a, t_attn_cons, t_am))], inc=ve)
                    w_po = [(ve, t_oev)]
                tse = E("vector", lambda v: v.tensor_scalar(
                    den16[0:16, :], den16[0:16, :], 1.0, 1e-30, ALU.mult, ALU.add),
                    waits=[(dgp, t_dprev), t_rdm16_free], inc=ve)
                rc = E("vector", lambda v: v.reciprocal(
                    den16[0:16, :], den16[0:16, :]),
                    waits=[(ve, tse)], inc=ve)
                t_den16_free = (ve, rc)
                rm = E("vector", lambda v: v.tensor_tensor(
                    rdm16b[0:16, :], den16[0:16, :], mrow16[0:16, :], ALU.mult),
                    waits=[(ve, rc)], inc=ve)
                t_rdm16_free = (ve, rm)
                t_m16_free = (ve, rm)
                bm = None
                for hc2 in range(NCH):
                    bm = E("tensor", lambda t, hc2=hc2: t.matmul(
                        p_rs[:], cs["sel16"][:, hc2 * 128:(hc2 + 1) * 128],
                        rdm16b[0:16, :], start=True, stop=True),
                        waits=[(ve, rm), t_prs_free], inc=pe)
                    t_am = E("vector", lambda v, hc2=hc2: v.tensor_tensor(
                        r_blk[:, hc2, :], attn_blk[:, hc2, :], p_rs[:], ALU.mult),
                        waits=[(pe, bm), t_r_cons_pe, t_r_cons_sc], inc=ve)
                    t_prs_free = (ve, t_am)
                t_r = [0] * NCH
                for c in range(NCH):
                    t_r[c] = E("vector", lambda v, c=c, slot=slot: v.tensor_tensor(
                        r_blk[:, c, :], r_blk[:, c, :],
                        x_blk[:, slot, c, :], ALU.add),
                        waits=[(ve, t_am)] + wrdy, inc=ve)


                def ln_stats(src_blk, t_src):
                    mm1 = None
                    for c in range(NCH):
                        mm1 = E("tensor", lambda t, c=c, sbk=src_blk: t.matmul(
                            p_sum[:], cs["c_128x32"][:], sbk[:, c, :],
                            start=(c == 0), stop=(c == NCH - 1)),
                            waits=[(ve, t_src[c]), t_psum_free], inc=pe)
                    mm2 = 0
                    tsq = 0
                    for c in range(NCH):
                        tsq = E("scalar", lambda sce, c=c, sbk=src_blk: sce.activation(
                            r2f[:], sbk[:, c, :], AF.Square),
                            waits=[(ve, t_src[c]), (pe, mm2)], inc=sc)
                        mm2 = E("tensor", lambda t, c=c: t.matmul(
                            p_sq[:], cs["c_128x32"][:], r2f[:],
                            start=(c == 0), stop=(c == NCH - 1)),
                            waits=[(sc, tsq), t_psq_free], inc=pe)
                    return mm1, mm2, tsq

                def ln_rows(mm1, mm2, eps_ap):
                    a1 = E("scalar", lambda sce: sce.mul(row_mu[:], p_sum[0:32, :], 1.0 / D),
                           waits=[(pe, mm1), t_rows_free], inc=sc)
                    a2 = E("scalar", lambda sce: sce.activation(row_mu2[:], row_mu[:], AF.Square),
                           waits=[(sc, a1)], inc=sc)
                    a3 = E("vector", lambda v: v.scalar_tensor_tensor(
                        row_s2[:], p_sq[0:32, :], 1.0 / D, row_mu2[:], ALU.mult, ALU.subtract),
                        waits=[(pe, mm2), (sc, a2)], inc=ve)
                    a4 = E("scalar", lambda sce, e=eps_ap: sce.activation(
                        row_mu2[:], row_s2[:], AF.Ln, bias=e[:], scale=1.0),
                        waits=[(ve, max(a3, t_epsm))], inc=sc)
                    a5 = E("scalar", lambda sce: sce.activation(
                        row_rstd[:], row_mu2[:], AF.Exp, bias=0.0, scale=-0.5),
                        waits=[(sc, a4)], inc=sc)
                    return a1, a3, a5

                mm1, mm2, tsq1 = ln_stats(r_blk, t_r)
                a_mu, a_s2, a_rstd = ln_rows(mm1, mm2, epsA)
                t_psum_free = (sc, a_mu)
                t_psq_free = (ve, a_s2)
                b1 = E("vector", lambda v: v.tensor_tensor(
                    row_murstd[:], row_mu[:], row_rstd[:], ALU.mult),
                    waits=[(sc, a_rstd)], inc=ve)
                b2 = E("vector", lambda v: v.tensor_copy(nurows[0:1, :], row_murstd[0:1, :]),
                       waits=[(ve, b1), (pe, t_nurows_cons)], inc=ve)
                b3 = E("vector", lambda v: v.tensor_copy(rstd_bf[:], row_rstd[:]),
                       waits=[(sc, a_rstd), (pe, t_rstdbf_cons)], inc=ve)
                b4 = E("tensor", lambda t: t.matmul(
                    p_rs[:], cs["c_32x128d"][:], rstd_bf[0:32, :],
                    start=True, stop=True),
                    waits=[(ve, b3), t_prs_free], inc=pe)
                t_rows_free = (ve, b3)
                t_pass2 = [0] * NCH
                nmm = 0
                for c in range(NCH):
                    nuh = c % 2
                    nmm = E("tensor", lambda t, c=c, nuh=nuh: t.matmul(
                        p_nu[nuh], gb_sb[:, c * 128:(c + 1) * 128], nurows[:],
                        start=True, stop=True),
                        waits=[(ve, max(b2, t_nu_free[nuh])), (sc, t_vev)], inc=pe)
                    t_nurows_cons = nmm
                    tp1 = E("vector", lambda v, c=c: v.scalar_tensor_tensor(
                        t1_blk[:, c, :], r_blk[:, c, :], gcol_sb[:, c:c + 1],
                        p_rs[:], ALU.mult, ALU.mult),
                        waits=[(ve, t_r[c]), (pe, b4)], inc=ve)
                    t_pass2[c] = E("vector", lambda v, c=c, nuh=nuh: v.tensor_tensor(
                        x1_blk[:, c, :], t1_blk[:, c, :], p_nu[nuh], ALU.subtract),
                        waits=[(ve, tp1), (pe, nmm), (dgs, t_x1_store)], inc=ve)
                    t_nu_free[nuh] = t_pass2[c]
                    t_prs_free = (ve, tp1)
                t_v = [0] * NCH
                for c in range(NCH):
                    t_v[c] = E("vector", lambda v, c=c, slot=slot: v.tensor_tensor(
                        r_blk[:, c, :], x1_blk[:, c, :],
                        x_blk[:, slot, c, :], ALU.add),
                        waits=[(ve, t_pass2[c]), (pe, mm1), (sc, tsq1)], inc=ve)
                t_xblk_cons[slot] = (ve, t_v[NCH - 1])
                mm1b, mm2b, tsq2 = ln_stats(r_blk, t_v)
                t_r_cons_pe = (pe, mm1b)
                t_r_cons_sc = (sc, tsq2)
                a_mu2, a_s2b, a_rstd2 = ln_rows(mm1b, mm2b, epsB)
                t_psum_free = (sc, a_mu2)
                t_psq_free = (ve, a_s2b)
                c1 = E("vector", lambda v: v.tensor_copy(mu_bf[:], row_mu[:]),
                       waits=[(sc, a_mu2), (pe, t_mubf_cons)], inc=ve)
                c2 = E("vector", lambda v: v.tensor_copy(rstd_bf[:], row_rstd[:]),
                       waits=[(sc, a_rstd2), (pe, b4)], inc=ve)
                t_rows_free = (ve, c2)
                c3 = E("tensor", lambda t: t.matmul(
                    p_nu[0], cs["c_32x128d"][:], mu_bf[0:32, :],
                    start=True, stop=True),
                    waits=[(ve, max(c1, t_nu_free[0]))], inc=pe)
                c4 = E("tensor", lambda t: t.matmul(
                    p_nu[1], cs["c_32x128d"][:], rstd_bf[0:32, :],
                    start=True, stop=True),
                    waits=[(ve, max(c2, t_nu_free[1]))], inc=pe)
                t_mubf_cons = c3
                t_rstdbf_cons = c4
                t_p4 = [0] * NCH
                for c in range(NCH):
                    t_p4[c] = E("vector", lambda v, c=c: v.tensor_tensor(
                        t1_blk[:, c, :], r_blk[:, c, :], p_nu[0], ALU.subtract),
                        waits=[(ve, t_v[c]), (pe, max(c3, mm2b))], inc=ve)
                t_p5 = [0] * NCH
                for c in range(NCH):
                    t_p5[c] = E("vector", lambda v, c=c: v.tensor_tensor(
                        x1_blk[:, c, :], t1_blk[:, c, :], p_nu[1], ALU.mult),
                        waits=[(ve, t_p4[c]), (pe, c4)], inc=ve)
                t_nu_free[0] = t_p4[NCH - 1]
                t_nu_free[1] = t_p5[NCH - 1]
                t_attn_cons = t_p5[NCH - 1]
                t_x1_store = E("gpsimd", lambda g, qb=qb, s=s: g.dma_start(
                    out=xa[s][:, :, qb * QB:(qb + 1) * QB].rearrange("c p t -> p c t"),
                    in_=x1_blk[:]), waits=[(ve, t_p5[NCH - 1])], inc=dgs, inc_by=16)

        pr.barrier_all(allsems)


def prep_a(inputs, n_cores=8):
    x = np.asarray(inputs["L0_attn_x"])
    Wq = np.asarray(inputs["L0_attn_Wq"])
    Wk = np.asarray(inputs["L0_attn_Wk"])
    Wv = np.asarray(inputs["L0_attn_Wv"])
    bq = np.asarray(inputs["L0_attn_bq"])
    bk = np.asarray(inputs["L0_attn_bk"])
    bv = np.asarray(inputs["L0_attn_bv"])
    gamma = np.asarray(inputs["L0_attn_gamma"])
    beta = np.asarray(inputs["L0_attn_beta"])
    consts = consts_a()
    wq_l = np.ascontiguousarray(Wq.reshape(NCH, 128, D))
    wk_l = np.ascontiguousarray(Wk.reshape(NCH, 128, D))
    wv_l = np.ascontiguousarray(Wv.reshape(NCH, 128, D))
    bq_l = np.ascontiguousarray(bq.reshape(NCH, 128).T)
    bk_l = np.ascontiguousarray(bk.reshape(NCH, 128).T)
    bv_l = bv.reshape(1, D)
    gcol_l = np.ascontiguousarray(gamma.reshape(NCH, 128).T)
    gbrow_l = np.stack([gamma, beta], axis=0)
    shared = dict(wq=wq_l, wk=wk_l, wv=wv_l, bq=bq_l, bk=bk_l, bv=bv_l,
                  gcol=gcol_l, gbrow=gbrow_l, **consts)
    in_maps = []
    for core in range(n_cores):
        m = dict(shared)
        m["xin"] = np.ascontiguousarray(x[core * NSEQ:(core + 1) * NSEQ])
        in_maps.append(m)
    return in_maps


CONST_SHAPES_B = {
    "c_128x32": ([128, 32], BF16), "c_1x32": ([1, 32], BF16),
    "c_32x256": ([32, 256], BF16), "c_1x128f": ([1, 128], F32),
}


def consts_b():
    c = consts_a()
    out = {k: c[k] for k in CONST_SHAPES_B if k in c}
    out["c_32x256"] = np.ones((32, 256), dtype=BF)
    out["c_1x128f"] = np.ones((1, 128), dtype=np.float32)
    return out


def emit_b(nc, pr, sems, ctx, dram):
    T, NQB, NT = (globals()["T"], globals()["NQB"], globals()["NT"])
    HS = 2048
    NHC = HS // 128
    RG = T // 8
    xi = dram["xab"]
    sw1, sw3, sw2 = dram["b_sw1"], dram["b_sw3"], dram["b_sw2"]
    w1, w2 = dram["b_w1"], dram["b_w2"]
    sb1c, sb3c, sb2c = dram["b_sb1c"], dram["b_sb3c"], dram["b_sb2c"]
    b1c, b2r = dram["b_b1c"], dram["b_b2r"]
    cin = {k: dram[k] for k in CONST_SHAPES_B}
    xo = dram["xbc"]
    temb = dram["b_temb"]
    if True:
        sb = lambda name, shape, dt=BF16: ctx.enter_context(nc.sbuf_tensor("b_" + name, shape, dt))
        ps = lambda name, shape, dt=F32: ctx.enter_context(nc.psum_tensor("b_" + name, shape, dt))
        arena = sb("arena", [128, 3 * NCH * HS])
        sw1_v = arena[:, 0:NCH * HS].rearrange("p (c d) -> p c d", c=NCH)
        sw3_v = arena[:, NCH * HS:2 * NCH * HS].rearrange("p (c d) -> p c d", c=NCH)
        sw2_v = arena[:, 2 * NCH * HS:2 * NCH * HS + NHC * D].rearrange(
            "p (c d) -> p c d", c=NHC)
        EW = NCH * D
        wexp = [None, None]
        for es in range(2):
            wexp[es] = (arena[:, es * 2 * EW:es * 2 * EW + EW].rearrange(
                "p (c d) -> p c d", c=NCH),
                arena[:, es * 2 * EW + EW:(es + 1) * 2 * EW].rearrange(
                "p (c d) -> p c d", c=NCH))
        x1_s = sb("x1_s", [128, NCH, T])
        facc = sb("facc", [128, NCH, T])
        h_s = sb("h_s", [128, NHC, QB])
        he_sb = sb("he_sb", [128, NCH, RG])
        silu_sb = sb("silu_sb", [128, 1, QB])
        v_blk = h_s[:, 8:16, :]
        t_blk = h_s[:, 0:8, :]
        r2f = sb("r2f", [128, QB])
        b2rep = sb("b2rep", [32, D])
        b2row = sb("b2row", [1, D])
        sb1_sb = sb("sb1_sb", [128, NHC], F32)
        sb3_sb = sb("sb3_sb", [128, NHC], F32)
        sb2_sb = sb("sb2_sb", [128, NCH], F32)
        sb2h = sb("sb2h", [128, NCH], F32)
        b1_sb = sb("b1_sb", [128, 2, NCH], F32)
        cs = {k: sb(k + "_s", sh, dt) for k, (sh, dt) in CONST_SHAPES_B.items()}
        epsB = sb("epsB", [1, 1], F32)
        row_mu = sb("row_mu", [1, QB], F32)
        rec_rowA = row_mu
        row_mu2 = sb("row_mu2", [1, QB], F32)
        den_row = row_mu2
        row_s2 = sb("row_s2", [1, QB], F32)
        rec_rowB = row_s2
        row_rstd = sb("row_rstd", [1, QB], F32)
        pA = ps("pA", [128, QB])
        pB = ps("pB", [128, QB])
        pC = ps("pC", [128, QB])
        pD = ps("pD", [128, QB])
        p_sum = ps("p_sum", [32, QB])
        p_sq = ps("p_sq", [32, QB])
        p_r1 = ps("p_r1", [128, QB])
        p_r2 = ps("p_r2", [128, QB])

        pe, sc, ve, dgp, dgx, dgs = (sems[k] for k in
                                     ("pe", "sc", "ve", "dgp", "dgx", "dgs"))
        dgw = [sems["dgw0"], sems["dgw1"]]
        allsems = list(sems.values())
        E = pr.emit
        E("gpsimd", lambda g: g.dma_start(out=sb("temb_sb", [1, 16], F32)[:],
                                          in_=temb[0:1, 0:16]), inc=dgp, inc_by=16)

        for dst, src_ in [(sb1_sb, sb1c), (sb3_sb, sb3c), (sb2_sb, sb2c)] + \
                         [(cs[k], cin[k]) for k in CONST_SHAPES_B]:
            E("gpsimd", lambda g, d=dst, s_=src_: g.dma_start(out=d[:], in_=s_),
              inc=dgp, inc_by=16)
        LDGP = dgp.n
        E("vector", lambda v: v.memset(epsB[:], 1e-5), inc=ve)
        t_sb2h = E("scalar", lambda c: c.mul(sb2h[:], sb2_sb[:], 1.0),
                   waits=[(dgp, LDGP)], inc=sc)

        for s in range(NSEQ):
            pr.barrier_all(allsems)
            for dst, src_ in [(sw1_v, sw1), (sw3_v, sw3), (sw2_v, sw2)]:
                E("gpsimd", lambda g, d=dst, s_=src_: g.dma_start(
                    out=d, in_=s_.rearrange("c p d -> p c d")), inc=dgp, inc_by=16)
            LDGP = dgp.n
            t_x1 = E("gpsimd", lambda g, s=s: g.dma_start(
                out=x1_s[:], in_=xi[s].rearrange("c p t -> p c t")),
                inc=dgx, inc_by=16)
            XRDY = (dgx, t_x1)

            t_bankG = [0, 0]
            t_bankGs = [0, 0]
            t_bankY = [0, 0]
            t_hs_cons = 0
            t_facc_st = 0
            for tb in range(NQB):
                t_h = [0] * NHC
                for hc in range(NHC):
                    bk = [pA, pB][hc % 2]
                    mm = None
                    for kc in range(NCH):
                        mm = E("tensor", lambda t, hc=hc, kc=kc, tb=tb, bk=bk: t.matmul(
                            bk[:], sw1_v[:, kc, hc * 128:(hc + 1) * 128],
                            x1_s[:, kc, tb * QB:(tb + 1) * QB],
                            start=(kc == 0), stop=(kc == NCH - 1)),
                            waits=[XRDY, (dgp, LDGP), (ve, t_bankG[hc % 2]),
                                   (sc, t_bankGs[hc % 2])], inc=pe)
                    t_sil = E("scalar", lambda sce, hc=hc, bk=bk: sce.activation(
                        silu_sb[:, 0, :], bk[:], AF.Silu,
                        bias=sb1_sb[:, hc:hc + 1], scale=1.0),
                        waits=[(pe, mm), (ve, max(t_bankG[0], t_bankG[1]))], inc=sc)
                    t_bankGs[hc % 2] = t_sil
                    mm = None
                    for kc in range(NCH):
                        mm = E("tensor", lambda t, hc=hc, kc=kc, tb=tb, bk=bk: t.matmul(
                            bk[:], sw3_v[:, kc, hc * 128:(hc + 1) * 128],
                            x1_s[:, kc, tb * QB:(tb + 1) * QB],
                            start=(kc == 0), stop=(kc == NCH - 1)),
                            waits=[(sc, t_sil)], inc=pe)
                    t_h[hc] = E("vector", lambda v, hc=hc, bk=bk: v.scalar_tensor_tensor(
                        h_s[:, hc, :], bk[:], sb3_sb[:, hc:hc + 1],
                        silu_sb[:, 0, :], ALU.add, ALU.mult),
                        waits=[(pe, mm), (sc, t_sil), (pe, t_hs_cons)], inc=ve)
                    t_bankG[hc % 2] = t_h[hc]
                for oc in range(NCH):
                    bk = [pC, pD][oc % 2]
                    mm = None
                    for hc in range(NHC):
                        mm = E("tensor", lambda t, oc=oc, hc=hc, bk=bk: t.matmul(
                            bk[:], sw2_v[:, hc, oc * 128:(oc + 1) * 128],
                            h_s[:, hc, :],
                            start=(hc == 0), stop=(hc == NHC - 1)),
                            waits=[(ve, max(t_h[hc], t_bankY[oc % 2]))], inc=pe)
                    t_bankY[oc % 2] = E("vector", lambda v, oc=oc, tb=tb, bk=bk: v.tensor_scalar(
                        facc[:, oc, tb * QB:(tb + 1) * QB], bk[:],
                        sb2h[:, oc:oc + 1], 0.5, ALU.add, ALU.mult),
                        waits=[(pe, mm), (sc, t_sb2h)], inc=ve)
                t_hs_cons = mm

            t_wld = [0, 0]
            t_wcons = [t_hs_cons, t_hs_cons]
            t_b2ld = [0, 0]
            t_he_cons = 0
            t_bankE = [0, 0]
            t_acc = 0
            t_b2rep_ev = 0
            t_b2rep_cons = 0
            t_b2row_cons = 0
            t_ywar = [t_bankY[0], t_bankY[1]]
            for e in range(8):
                es = e % 2
                E("gpsimd", lambda g, e=e, es=es: g.dma_start(
                    out=wexp[es][0], in_=w1[e].rearrange("c p d -> p c d")),
                    waits=[(pe, t_wcons[es])], inc=dgw[es], inc_by=16)
                E("gpsimd", lambda g, e=e, es=es: g.dma_start(
                    out=wexp[es][1], in_=w2[e].rearrange("c p d -> p c d")),
                    inc=dgw[es], inc_by=16)
                E("gpsimd", lambda g, e=e: g.dma_start(
                    out=b2row[:], in_=b2r[e]),
                    waits=[(pe, t_b2row_cons)], inc=dgw[es], inc_by=16)
                t_wall = E("gpsimd", lambda g, e=e, es=es: g.dma_start(
                    out=b1_sb[:, es, :], in_=b1c[e]),
                    waits=[(sc, t_bankE[es])], inc=dgw[es], inc_by=16)
                t_wld[es] = t_wall
                t_b2ld[es] = t_wall
                t_b1ld = t_wall
                t_bc = t_b2rep_ev
                for half in range(2):
                    mmb = E("tensor", lambda t, h=half: t.matmul(
                        p_r1[0:32, :], cs["c_1x32"][:], b2row[:, h * QB:(h + 1) * QB],
                        start=True, stop=True),
                        waits=[(dgw[es], t_b2ld[es]), (ve, t_bc),
                               (pe, t_b2rep_cons)], inc=pe)
                    t_b2row_cons = mmb
                    t_bc = E("vector", lambda v, h=half: v.tensor_scalar(
                        b2rep[:, h * QB:(h + 1) * QB], p_r1[0:32, :], 1.0 / 32.0,
                        None, ALU.mult), waits=[(pe, mmb)], inc=ve)
                t_b2rep_ev = t_bc
                for r in ((3 * e) % 8, (3 * e + 3) % 8):
                    for hr in range(NCH):
                        bk = [pA, pB][hr % 2]
                        mm = None
                        for kc in range(NCH):
                            mm = E("tensor", lambda t, hr=hr, kc=kc, r=r, es=es, bk=bk: t.matmul(
                                bk[:, 0:RG],
                                wexp[es][0][:, kc, hr * 128:(hr + 1) * 128],
                                x1_s[:, kc, :].rearrange(
                                    "p (n eight) -> p n eight", eight=8)[:, :, r],
                                start=(kc == 0), stop=(kc == NCH - 1)),
                                waits=[(dgw[es], t_wld[es]), XRDY,
                                       (sc, t_bankE[hr % 2]), (ve, t_bankG[hr % 2])],
                                inc=pe)
                        t_bankE[hr % 2] = E("scalar", lambda sce, hr=hr, es=es, bk=bk:
                                            sce.activation(
                            he_sb[:, hr, :], bk[:, 0:RG], AF.Gelu,
                            bias=b1_sb[:, es, hr:hr + 1], scale=1.0),
                            waits=[(pe, mm), (dgw[es], t_b1ld), (pe, t_he_cons)], inc=sc)
                        t_bankG[hr % 2] = 0
                    for oc in range(NCH):
                        bk = [pC, pD][oc % 2]
                        mm = None
                        for hr in range(NCH):
                            mm = E("tensor", lambda t, oc=oc, hr=hr, es=es, bk=bk: t.matmul(
                                bk[:, 0:RG],
                                wexp[es][1][:, hr, oc * 128:(oc + 1) * 128],
                                he_sb[:, hr, :],
                                start=(hr == 0), stop=False),
                                waits=[(sc, t_bankE[hr % 2]), (ve, max(t_ywar[oc % 2], t_acc))],
                                inc=pe)
                        mm = E("tensor", lambda t, oc=oc, bk=bk: t.matmul(
                            bk[:, 0:RG], b2rep[:, oc * 128:(oc + 1) * 128],
                            cs["c_32x256"][:, 0:RG], start=False, stop=True),
                            waits=[(ve, t_b2rep_ev)], inc=pe)
                        t_b2rep_cons = mm
                        t_acc = E("vector", lambda v, oc=oc, r=r, bk=bk: v.scalar_tensor_tensor(
                            facc[:, oc, :].rearrange(
                                "p (n eight) -> p n eight", eight=8)[:, :, r],
                            bk[:, 0:RG], 0.25,
                            facc[:, oc, :].rearrange(
                                "p (n eight) -> p n eight", eight=8)[:, :, r],
                            ALU.mult, ALU.add),
                            waits=[(pe, mm), (ve, t_bankY[oc % 2])], inc=ve)
                        t_bankY[oc % 2] = 0
                        t_ywar[oc % 2] = 0
                        t_he_cons = mm
                t_wcons[es] = mm

            t_psum_free = (ve, 0)
            t_psq_free = (ve, 0)
            t_rows_free = (ve, 0)
            t_mubf_cons = 0
            t_rstdbf_cons = 0
            t_v_cons_pe = (pe, 0)
            t_v_cons_sc = (sc, 0)
            t_st = 0
            for tb in range(NQB):
                t_v = [0] * NCH
                for c in range(NCH):
                    t_v[c] = E("vector", lambda v, c=c, tb=tb: v.tensor_tensor(
                        v_blk[:, c, :], x1_s[:, c, tb * QB:(tb + 1) * QB],
                        facc[:, c, tb * QB:(tb + 1) * QB], ALU.add),
                        waits=[(ve, t_acc), t_v_cons_pe, t_v_cons_sc,
                               (dgs, t_st)], inc=ve)
                mm1 = None
                for c in range(NCH):
                    mm1 = E("tensor", lambda t, c=c: t.matmul(
                        p_sum[:], cs["c_128x32"][:], v_blk[:, c, :],
                        start=(c == 0), stop=(c == NCH - 1)),
                        waits=[(ve, t_v[c]), t_psum_free], inc=pe)
                mm2 = 0
                tsq = 0
                for c in range(NCH):
                    tsq = E("scalar", lambda sce, c=c: sce.activation(
                        r2f[:], v_blk[:, c, :], AF.Square),
                        waits=[(ve, t_v[c]), (pe, mm2)], inc=sc)
                    mm2 = E("tensor", lambda t, c=c: t.matmul(
                        p_sq[:], cs["c_128x32"][:], r2f[:],
                        start=(c == 0), stop=(c == NCH - 1)),
                        waits=[(sc, tsq), t_psq_free], inc=pe)
                a1 = E("scalar", lambda sce: sce.mul(row_mu[:], p_sum[0:1, :], 1.0 / D),
                       waits=[(pe, mm1), t_rows_free], inc=sc)
                a2 = E("scalar", lambda sce: sce.activation(row_mu2[:], row_mu[:], AF.Square),
                       waits=[(sc, a1)], inc=sc)
                a3 = E("vector", lambda v: v.scalar_tensor_tensor(
                    row_s2[:], p_sq[0:1, :], 1.0 / D, row_mu2[:], ALU.mult, ALU.subtract),
                    waits=[(pe, mm2), (sc, a2)], inc=ve)
                a4 = E("scalar", lambda sce: sce.activation(
                    row_mu2[:], row_s2[:], AF.Ln, bias=epsB[:], scale=1.0),
                    waits=[(ve, a3)], inc=sc)
                a5 = E("scalar", lambda sce: sce.activation(
                    row_rstd[:], row_mu2[:], AF.Exp, bias=0.0, scale=-0.5),
                    waits=[(sc, a4)], inc=sc)
                t_psum_free = (sc, a1)
                t_psq_free = (ve, a3)
                c3 = E("tensor", lambda t: t.matmul(
                    p_r1[:], cs["c_1x128f"][:], row_mu[:], start=True, stop=True),
                    waits=[(sc, a5)], inc=pe)
                c4 = E("tensor", lambda t: t.matmul(
                    p_r2[:], cs["c_1x128f"][:], row_rstd[:], start=True, stop=True),
                    waits=[(sc, a5)], inc=pe)
                t_rows_free = (pe, c4)
                t_p4 = [0] * NCH
                for c in range(NCH):
                    t_p4[c] = E("vector", lambda v, c=c: v.tensor_tensor(
                        t_blk[:, c, :], v_blk[:, c, :], p_r1[:], ALU.subtract),
                        waits=[(ve, t_v[c]), (pe, max(c3, mm2))], inc=ve)
                t_p5 = [0] * NCH
                for c in range(NCH):
                    t_p5[c] = E("vector", lambda v, c=c: v.tensor_tensor(
                        v_blk[:, c, :], t_blk[:, c, :], p_r2[:], ALU.mult),
                        waits=[(ve, t_p4[c]), (pe, c4)], inc=ve)
                t_v_cons_pe = (pe, mm1)
                t_v_cons_sc = (sc, tsq)
                t_st = E("gpsimd", lambda g, tb=tb, s=s: g.dma_start(
                    out=xo[s][:, :, tb * QB:(tb + 1) * QB].rearrange("c p t -> p c t"),
                    in_=v_blk[:]), waits=[(ve, t_p5[NCH - 1])], inc=dgs, inc_by=16)
                t_v_cons_ve_store = t_st
                t_v_cons_pe = (pe, mm1)
        pr.barrier_all(allsems)


def prep_b(inputs, n_cores=8, xin_override=None):
    sw1 = np.asarray(inputs["L0_ffn_sw1"])
    sw3 = np.asarray(inputs["L0_ffn_sw3"])
    sw2 = np.asarray(inputs["L0_ffn_sw2"])
    sb1 = np.asarray(inputs["L0_ffn_sb1"])
    sb3 = np.asarray(inputs["L0_ffn_sb3"])
    sb2 = np.asarray(inputs["L0_ffn_sb2"])
    w1 = np.asarray(inputs["L0_ffn_w1"])
    w2 = np.asarray(inputs["L0_ffn_w2"])
    b1 = np.asarray(inputs["L0_ffn_b1"])
    b2 = np.asarray(inputs["L0_ffn_b2"])
    HS = 2048
    shared = dict(
        sw1=np.ascontiguousarray(sw1.reshape(NCH, 128, HS)),
        sw3=np.ascontiguousarray(sw3.reshape(NCH, 128, HS)),
        sw2=np.ascontiguousarray(sw2.reshape(HS // 128, 128, D)),
        w1=np.ascontiguousarray(w1.reshape(8, NCH, 128, D)),
        w2=np.ascontiguousarray(w2.reshape(8, NCH, 128, D)),
        sb1c=np.ascontiguousarray(sb1.reshape(HS // 128, 128).T),
        sb3c=np.ascontiguousarray(sb3.reshape(HS // 128, 128).T),
        sb2c=np.ascontiguousarray(sb2.reshape(NCH, 128).T),
        b1c=np.ascontiguousarray(b1.reshape(8, NCH, 128).transpose(0, 2, 1)),
        b2r=np.ascontiguousarray(b2.reshape(8, 1, D)),
        **consts_b())
    in_maps = []
    for core in range(n_cores):
        m = dict(shared)
        m["xi"] = np.ascontiguousarray(xin_override[core])
        in_maps.append(m)
    return in_maps


CONST_SHAPES_C = {
    "c_128x32": ([128, 32], BF16), "c_1x128f": ([1, 128], F32),
    "c_tri128": ([128, 128], BF16),
    "c_E2": ([128, 2], BF16),
}


def consts_c():
    c = {}
    c["c_128x32"] = np.ones((128, 32), dtype=BF)
    c["c_1x128f"] = np.ones((1, 128), dtype=np.float32)
    k = np.arange(128)[:, None]
    j = np.arange(128)[None, :]
    c["c_tri128"] = (j >= k).astype(BF)
    e2 = np.zeros((128, 2), dtype=BF)
    e2[0:64, 0] = 1
    e2[64:128, 1] = 1
    c["c_E2"] = e2
    return c


def emit_c(nc, pr, sems, ctx, dram):
    T, NQB, NT = (globals()["T"], globals()["NQB"], globals()["NT"])
    NU = T // 128
    NCK = T // 64
    xi = dram["xbc"]
    wq, wk, wv, wo = dram["c_wq"], dram["c_wk"], dram["c_wv"], dram["c_wo"]
    bo = dram["c_bo"]
    cin = {k: dram[k] for k in CONST_SHAPES_C}
    xo = dram["xcd"]
    if True:
        sb = lambda name, shape, dt=BF16: ctx.enter_context(nc.sbuf_tensor("c_" + name, shape, dt))
        ps = lambda name, shape, dt=F32: ctx.enter_context(nc.psum_tensor("c_" + name, shape, dt))
        arena = sb("arena", [128, 2 * NCH * D])
        wslot = [arena[:, 0:NCH * D].rearrange("p (c d) -> p c d", c=NCH),
                 arena[:, NCH * D:2 * NCH * D].rearrange("p (c d) -> p c d", c=NCH)]
        x_tm = sb("x_tm", [128, D])
        x2b = sb("x2b", [128, NCH, QB])
        q_fm = sb("q_fm", [128, NCH, T])
        kl_fm = sb("kl_fm", [128, 2, T])
        kg_tm = sb("kg_tm", [128, NT, 768])
        v_tm = sb("v_tm", [128, NT * 1024])
        eq = sb("eq", [128, 6, T])
        q2f = eq[:].rearrange("p c t -> p (c t)")[:, 6 * T - QB:6 * T]
        elu2 = q2f
        attn_fm = sb("attn_fm", [128, NCH, T])
        ctx_sb = sb("ctx_sb", [128, 6, 64])
        rdm_sb = ctx_sb[0:64, 0:2, :].rearrange("p a b -> p (a b)")
        NCKl = T // 64
        scr_f32 = sb("scr_f32", [128, max(12 * NCKl, 384)], F32)
        ctx_acc1 = scr_f32[:, 0:384].rearrange("p (g c) -> p g c", g=6)
        ctx_acc = [ctx_acc1, ctx_acc1]
        bkc = scr_f32[:, 0:6 * NCKl].rearrange("p (g c) -> p g c", g=6)
        bkp = scr_f32[:, 6 * NCKl:12 * NCKl].rearrange("p (g c) -> p g c", g=6)
        rbk = sb("rbk", [128, 6, NCKl + 1])
        elu1 = sb("elu1", [128, QB])
        bacc = sb("bacc", [1, 64])
        rowb1 = sb("rowb1", [1, 1], F32)
        rowb2 = sb("rowb2", [1, 1], F32)
        rowb3 = sb("rowb3", [1, 1], F32)
        bias_loc = sb("bias_loc", [128, 1], F32)
        bias_glob = sb("bias_glob", [128, 1], F32)
        bo_sb = sb("bo_sb", [128, NCH], F32)

        cs = {k: sb(k + "_s", sh, dt) for k, (sh, dt) in CONST_SHAPES_C.items()}
        epsB = sb("epsB", [1, 1], F32)
        row_mu = sb("row_mu", [1, QB], F32)
        rec_rowA = row_mu
        row_mu2 = sb("row_mu2", [1, QB], F32)
        den_row = row_mu2
        row_s2 = sb("row_s2", [1, QB], F32)
        rec_rowB = row_s2
        row_rstd = sb("row_rstd", [1, QB], F32)
        r2f = elu1
        if T >= 2048:
            eqf = eq[:].rearrange("p c t -> p (c t)")
            r_blk = eqf[:, 0:NCH * QB].rearrange("p (c t) -> p c t", c=NCH)
            t_blk = eqf[:, NCH * QB:2 * NCH * QB].rearrange("p (c t) -> p c t", c=NCH)
        else:
            r_blk = sb("r_blk", [128, NCH, QB])
            t_blk = sb("t_blk", [128, NCH, QB])

        p_s = [ps("p_s0", [128, QB]), ps("p_s1", [128, QB])]
        p_big = ps("p_big", [128, 1024])
        p_ctx = p_big[:, 0:384]
        p_at = p_big[:, 512:896]
        p_bk = p_big[:, 896:908].rearrange("p (g two) -> p g two", two=2)
        p_loc = p_big[:, 0:128]
        p_rs = ps("p_rs", [128, QB])
        p_sum = ps("p_sum", [32, QB])
        p_sq = ps("p_sq", [32, QB])

        pe, sc, ve, dgp, dgs = (sems[k] for k in ("pe", "sc", "ve", "dgp", "dgs"))
        dgw = [sems["dgw0"], sems["dgw1"]]
        dgx = [sems["dgx0"], sems["dgx1"]]
        dst_ = [sems["ds0"], sems["ds1"]]
        allsems = list(sems.values())
        E = pr.emit

        for dst, src_ in [(bo_sb, bo)] + [(cs[k], cin[k]) for k in CONST_SHAPES_C]:
            E("gpsimd", lambda g, d=dst, s_=src_: g.dma_start(out=d[:], in_=s_),
              inc=dgp, inc_by=16)
        LDGP = dgp.n
        E("vector", lambda v: v.memset(epsB[:], 1e-5), inc=ve)
        t_bacc0 = E("vector", lambda v: v.memset(bacc[:], 0.0), inc=ve)

        t_w = [0, 0]
        t_wcons = [0, 0]
        t_tr_prev = [0, 0]
        t_ms_prev = [0, 0]

        def load_w(slot, mat):
            t_w[slot] = E("gpsimd", lambda g, sl=slot, m=mat: g.dma_start(
                out=wslot[sl], in_=m.rearrange("c p d -> p c d")),
                waits=[(pe, t_wcons[slot])], inc=dgw[slot], inc_by=16)
            return t_w[slot]

        def load_xb(s, tb, t_prev_cons):
            tl = E("gpsimd", lambda g, tb=tb, s=s: g.dma_start(
                out=x2b[:], in_=xi[s][:, :, tb * QB:(tb + 1) * QB].rearrange(
                    "c p t -> p c t")),
                waits=[t_prev_cons], inc=dgx[tb % 2], inc_by=16)
            return [(dgx[tb % 2], tl)]

        for s in range(NSEQ):
            pr.barrier_all(allsems)
            t_tr_prev = [0, 0]
            load_w(0, wq)
            load_w(1, wk)
            t_bank = [0, 0]
            t_bank_ve = [0, 0]
            t_xcons = (pe, 0)
            t_qcons = 0
            t_klcons = 0
            t_kgcons = 0
            t_vcons = 0
            for tb in range(NQB):
                wrdy = load_xb(s, tb, t_xcons)
                mmlast = None
                for oc in range(NCH):
                    bi = oc % 2
                    mm = None
                    for kc in range(NCH):
                        mm = E("tensor", lambda t, oc=oc, kc=kc, bi=bi: t.matmul(
                            p_s[bi][:], wslot[0][:, kc, oc * 128:(oc + 1) * 128],
                            x2b[:, kc, :], start=(kc == 0), stop=(kc == NCH - 1)),
                            waits=wrdy + [(dgw[0], t_w[0]), (sc, t_bank[bi]),
                                          (ve, t_bank_ve[bi])], inc=pe)
                    t_bank[bi] = E("scalar", lambda sce, oc=oc, tb=tb, bi=bi: sce.copy(
                        q_fm[:, oc, tb * QB:(tb + 1) * QB], p_s[bi][:]),
                        waits=[(pe, mm)], inc=sc)
                    t_lastq = mm
                for oc in range(2):
                    bi = oc % 2
                    mm = None
                    for kc in range(NCH):
                        mm = E("tensor", lambda t, oc=oc, kc=kc, bi=bi: t.matmul(
                            p_s[bi][:], wslot[1][:, kc, oc * 128:(oc + 1) * 128],
                            x2b[:, kc, :], start=(kc == 0), stop=(kc == NCH - 1)),
                            waits=[(dgw[1], t_w[1]), (sc, t_bank[bi]),
                                   (ve, t_bank_ve[bi])], inc=pe)
                    t_bank[bi] = E("scalar", lambda sce, oc=oc, tb=tb, bi=bi: sce.copy(
                        kl_fm[:, oc, tb * QB:(tb + 1) * QB], p_s[bi][:]),
                        waits=[(pe, mm)], inc=sc)
                for j in range(4):
                    i = tb * 4 + j
                    mm0 = mm1 = None
                    for kc in range(NCH):
                        mm0 = E("tensor", lambda t, j=j, kc=kc: t.matmul(
                            p_s[0][:], x2b[:, kc, j * 128:(j + 1) * 128],
                            wslot[1][:, kc, 256:768],
                            start=(kc == 0), stop=(kc == NCH - 1)),
                            waits=[(sc, t_bank[0]), (ve, t_bank_ve[0])], inc=pe)
                        mm1 = E("tensor", lambda t, j=j, kc=kc: t.matmul(
                            p_s[1][:, 0:256], x2b[:, kc, j * 128:(j + 1) * 128],
                            wslot[1][:, kc, 768:1024],
                            start=(kc == 0), stop=(kc == NCH - 1)),
                            waits=[(sc, t_bank[1]), (ve, t_bank_ve[1])], inc=pe)
                    for gsl, (bank, width, c0) in enumerate(
                            [(p_s[0], QB, 0), (p_s[1], 256, QB)]):
                        e1 = E("vector", lambda v, bank=bank, w=width: v.tensor_scalar(
                            elu1[:, 0:w], bank[:, 0:w], 0.0, None, ALU.min),
                            waits=[(pe, mm1)], inc=ve)
                        e2 = E("scalar", lambda sce, w=width: sce.activation(
                            elu2[:, 0:w], elu1[:, 0:w], AF.Exp, bias=0.0, scale=1.0),
                            waits=[(ve, e1)], inc=sc)
                        tok = E("vector", lambda v, i=i, bank=bank, w=width, c0=c0:
                                v.scalar_tensor_tensor(
                            kg_tm[:, i, c0:c0 + w], bank[:, 0:w], 0.0,
                            elu2[:, 0:w], ALU.max, ALU.add),
                            waits=[(sc, e2), (pe, t_kgcons)], inc=ve)
                        t_bank_ve[gsl] = tok
                        t_bank[gsl] = e2
                    mmlast = mm1
                t_xcons = (pe, mmlast if mmlast is not None else 0)
            t_wcons[0] = t_lastq
            t_wcons[1] = mmlast
            t_nq = 0
            idx = 0
            t_red = t_bacc0
            for srcbuf, nch_, tok_rdy in [(q_fm, NCH, None), (kl_fm, 2, None)]:
                for c in range(nch_):
                    for tb in range(NQB):
                        tsq = E("scalar", lambda sce, c=c, tb=tb, sbuf=srcbuf: sce.activation(
                            q2f[:], sbuf[:, c, tb * QB:(tb + 1) * QB], AF.Square),
                            waits=[(sc, t_bank[0]), (sc, t_bank[1]), (pe, t_nq),
                                   (ve, t_bank_ve[0]), (ve, t_bank_ve[1])], inc=sc)
                        mm = E("tensor", lambda t: t.matmul(
                            p_sum[:], cs["c_128x32"][:], q2f[:],
                            start=True, stop=True),
                            waits=[(sc, tsq), (ve, t_red)], inc=pe)
                        t_nq = mm
                        t_red = E("vector", lambda v, idx=idx: v.reduce_max(
                            bacc[0:1, idx:idx + 1], p_sum[0:1, :], axis=AX.X),
                            waits=[(pe, mm)], inc=ve)
                        idx += 1
            nq_total = idx
            b1 = E("vector", lambda v: v.reduce_max(
                rowb1[:], bacc[0:1, 0:NCH * NQB], axis=AX.X),
                waits=[(ve, t_red)], inc=ve)
            b2 = E("vector", lambda v: v.reduce_max(
                rowb2[:], bacc[0:1, NCH * NQB:nq_total], axis=AX.X),
                waits=[(ve, t_red)], inc=ve)
            l1 = E("scalar", lambda sce: sce.activation(rowb1[:], rowb1[:], AF.Ln)
                   if False else sce.activation(rowb3[:], rowb1[:], AF.Ln),
                   waits=[(ve, b1)], inc=sc)
            l2 = E("scalar", lambda sce: sce.activation(
                rec_rowB[0:1, 0:1], rowb2[:], AF.Ln), waits=[(ve, b2)], inc=sc)
            l3 = E("vector", lambda v: v.tensor_tensor(
                rowb1[:], rowb3[:], rec_rowB[0:1, 0:1], ALU.add),
                waits=[(sc, l1), (sc, l2)], inc=ve)
            l4 = E("scalar", lambda sce: sce.activation(rowb1[:], rowb1[:], AF.Exp)
                   if False else sce.activation(rowb2[:], rowb1[:], AF.Exp,
                                                bias=0.0, scale=0.5),
                   waits=[(ve, l3)], inc=sc)
            l5 = E("vector", lambda v: v.tensor_scalar(
                rowb2[:], rowb2[:], -0.125, None, ALU.mult)
                if False else v.tensor_scalar(rowb1[:], rowb2[:], -0.125, None, ALU.mult),
                waits=[(sc, l4)], inc=ve)
            gq = E("scalar", lambda sce: sce.activation(rowb2[:], rowb3[:], AF.Exp,
                                                        bias=0.0, scale=0.5),
                   waits=[(ve, l5)], inc=sc)
            gq2 = E("vector", lambda v: v.tensor_scalar(
                rowb3[:], rowb2[:], -1.0, None, ALU.mult),
                waits=[(sc, gq), (ve, l5)], inc=ve)
            bm1 = E("tensor", lambda t: t.matmul(
                p_rs[:, 0:1], cs["c_1x128f"][:], rowb1[:], start=True, stop=True),
                waits=[(ve, l5)], inc=pe)
            bl = E("vector", lambda v: v.tensor_copy(bias_loc[:], p_rs[:, 0:1]),
                   waits=[(pe, bm1)], inc=ve)
            bm2 = E("tensor", lambda t: t.matmul(
                p_rs[:, 0:1], cs["c_1x128f"][:], rowb3[:], start=True, stop=True),
                waits=[(ve, max(gq2, bl))], inc=pe)
            bg = E("vector", lambda v: v.tensor_copy(bias_glob[:], p_rs[:, 0:1]),
                   waits=[(pe, bm2)], inc=ve)
            t_prs_free = bg

            load_w(0, wv)
            t_vev = 0
            t_xcons2 = t_xcons
            for tb in range(NQB):
                wrdy = load_xb(s, tb, t_xcons2)
                for j in range(4):
                    i = tb * 4 + j
                    mm = None
                    for kc in range(NCH):
                        for half in range(2):
                            mm = E("tensor", lambda t, j=j, kc=kc, hf=half: t.matmul(
                                p_big[:, hf * QB:(hf + 1) * QB],
                                x2b[:, kc, j * 128:(j + 1) * 128],
                                wslot[0][:, kc, hf * QB:(hf + 1) * QB],
                                start=(kc == 0), stop=(kc == NCH - 1)),
                                waits=wrdy + [(dgw[0], t_w[0]), (sc, t_vev),
                                              (ve, 0)], inc=pe)
                    t_vev = E("scalar", lambda sce, i=i: sce.copy(
                        v_tm[:, i * 1024:(i + 1) * 1024], p_big[:]),
                        waits=[(pe, mm)], inc=sc)
                t_xcons2 = (pe, mm)
                t_wcons[0] = mm
            load_w(1, wo)

            t_praw_cons = 0
            t_pb_cons = 0
            t_elu2_cons = 0
            t_rdm_cons = 0
            t_attnw = 0
            for h in range(4):
                hb = (h % 2) * 64
                hcl = h // 2
                for u in range(NU):
                    parts = ([u - 1] if u > 0 else []) + [u]
                    olast = None
                    dlast = None
                    for pi, kt in enumerate(parts):
                        bi = kt % 2
                        smm = E("tensor", lambda t, hb=hb, hcl=hcl, kt=kt, u=u, bi=bi:
                                t.matmul(
                            p_s[bi][:, 0:128],
                            kl_fm[hb:hb + 64, hcl, kt * 128:(kt + 1) * 128],
                            q_fm[hb:hb + 64, hcl, u * 128:(u + 1) * 128],
                            start=True, stop=True),
                            waits=[(sc, t_bank[bi]), (ve, t_bank_ve[bi])], inc=pe)
                        t_qcons = smm
                        ex = E("scalar", lambda sce, bi=bi: sce.activation(
                            elu1[:, 0:128], p_s[bi][:, 0:128], AF.Exp,
                            bias=bias_loc[:], scale=0.125),
                            waits=[(pe, smm), (ve, max(t_praw_cons, bl))], inc=sc)
                        t_bank[bi] = ex
                        t_bank_ve[bi] = 0
                        if kt == u:
                            pm = E("vector", lambda v: v.tensor_tensor(
                                elu2[:, 0:128], elu1[:, 0:128], cs["c_tri128"][:],
                                ALU.mult), waits=[(sc, ex), (pe, t_elu2_cons)], inc=ve)
                            psrc = elu2
                            t_praw_cons = pm
                            pwait = [(ve, pm)]
                        else:
                            psrc = elu1
                            pwait = [(sc, ex)]
                        first, last = (pi == 0), (pi == len(parts) - 1)
                        dlast = E("tensor", lambda t, psrc=psrc, first=first,
                                  last=last: t.matmul(
                            p_big[0:32, 512:640], cs["c_128x32"][:],
                            psrc[:, 0:128], start=first, stop=last),
                            waits=pwait + [(sc, t_vev), (ve, t_pb_cons)], inc=pe)
                        olast = E("tensor", lambda t, h=h, kt=kt, psrc=psrc,
                                  first=first, last=last: t.matmul(
                            p_loc[0:64, :], v_tm[:, kt * 1024 + h * 64:
                                                 kt * 1024 + h * 64 + 64],
                            psrc[:, 0:128], start=first, stop=last), inc=pe)
                        if psrc is elu2:
                            t_elu2_cons = dlast
                    d1 = E("vector", lambda v: v.tensor_scalar(
                        den_row[0:1, 0:128], p_big[0:1, 512:640], 1e-30, None, ALU.add),
                        waits=[(pe, olast)], inc=ve)
                    d2 = E("vector", lambda v: v.reciprocal(
                        rec_rowA[0:1, 0:128], den_row[0:1, 0:128]),
                        waits=[(ve, d1)], inc=ve)
                    d4 = E("tensor", lambda t: t.matmul(
                        p_rs[0:64, 0:128], cs["c_1x128f"][:, 0:64], rec_rowA[0:1, 0:128],
                        start=True, stop=True),
                        waits=[(ve, d2), (ve, t_prs_free), (pe, t_rdm_cons)], inc=pe)
                    d5 = E("vector", lambda v: v.tensor_copy(
                        rdm_sb[:], p_rs[0:64, 0:128]), waits=[(pe, d4)], inc=ve)
                    t_rdm_cons = d4
                    t_attnw = E("vector", lambda v, hb=hb, hcl=hcl, u=u: v.tensor_tensor(
                        attn_fm[hb:hb + 64, hcl, u * 128:(u + 1) * 128],
                        p_loc[0:64, :], rdm_sb[:], ALU.mult),
                        waits=[(ve, d5)], inc=ve)
                    t_pb_cons = t_attnw
                    t_prs_free = d5

            t_eq = [0] * 6
            for pg in range(6):
                tk = 0
                for tb in range(NQB):
                    tk = E("scalar", lambda sce, pg=pg, tb=tb: sce.activation(
                        eq[:, pg, tb * QB:(tb + 1) * QB],
                        q_fm[:, 2 + pg, tb * QB:(tb + 1) * QB],
                        AF.Exp, bias=bias_glob[:], scale=1.0),
                        waits=[(ve, max(bg, t_attnw)), (pe, t_qcons),
                               (pe, t_elu2_cons)], inc=sc)
                t_eq[pg] = tk
            t_bkev = 0
            for i in range(NT):
                mm = None
                for pg in range(6):
                    for rr in range(2):
                        g = 4 + pg * 2 + rr
                        mm = E("tensor", lambda t, i=i, pg=pg, rr=rr, g=g: t.matmul(
                            p_bk[rr * 64:rr * 64 + 64, pg, :],
                            kg_tm[:, i, (g - 4) * 64:(g - 4) * 64 + 64],
                            cs["c_E2"][:], start=True, stop=True),
                            waits=[(ve, t_bank_ve[0]), (ve, t_bank_ve[1]),
                                   (ve, t_bkev)], inc=pe)
                t_bkev = E("vector", lambda v, i=i: v.tensor_copy(
                    bkc[:, :, 2 * i:2 * i + 2], p_bk[:]),
                    waits=[(pe, mm)], inc=ve)
            cur, oth = bkc, bkp
            t_cs = t_bkev
            sh = 1
            while sh < NCK:
                t1 = E("vector", lambda v, cur=cur, oth=oth, sh=sh: v.tensor_tensor(
                    oth[:, :, sh:NCK], cur[:, :, sh:NCK], cur[:, :, 0:NCK - sh],
                    ALU.add), waits=[(ve, t_cs)], inc=ve)
                t2 = E("vector", lambda v, cur=cur, oth=oth, sh=sh: v.tensor_copy(
                    oth[:, :, 0:sh], cur[:, :, 0:sh]),
                    waits=[(ve, t_cs)], inc=ve)
                t_cs = max(t1, t2)
                cur, oth = oth, cur
                sh *= 2
            tnv = E("vector", lambda v: v.memset(rbk[:, :, 0:1], 1.0), waits=[], inc=ve)
            scratch = oth
            ta = E("vector", lambda v, cur=cur, scratch=scratch: v.tensor_scalar(
                scratch[:, :, :], cur[:, :, :], 1e-6, None, ALU.add),
                waits=[(ve, t_cs)], inc=ve)
            tb_ = E("vector", lambda v, scratch=scratch, cur=cur: v.reciprocal(
                cur[:, :, :], scratch[:, :, :]), waits=[(ve, ta)], inc=ve)
            tb2_ = E("vector", lambda v, cur=cur: v.tensor_copy(
                rbk[:, :, 1:NCK + 1], cur[:, :, :]), waits=[(ve, tb_)], inc=ve)
            t_rbk = max(tnv, tb2_)
            t_qp = [0] * 6
            for pg in range(6):
                t_qp[pg] = E("vector", lambda v, pg=pg: v.tensor_tensor(
                    attn_fm[:, 2 + pg, :].rearrange("p (c n) -> p c n", n=64),
                    eq[:, pg, :].rearrange("p (c n) -> p c n", n=64),
                    rbk[:, pg, 0:NCK].broadcast_to([128, NCK, 64]), ALU.mult),
                    waits=[(ve, t_rbk), (sc, t_eq[pg]), (ve, t_attnw)], inc=ve)
            t_den_mm = 0
            t_qprime = [0] * 6
            for pg in range(6):
                tlast = 0
                for tb in range(NQB):
                    mmA = E("tensor", lambda t, pg=pg, tb=tb: t.matmul(
                        p_sum[:], cs["c_128x32"][0:64, :],
                        eq[0:64, pg, tb * QB:(tb + 1) * QB], start=True, stop=True),
                        waits=[(sc, t_eq[pg]), (ve, t_den_ve(0))] if False else
                              [(sc, t_eq[pg])], inc=pe)
                    mmB = E("tensor", lambda t, pg=pg, tb=tb: t.matmul(
                        p_sq[:], cs["c_128x32"][64:128, :],
                        eq[64:128, pg, tb * QB:(tb + 1) * QB], start=True, stop=True),
                        inc=pe)
                    dA = E("vector", lambda v: v.tensor_scalar(
                        den_row[:], p_sum[0:1, :], 1e-30, None, ALU.add),
                        waits=[(pe, mmA)], inc=ve)
                    rA = E("vector", lambda v: v.reciprocal(rec_rowA[:], den_row[:]),
                           waits=[(ve, dA)], inc=ve)
                    dB = E("vector", lambda v: v.tensor_scalar(
                        den_row[:], p_sq[0:1, :], 1e-30, None, ALU.add),
                        waits=[(pe, mmB), (ve, rA)], inc=ve)
                    rB = E("vector", lambda v: v.reciprocal(rec_rowB[:], den_row[:]),
                           waits=[(ve, dB)], inc=ve)
                    bA = E("tensor", lambda t: t.matmul(
                        p_rs[0:64, :], cs["c_1x128f"][:, 0:64], rec_rowA[:],
                        start=True, stop=True),
                        waits=[(ve, rA), (ve, t_prs_free), (pe, t_rdm_cons)], inc=pe)
                    bB = E("tensor", lambda t: t.matmul(
                        p_rs[64:128, :], cs["c_1x128f"][:, 0:64], rec_rowB[:],
                        start=True, stop=True), waits=[(ve, rB)], inc=pe)
                    t_rdm_cons = bB
                    tlast = E("vector", lambda v, pg=pg, tb=tb: v.tensor_tensor(
                        eq[:, pg, tb * QB:(tb + 1) * QB],
                        attn_fm[:, 2 + pg, tb * QB:(tb + 1) * QB],
                        p_rs[:], ALU.mult),
                        waits=[(pe, bB), (ve, t_qp[pg])], inc=ve)
                    t_prs_free = tlast
                    t_den_mm = mmB
                t_qprime[pg] = tlast
            t_ctx0 = E("vector", lambda v: v.memset(ctx_sb[:], 0.0),
                       waits=[(ve, t_attnw)], inc=ve)
            t_ctx1 = E("vector", lambda v: v.memset(ctx_acc[1][:], 0.0),
                       waits=[(ve, t_rbk)], inc=ve)
            t_snap = max(t_ctx0, t_ctx1)
            t_atev = 0
            t_ctxmm = 0
            t_accr = 0
            for c in range(NCK):
                atl = None
                for pg in range(6):
                    for rr in range(2):
                        gb = rr * 64
                        atl = E("tensor", lambda t, pg=pg, gb=gb, c=c: t.matmul(
                            p_at[gb:gb + 64, :].rearrange(
                                "p (g n) -> p g n", n=64)[:, pg, :],
                            ctx_sb[gb:gb + 64, pg, :],
                            eq[gb:gb + 64, pg, c * 64:(c + 1) * 64],
                            start=True, stop=True),
                            waits=[(ve, max(t_snap, t_qprime[pg], t_atev))], inc=pe)
                t_atev = E("vector", lambda v, c=c: v.tensor_copy(
                    attn_fm[:, 2:8, c * 64:(c + 1) * 64],
                    p_at[:].rearrange("p (g n) -> p g n", n=64)),
                    waits=[(pe, atl), (pe, t_ctxmm)], inc=ve)
                if c < NCK - 1:
                    mm = None
                    for pg in range(6):
                        for rr in range(2):
                            g = 4 + pg * 2 + rr
                            i, half = c // 2, (c % 2) * 64
                            mm = E("tensor", lambda t, pg=pg, rr=rr, g=g, i=i,
                                   half=half: t.matmul(
                                p_ctx[rr * 64:rr * 64 + 64, :].rearrange(
                                    "p (g n) -> p g n", n=64)[:, pg, :],
                                kg_tm[half:half + 64, i,
                                      (g - 4) * 64:(g - 4) * 64 + 64],
                                v_tm[half:half + 64, i * 1024 + g * 64:
                                     i * 1024 + g * 64 + 64],
                                start=True, stop=True),
                                waits=[(ve, t_accr), (sc, t_vev)], inc=pe)
                    t_ctxmm = mm
                    cur, prv = ctx_acc[c % 2], ctx_acc[(c + 1) % 2]
                    t_accr = E("vector", lambda v, cur=cur, prv=prv: v.tensor_tensor(
                        cur[:], prv[:], p_ctx[:].rearrange("p (g n) -> p g n", n=64),
                        ALU.add), waits=[(pe, mm), (ve, t_snap)], inc=ve)
                    t_snap = E("vector", lambda v, cur=cur: v.tensor_copy(
                        ctx_sb[:], cur[:]), waits=[(ve, t_accr)], inc=ve)
            t_attn_done = t_atev

            t_psum_free = (ve, 0)
            t_psq_free = (ve, 0)
            t_rows_free = (ve, 0)
            t_v_cons_pe = (pe, 0)
            t_v_cons_sc = (sc, 0)
            t_st = 0
            t_xc = t_xcons2
            for tb in range(NQB):
                wrdy = load_xb(s, tb, t_xc)
                t_r = [0] * NCH
                for oc in range(NCH):
                    bi = oc % 2
                    mm = None
                    for kc in range(NCH):
                        mm = E("tensor", lambda t, oc=oc, kc=kc, tb=tb, bi=bi: t.matmul(
                            p_s[bi][:], wslot[1][:, kc, oc * 128:(oc + 1) * 128],
                            attn_fm[:, kc, tb * QB:(tb + 1) * QB],
                            start=(kc == 0), stop=(kc == NCH - 1)),
                            waits=[(dgw[1], t_w[1]), (ve, t_attn_done),
                                   (sc, t_bank[bi]), (ve, t_bank_ve[bi]),
                                   (ve, t_rblk_war(0))] if False else
                                  [(dgw[1], t_w[1]), (ve, t_attn_done),
                                   (sc, t_bank[bi]), (ve, t_bank_ve[bi])], inc=pe)
                    t_r[oc] = E("vector", lambda v, oc=oc, bi=bi: v.scalar_tensor_tensor(
                        r_blk[:, oc, :], p_s[bi][:], bo_sb[:, oc:oc + 1],
                        x2b[:, oc, :], ALU.add, ALU.add),
                        waits=[(pe, mm), (dgp, LDGP), t_v_cons_pe, t_v_cons_sc,
                               (dgs, t_st)] + wrdy,
                        inc=ve)
                    t_bank_ve[bi] = t_r[oc]
                    t_bank[bi] = 0
                t_xc = (ve, t_r[NCH - 1])
                mm1 = None
                for c in range(NCH):
                    mm1 = E("tensor", lambda t, c=c: t.matmul(
                        p_sum[:], cs["c_128x32"][:], r_blk[:, c, :],
                        start=(c == 0), stop=(c == NCH - 1)),
                        waits=[(ve, t_r[c]), t_psum_free], inc=pe)
                mm2 = 0
                tsq = 0
                for c in range(NCH):
                    tsq = E("scalar", lambda sce, c=c: sce.activation(
                        r2f[:], r_blk[:, c, :], AF.Square),
                        waits=[(ve, t_r[c]), (pe, mm2)], inc=sc)
                    mm2 = E("tensor", lambda t, c=c: t.matmul(
                        p_sq[:], cs["c_128x32"][:], r2f[:],
                        start=(c == 0), stop=(c == NCH - 1)),
                        waits=[(sc, tsq), t_psq_free], inc=pe)
                a1 = E("scalar", lambda sce: sce.mul(row_mu[:], p_sum[0:1, :], 1.0 / D),
                       waits=[(pe, mm1), t_rows_free], inc=sc)
                a2 = E("scalar", lambda sce: sce.activation(row_mu2[:], row_mu[:],
                                                            AF.Square),
                       waits=[(sc, a1)], inc=sc)
                a3 = E("vector", lambda v: v.scalar_tensor_tensor(
                    row_s2[:], p_sq[0:1, :], 1.0 / D, row_mu2[:], ALU.mult,
                    ALU.subtract), waits=[(pe, mm2), (sc, a2)], inc=ve)
                a4 = E("scalar", lambda sce: sce.activation(
                    row_mu2[:], row_s2[:], AF.Ln, bias=epsB[:], scale=1.0),
                    waits=[(ve, a3)], inc=sc)
                a5 = E("scalar", lambda sce: sce.activation(
                    row_rstd[:], row_mu2[:], AF.Exp, bias=0.0, scale=-0.5),
                    waits=[(sc, a4)], inc=sc)
                t_psum_free = (sc, a1)
                t_psq_free = (ve, a3)
                c3 = E("tensor", lambda t: t.matmul(
                    p_rs[:], cs["c_1x128f"][:], row_mu[:], start=True, stop=True),
                    waits=[(sc, a5), (ve, t_prs_free), (pe, t_rdm_cons)], inc=pe)
                c4 = E("tensor", lambda t: t.matmul(
                    p_big[:, 0:QB], cs["c_1x128f"][:], row_rstd[:],
                    start=True, stop=True),
                    waits=[(sc, a5), (ve, t_snap), (ve, t_c4war(0))] if False else
                          [(sc, a5), (ve, t_snap)], inc=pe)
                t_rows_free = (pe, c4)
                t_p4 = [0] * NCH
                for c in range(NCH):
                    t_p4[c] = E("vector", lambda v, c=c: v.tensor_tensor(
                        t_blk[:, c, :], r_blk[:, c, :], p_rs[:], ALU.subtract),
                        waits=[(ve, t_r[c]), (pe, max(c3, mm2))], inc=ve)
                t_prs_free = t_p4[NCH - 1]
                t_p5 = [0] * NCH
                for c in range(NCH):
                    t_p5[c] = E("vector", lambda v, c=c: v.tensor_tensor(
                        r_blk[:, c, :], t_blk[:, c, :], p_big[:, 0:QB], ALU.mult),
                        waits=[(ve, t_p4[c]), (pe, c4), (dgs, t_st)], inc=ve)
                t_snap = t_p5[NCH - 1]
                t_v_cons_pe = (pe, mm1)
                t_v_cons_sc = (sc, tsq)
                t_st = E("gpsimd", lambda g, tb=tb, s=s: g.dma_start(
                    out=xo[s][:, :, tb * QB:(tb + 1) * QB].rearrange("c p t -> p c t"),
                    in_=r_blk[:]), waits=[(ve, t_p5[NCH - 1])], inc=dgs, inc_by=16)
        pr.barrier_all(allsems)


def t_pb_cons_v(x):
    return x


def t_den_cons_s(x):
    return x


def t_rblk_war(x):
    return x


def t_c4war(x):
    return x


def prep_c(inputs, n_cores=8, xin_override=None):
    Wq = np.asarray(inputs["L1_attn_Wq"])
    Wk = np.asarray(inputs["L1_attn_Wk"])
    Wv = np.asarray(inputs["L1_attn_Wv"])
    Wo = np.asarray(inputs["L1_attn_Wo"])
    bo = np.asarray(inputs["L1_attn_bo"])
    shared = dict(
        wq=np.ascontiguousarray(Wq.reshape(NCH, 128, D)),
        wk=np.ascontiguousarray(Wk.reshape(NCH, 128, D)),
        wv=np.ascontiguousarray(Wv.reshape(NCH, 128, D)),
        wo=np.ascontiguousarray(Wo.reshape(NCH, 128, D)),
        bo=np.ascontiguousarray(bo.reshape(NCH, 128).T),
        **consts_c())
    in_maps = []
    for core in range(n_cores):
        m = dict(shared)
        m["xi"] = np.ascontiguousarray(xin_override[core])
        in_maps.append(m)
    return in_maps


IH = 2816
NNC = IH // 128

CONST_SHAPES_D = {
    "c_128x32": ([128, 32], BF16), "c_1x128f": ([1, 128], F32),
}


def consts_d():
    return {"c_128x32": np.ones((128, 32), dtype=BF),
            "c_1x128f": np.ones((1, 128), dtype=np.float32)}


def emit_d(nc, pr, sems, ctx, dram):
    T, NQB, NT = (globals()["T"], globals()["NQB"], globals()["NT"])
    xi = dram["xcd"]
    wg, wu, wd = dram["d_wg"], dram["d_wu"], dram["d_wd"]
    cin = {k: dram[k] for k in CONST_SHAPES_D}
    gT, uT, dT = dram["d_gT"], dram["d_uT"], dram["d_dT"]
    xout = dram["xout"]
    NBW = [512] * 5 + [256]
    NBO = [0, 512, 1024, 1536, 2048, 2560]
    if True:
        sb = lambda name, shape, dt=BF16: ctx.enter_context(nc.sbuf_tensor("d_" + name, shape, dt))
        ps = lambda name, shape, dt=F32: ctx.enter_context(nc.psum_tensor("d_" + name, shape, dt))
        ctx0 = ExitStack()
        sb0 = lambda name, shape, dt=BF16: ctx0.enter_context(nc.sbuf_tensor("d_" + name, shape, dt))
        qsets = []
        for sid in range(3):
            qsets.append({
                "f1": sb("q%df1" % sid, [128, D], F32),
                "f2": sb("q%df2" % sid, [128, D], F32),
                "f3": sb("q%df3" % sid, [128, D], F32),
                "scl": sb("q%dscl" % sid, [128, D // 16], F32),
                "rscl": sb("q%drscl" % sid, [128, D // 16], F32),
                "rln": sb("q%drln" % sid, [128, D // 16], F32),
                "sgb": sb("q%dsgb" % sid, [128, D])})
        qi = sb("qi", [128, D], mybir.dt.int32)
        qb1 = sb("qb1", [128, D])
        qb2 = sb("qb2", [128, D])
        m4c = sb("m4c", [128, 1], F32)
        m6c = sb("m6c", [128, 1], F32)
        p2c = sb("p2c", [128, 1], F32)
        p4c = sb("p4c", [128, 1], F32)
        phc = sb("phc", [128, 1], F32)
        cs = {k: sb(k + "_s", sh, dt) for k, (sh, dt) in CONST_SHAPES_D.items()}
        epsB = sb("epsB", [1, 1], F32)
        wraw = sb0("wraw", [128, 2, D], F32)
        wqb = sb0("wqb", [128, 2, D])
        wqT = sb0("wqT", [128, 2, D])

        p_g = ps("p_g", [128, QB])
        p_u = ps("p_u", [128, QB])
        p_d = [ps("p_d0", [128, QB]), ps("p_d1", [128, QB])]
        p_sum = ps("p_sum", [32, QB])
        p_sq = ps("p_sq", [32, QB])
        p_r1 = ps("p_r1", [128, QB])
        p_r2 = ps("p_r2", [128, QB])

        pe, sc, ve, dgp, dgs = (sems[k] for k in ("pe", "sc", "ve", "dgp", "dgs"))
        gs = sems["gs"]
        dgw = [sems["dgw0"], sems["dgw1"]]
        dgx = [sems["dgx0"], sems["dgx1"]]
        dst_ = [sems["ds0"], sems["ds1"]]
        dgss = [sems["dgs"], sems["dgs1"]]
        allsems = list(sems.values())
        E = pr.emit
        _dma_last = {}

        def dch(sem):
            t = _dma_last.get(id(sem))
            return [(sem, t)] if t else []

        def dset(sem, tok):
            _dma_last[id(sem)] = tok
            return tok

        for dst, src_ in [(cs[k], cin[k]) for k in CONST_SHAPES_D]:
            E("gpsimd", lambda g, d=dst, s_=src_: g.dma_start(out=d[:], in_=s_),
              inc=dgp, inc_by=16)
        LDGP = dgp.n
        E("vector", lambda v: v.memset(epsB[:], 1e-5), inc=ve)
        E("vector", lambda v: v.memset(m4c[:], -4.0), inc=ve)
        E("vector", lambda v: v.memset(m6c[:], -6.0), inc=ve)
        E("vector", lambda v: v.memset(p2c[:], 2.0), inc=ve)
        E("vector", lambda v: v.memset(p4c[:], 4.0), inc=ve)
        t_cst = (ve, E("vector", lambda v: v.memset(phc[:], 0.5 - 2.0 ** -16),
                       inc=ve))

        qshared = {"qi": (sc, 0), "qb": (gs, 0)}
        for S in qsets:
            S.update({"f1_free": (gs, 0), "f2_free": (ve, 0), "f3_free": (ve, 0),
                      "scl_free": (gs, 0), "rscl_free": (gs, 0),
                      "rln_free": (sc, 0), "sgb_free": (gs, 0)})
        qp = {"p1": None, "p2": None, "n": 0}

        def _q_fha(inv):
            S = inv["S"]; W = inv["W"]; NB16 = W // 16
            f2, f3 = S["f2"], S["f3"]
            t0 = E("scalar", lambda c, src=inv["src"], f2=f2, W=W: c.activation(
                f2[:, 0:W], src, AF.Abs), waits=inv["wsrc"] + [S["f2_free"]], inc=sc)
            sgn = E("scalar", lambda c, src=inv["src"], sg_=S["sgb"], W=W:
                    c.activation(sg_[:, 0:W], src, AF.Sign),
                    waits=[S["sgb_free"]], inc=sc)
            cur_off, cur_n = 0, W
            tprev = (sc, t0)
            lev = 0
            while cur_n > NB16:
                nxt = cur_n // 2
                src_lo = (f2 if lev == 0 else f3)
                so = 0 if lev == 0 else cur_off
                do = 0 if lev % 2 == 0 else W // 2
                w = [tprev] + ([S["f3_free"]] if lev == 0 else [])
                tt = E("vector", lambda v, so=so, do=do, nxt=nxt, src_lo=src_lo,
                       f3=f3: v.tensor_tensor(
                    f3[:, do:do + nxt],
                    src_lo[:, so:so + 2 * nxt].rearrange(
                        "p (n two) -> p n two", two=2)[:, :, 0],
                    src_lo[:, so:so + 2 * nxt].rearrange(
                        "p (n two) -> p n two", two=2)[:, :, 1], ALU.max),
                    waits=w, inc=ve)
                tprev = (ve, tt)
                cur_off, cur_n = do, nxt
                lev += 1
            tsc = E("vector", lambda v, cur_off=cur_off, NB16=NB16, f3=f3,
                    sl_=S["scl"]: v.tensor_scalar(
                sl_[:, 0:NB16], f3[:, cur_off:cur_off + NB16],
                1.0 / 6.0, 1e-12, ALU.mult, ALU.max),
                waits=[tprev, S["scl_free"]], inc=ve)
            lnr = E("scalar", lambda c, NB16=NB16, sl_=S["scl"], rl_=S["rln"]:
                    c.activation(rl_[:, 0:NB16], sl_[:, 0:NB16], AF.Ln),
                    waits=[(ve, tsc), S["rln_free"]], inc=sc)
            exr = E("scalar", lambda c, NB16=NB16, rl_=S["rln"], rs_=S["rscl"]:
                    c.activation(rs_[:, 0:NB16], rl_[:, 0:NB16], AF.Exp,
                                 bias=0.0, scale=-1.0),
                    waits=[(sc, lnr), S["rscl_free"]], inc=sc)
            xnv = E("gpsimd", lambda g, W=W, NB16=NB16, f2=f2, f1=S["f1"],
                    rs_=S["rscl"]: g.tensor_tensor(
                f1[:, 0:W].rearrange("p (n b) -> p n b", b=16),
                f2[:, 0:W].rearrange("p (n b) -> p n b", b=16),
                rs_[:, 0:NB16].broadcast_to([128, NB16, 16]), ALU.mult),
                waits=[(sc, exr), S["f1_free"]], inc=gs)
            S["rscl_free"] = (gs, xnv)
            S["rln_free"] = (sc, exr)
            inv["tsc"], inv["xnv"] = tsc, xnv
            inv["src_done"] = (sc, sgn)

        def _q_fhb(inv):
            S, W = inv["S"], inv["W"]
            f1, f2, f3 = S["f1"], S["f2"], S["f3"]
            m1 = E("scalar", lambda c, W=W, f1=f1, f3=f3: c.activation(
                f3[:, 0:W], f1[:, 0:W], AF.Relu, bias=p2c[:], scale=1.0),
                waits=[(gs, inv["xnv"]), (ve, inv["tsc"]), t_cst], inc=sc)
            u1 = E("vector", lambda v, W=W, f1=f1, f2=f2, f3=f3:
                   v.scalar_tensor_tensor(
                f2[:, 0:W], f1[:, 0:W], 2.0, f3[:, 0:W], ALU.mult, ALU.min),
                waits=[(sc, m1)], inc=ve)
            inv["u1"] = u1

        def _q_sh1(inv):
            S, W = inv["S"], inv["W"]
            f1, f2, f3 = S["f1"], S["f2"], S["f3"]
            m2 = E("scalar", lambda c, W=W, f1=f1, f3=f3: c.activation(
                f3[:, 0:W], f1[:, 0:W], AF.Relu, bias=p4c[:], scale=0.5),
                waits=[(ve, inv["u1"]), t_cst], inc=sc)
            u2 = E("vector", lambda v, W=W, f1=f1, f2=f2, f3=f3: v.tensor_tensor(
                f1[:, 0:W], f2[:, 0:W], f3[:, 0:W], ALU.min),
                waits=[(sc, m2)], inc=ve)
            tadd = E("scalar", lambda c, W=W, f1=f1, f2=f2: c.activation(
                f2[:, 0:W], f1[:, 0:W], AF.Relu, bias=phc[:], scale=1.0),
                waits=[(ve, u2)], inc=sc)
            ic = E("scalar", lambda c, W=W, f2=f2: c.activation(
                qi[:, 0:W], f2[:, 0:W], AF.Copy),
                waits=[(sc, tadd), qshared["qi"]], inc=sc)
            fi = E("scalar", lambda c, W=W, f3=f3: c.activation(
                f3[:, 0:W], qi[:, 0:W], AF.Copy), waits=[(sc, ic)], inc=sc)
            qshared["qi"] = (sc, fi)
            gt = E("vector", lambda v, W=W, f1=f1, f2=f2, f3=f3: v.tensor_tensor(
                f1[:, 0:W], f3[:, 0:W], f2[:, 0:W], ALU.is_gt),
                waits=[(sc, fi)], inc=ve)
            isub = E("vector", lambda v, W=W, f1=f1, f3=f3: v.tensor_tensor(
                f3[:, 0:W], f3[:, 0:W], f1[:, 0:W], ALU.subtract),
                waits=[(ve, gt)], inc=ve)
            inv["isub"] = isub

        def _q_sh2(inv):
            S, W = inv["S"], inv["W"]
            NB16 = W // 16
            f1, f2, f3 = S["f1"], S["f2"], S["f3"]
            r1 = E("scalar", lambda c, W=W, f1=f1, f3=f3: c.activation(
                f1[:, 0:W], f3[:, 0:W], AF.Relu, bias=m4c[:], scale=1.0),
                waits=[(ve, inv["isub"]), t_cst], inc=sc)
            r2 = E("scalar", lambda c, W=W, f2=f2, f3=f3: c.activation(
                f2[:, 0:W], f3[:, 0:W], AF.Relu, bias=m6c[:], scale=1.0),
                waits=[(sc, r1)], inc=sc)
            s1 = E("vector", lambda v, W=W, f1=f1, f2=f2: v.scalar_tensor_tensor(
                qb1[:, 0:W], f1[:, 0:W], 0.5, f2[:, 0:W], ALU.mult, ALU.add),
                waits=[(sc, r2), qshared["qb"]], inc=ve)
            s2 = E("vector", lambda v, W=W, f3=f3: v.scalar_tensor_tensor(
                qb2[:, 0:W], f3[:, 0:W], 0.5, qb1[:, 0:W], ALU.mult, ALU.add),
                waits=[(ve, s1)], inc=ve)
            t2 = E("gpsimd", lambda g, W=W, f1=f1, sg_=S["sgb"]: g.tensor_tensor(
                f1[:, 0:W], qb2[:, 0:W], sg_[:, 0:W], ALU.mult),
                waits=[(ve, s2)], inc=gs)
            qshared["qb"] = (gs, t2)
            qv = E("gpsimd", lambda g, W=W, NB16=NB16, dst=inv["dst"], f1=f1,
                   sl_=S["scl"]: g.tensor_tensor(
                dst.rearrange("p (n b) -> p n b", b=16),
                f1[:, 0:W].rearrange("p (n b) -> p n b", b=16),
                sl_[:, 0:NB16].broadcast_to([128, NB16, 16]), ALU.mult),
                waits=[(gs, t2)] + inv["wdst_fn"](), inc=gs)
            S["f1_free"] = (gs, qv)
            S["f2_free"] = (ve, s1)
            S["f3_free"] = (ve, s2)
            S["scl_free"] = (gs, qv)
            S["sgb_free"] = (gs, t2)
            inv["done"] = (gs, qv)

        def q_feed(src_ap, dst_ap, F, wsrc, wdst_fn, meta=None):
            inv = {"S": qsets[qp["n"] % 3], "src": src_ap, "dst": dst_ap, "W": F,
                   "wsrc": list(wsrc), "wdst_fn": wdst_fn, "meta": meta}
            qp["n"] += 1
            _q_fha(inv)
            if qp["p1"] is not None:
                _q_sh1(qp["p1"])
            _q_fhb(inv)
            done = qp["p2"]
            if done is not None:
                _q_sh2(done)
            qp["p2"] = qp["p1"]
            qp["p1"] = inv
            return done

        def q_flush(cb):
            if qp["p1"] is not None:
                _q_sh1(qp["p1"])
            if qp["p2"] is not None:
                _q_sh2(qp["p2"])
                cb(qp["p2"])
            if qp["p1"] is not None:
                _q_sh2(qp["p1"])
                cb(qp["p1"])
            qp["p1"] = qp["p2"] = None

        t_qbf_cons = [0, 0]
        t_trd_cons = [0, 0]
        winv = [None, None]
        chunks = []
        for ci in range(NNC):
            chunks.append(("g", wg[ci], gT, ci, D, 0))
        for ci in range(NNC):
            chunks.append(("u", wu[ci], uT, ci, D, 0))
        for ci in range(NCH):
            for seg in range(3):
                W = 1024 if seg < 2 else 768
                chunks.append(("d", wd[ci], dT, ci, W, seg * 1024))

        def emit_wtr(pi):
            kind, dstT, ci, W, soff, slw = pi["meta"]
            NKC = W // 128
            tt = dset(dst_[slw], E("sync", lambda sy, slw=slw, W=W, NKC=NKC:
                                    sy.dma_start(
                out=wqT[:, slw, 0:W].rearrange("p (c f) -> p c f", c=NKC),
                in_=wqb[:, slw, 0:W].rearrange("p (c f) -> p c f", c=NKC),
                transpose=True),
                waits=[pi["done"], (dgss[slw], t_trd_cons[slw])] + dch(dst_[slw]),
                inc=dst_[slw], inc_by=16))
            t_qbf_cons[slw] = tt
            if kind in ("g", "u"):
                st = dset(dgss[slw], E("gpsimd", lambda g, dstT=dstT, ci=ci,
                                       slw=slw: g.dma_start(
                    out=dstT[:, :, ci * 128:(ci + 1) * 128].rearrange(
                        "c k n -> k c n"),
                    in_=wqT[:, slw, :].rearrange("p (c f) -> p c f", c=NCH)),
                    waits=[(dst_[slw], tt)] + dch(dgss[slw]),
                    inc=dgss[slw], inc_by=16))
            else:
                st = dset(dgss[slw], E("gpsimd", lambda g, ci=ci, slw=slw,
                                       soff=soff, NKC=NKC: g.dma_start(
                    out=dT[soff // 128:soff // 128 + NKC, :,
                           ci * 128:(ci + 1) * 128].rearrange("c k n -> k c n"),
                    in_=wqT[:, slw, 0:NKC * 128].rearrange(
                        "p (c f) -> p c f", c=NKC)),
                    waits=[(dst_[slw], tt)] + dch(dgss[slw]),
                    inc=dgss[slw], inc_by=16))
            t_trd_cons[slw] = st

        for n, (kind, src_ap, dstT, ci, W, soff) in enumerate(chunks):
            slw = n % 2
            wld_waits = [winv[slw]["src_done"]] if winv[slw] else []
            tl = dset(dgw[slw], E("gpsimd", lambda g, src_ap=src_ap, slw=slw,
                                  W=W, soff=soff:
                   g.dma_start(out=wraw[:, slw, 0:W], in_=src_ap[:, soff:soff + W]),
                   waits=wld_waits + dch(dgw[slw]), inc=dgw[slw], inc_by=16))
            prev = q_feed(wraw[:, slw, 0:W], wqb[:, slw, 0:W], W,
                          [(dgw[slw], tl)],
                          (lambda slw=slw: [(dst_[slw], t_qbf_cons[slw])]),
                          meta=(kind, dstT, ci, W, soff, slw))
            winv[slw] = qp["p1"]
            if prev is not None:
                emit_wtr(prev)
        q_flush(emit_wtr)
        WSD0, WSD1 = dgss[0].n, dgss[1].n
        ctx0.close()
        x3q = sb("x3q", [128, NCH, T])
        xtm = sb("xtm", [128, 2, D])
        xqtm = sb("xqtm", [128, 1, D])
        wg_nb = sb("wg_nb", [128, 2, NCH, QB])
        wu_nb = sb("wu_nb", [128, 2, NCH, QB])
        wd_nb = sb("wd_nb", [128, 1, 4, D])
        h_tm = sb("h_tm", [128, 4 * QB])
        hq_tm = sb("hq_tm", [128, 4 * QB])
        hqf = sb("hq_fm", [128, max(4 * T, 2 * NCH * QB)])
        hq_fm = hqf[:, 0:4 * T].rearrange("p (c t) -> p c t", c=4)
        v_blk = hqf[:, 0:NCH * QB].rearrange("p (c t) -> p c t", c=NCH)
        t_blk = hqf[:, NCH * QB:2 * NCH * QB].rearrange("p (c t) -> p c t", c=NCH)
        sg_sb = sb("sg_sb", [128, 2, QB])
        facc = sb("facc", [128, NCH, T])
        xfmt = sb("xfmt", [128, 1, D])
        r2f = h_tm[:, 0:QB]
        row_mu = sb("row_mu", [1, QB], F32)
        row_mu2 = sb("row_mu2", [1, QB], F32)
        row_s2 = sb("row_s2", [1, QB], F32)
        row_rstd = sb("row_rstd", [1, QB], F32)
        x3b = wg_nb[:, 0]

        xinv = [None, None]
        hinv = [None, None]
        t_hqc = [[], []]
        t_trx = [0, 0]
        for s in range(NSEQ):
            pr.barrier_all(allsems)
            t_xq_cons = [0, 0]
            t_xfmt_last = (dst_[0], 0)

            def emit_xtr(pi):
                i = pi["meta"]
                sl = i % 2
                tt = dset(dst_[sl], E("sync", lambda sy, i=i: sy.dma_start(
                    out=x3q[:, :, i * 128:(i + 1) * 128],
                    in_=xqtm[:, 0, :].rearrange("p (c f) -> p c f", c=NCH),
                    transpose=True),
                    waits=[pi["done"]] + dch(dst_[sl]),
                    inc=dst_[sl], inc_by=16))
                t_xq_cons[sl] = tt
                t_trx[sl] = tt

            for i in range(NT):
                sl = i % 2
                tld = dset(dgx[sl], E("gpsimd", lambda g, i=i, s=s: g.dma_start(
                    out=xfmt[:, 0, :].rearrange("p (c f) -> p c f", c=NCH),
                    in_=xi[s][:, :, i * 128:(i + 1) * 128].rearrange(
                        "c p f -> p c f")),
                    waits=[t_xfmt_last] + dch(dgx[sl]), inc=dgx[sl], inc_by=16))
                xw = [xinv[sl]["src_done"]] if xinv[sl] else []
                tl = dset(dst_[sl], E("sync", lambda sy, i=i, sl=sl: sy.dma_start(
                    out=xtm[:, sl, :].rearrange("p (c f) -> p c f", c=NCH),
                    in_=xfmt[:, 0, :].rearrange("p (c f) -> p c f", c=NCH),
                    transpose=True),
                    waits=xw + [(dgx[sl], tld)] + dch(dst_[sl]),
                    inc=dst_[sl], inc_by=16))
                t_xfmt_last = (dst_[sl], tl)
                prev = q_feed(xtm[:, sl, :], xqtm[:, 0, :], D, [(dst_[sl], tl)],
                              (lambda: [(dst_[0], t_xq_cons[0]),
                                        (dst_[1], t_xq_cons[1])]), meta=i)
                xinv[sl] = qp["p1"]
                if prev is not None:
                    emit_xtr(prev)
            q_flush(emit_xtr)
            XQRDY = [(dst_[0], t_trx[0]), (dst_[1], t_trx[1])]

            t_wnb = [0, 0]
            t_wnb_cons = [0, 0]
            t_facc_cons = (ve, 0)
            t_hfm_cons = 0
            t_wd_cons = 0
            for nb in range(6):
                W = NBW[nb]
                NKC = W // 128
                slw = nb % 2
                ld1 = dset(dgw[slw], E("gpsimd", lambda g, nb=nb, slw=slw, W=W:
                                       g.dma_start(
                    out=wg_nb[:, slw, :, 0:W], in_=gT[:, :, NBO[nb]:NBO[nb] + W]
                    .rearrange("c k n -> k c n")),
                    waits=[(pe, t_wnb_cons[slw]), (dgss[0], WSD0), (dgss[1], WSD1)]
                    + dch(dgw[slw]), inc=dgw[slw], inc_by=16))
                ld2 = dset(dgw[slw], E("gpsimd", lambda g, nb=nb, slw=slw, W=W:
                                       g.dma_start(
                    out=wu_nb[:, slw, :, 0:W], in_=uT[:, :, NBO[nb]:NBO[nb] + W]
                    .rearrange("c k n -> k c n")), waits=dch(dgw[slw]),
                    inc=dgw[slw], inc_by=16))
                ld3 = dset(dgw[slw], E("gpsimd", lambda g, nb=nb, slw=slw, NKC=NKC:
                                       g.dma_start(
                    out=wd_nb[:, 0, 0:NKC, :],
                    in_=dT[NBO[nb] // 128:NBO[nb] // 128 + NKC].rearrange(
                        "c k n -> k c n")),
                    waits=[(pe, t_wd_cons)] + dch(dgw[slw]),
                    inc=dgw[slw], inc_by=16))
                t_wnb[slw] = ld3
                t_hq_war = (pe, t_hfm_cons)
                t_pg_free = (sc, 0)
                t_pu_free = (ve, 0)
                t_sg_free = [(ve, 0), (ve, 0)]

                def emit_htr(pi):
                    p_, W_, NKC_, hb_ = pi["meta"]
                    tts = []
                    for j in (0, 1):
                        i = 2 * p_ + j
                        tts.append(dset(dst_[j], E("sync", lambda sy, i=i, j=j,
                                        W_=W_, NKC_=NKC_, hb_=hb_: sy.dma_start(
                            out=hq_fm[:, 0:NKC_, i * 128:(i + 1) * 128],
                            in_=hq_tm[:, hb_ + j * W_:hb_ + (j + 1) * W_]
                            .rearrange("p (c f) -> p c f", c=NKC_),
                            transpose=True),
                            waits=[pi["done"], pi["hq_war"]] + dch(dst_[j]),
                            inc=dst_[j], inc_by=16)))
                    t_hqc[p_ % 2] = [(dst_[0], tts[0]), (dst_[1], tts[1])]
                    t_trx[0], t_trx[1] = tts[0], tts[1]

                for p in range(NT // 2):
                    par = p % 2
                    hb = par * 2 * QB
                    ths = []
                    for j in (0, 1):
                        i = 2 * p + j
                        mmg = mmu = None
                        for kc in range(NCH):
                            mmg = E("tensor", lambda t, i=i, kc=kc, slw=slw, W=W:
                                    t.matmul(
                                p_g[:, 0:W], x3q[:, kc, i * 128:(i + 1) * 128],
                                wg_nb[:, slw, kc, 0:W], start=(kc == 0),
                                stop=(kc == NCH - 1)),
                                waits=XQRDY + [(dgw[slw], t_wnb[slw]), t_pg_free,
                                               t_pu_free], inc=pe)
                            mmu = E("tensor", lambda t, i=i, kc=kc, slw=slw, W=W:
                                    t.matmul(
                                p_u[:, 0:W], x3q[:, kc, i * 128:(i + 1) * 128],
                                wu_nb[:, slw, kc, 0:W], start=(kc == 0),
                                stop=(kc == NCH - 1)), inc=pe)
                        tsg = E("scalar", lambda c, j=j, W=W: c.activation(
                            sg_sb[:, j, 0:W], p_g[:, 0:W], AF.Silu),
                            waits=[(pe, mmu), t_sg_free[j]], inc=sc)
                        t_pg_free = (sc, tsg)
                        hw = [hinv[par]["src_done"]] if hinv[par] else []
                        th = E("vector", lambda v, j=j, W=W, hb=hb: v.tensor_tensor(
                            h_tm[:, hb + j * W:hb + (j + 1) * W], sg_sb[:, j, 0:W],
                            p_u[:, 0:W], ALU.mult),
                            waits=[(sc, tsg)] + hw, inc=ve)
                        t_pu_free = (ve, th)
                        t_sg_free[j] = (ve, th)
                        ths.append(th)
                    prev = q_feed(h_tm[:, hb:hb + 2 * W], hq_tm[:, hb:hb + 2 * W],
                                  2 * W, [(ve, ths[0]), (ve, ths[1])],
                                  (lambda par=par: list(t_hqc[par])),
                                  meta=(p, W, NKC, hb))
                    qp["p1"]["hq_war"] = t_hq_war
                    hinv[par] = qp["p1"]
                    if prev is not None:
                        emit_htr(prev)
                q_flush(emit_htr)
                HQRDY = [(dst_[0], t_trx[0]), (dst_[1], t_trx[1])]
                first_nb = (nb == 0)
                for oc in range(NCH):
                    bi = oc % 2
                    for tb in range(NQB):
                        mm = None
                        for kk in range(NKC):
                            mm = E("tensor", lambda t, oc=oc, tb=tb, kk=kk, slw=slw,
                                   bi=bi, NKC=NKC: t.matmul(
                                p_d[bi][:], wd_nb[:, 0, kk, oc * 128:(oc + 1) * 128],
                                hq_fm[:, kk, tb * QB:(tb + 1) * QB],
                                start=(kk == 0), stop=(kk == NKC - 1)),
                                waits=HQRDY + [(dgw[slw], t_wnb[slw]),
                                               t_facc_cons], inc=pe)
                        if first_nb:
                            tf = E("vector", lambda v, oc=oc, tb=tb, bi=bi: v.tensor_scalar(
                                facc[:, oc, tb * QB:(tb + 1) * QB], p_d[bi][:],
                                1.0, None, ALU.mult),
                                waits=[(pe, mm)], inc=ve)
                        else:
                            tf = E("vector", lambda v, oc=oc, tb=tb, bi=bi:
                                   v.scalar_tensor_tensor(
                                facc[:, oc, tb * QB:(tb + 1) * QB], p_d[bi][:], 1.0,
                                facc[:, oc, tb * QB:(tb + 1) * QB], ALU.mult, ALU.add),
                                waits=[(pe, mm)], inc=ve)
                        t_facc_cons = (ve, tf)
                t_hfm_cons = mm
                t_wnb_cons[slw] = mm
                t_wd_cons = mm

            t_psum_free = (ve, 0)
            t_psq_free = (ve, 0)
            t_rows_free = (ve, 0)
            t_x3b_cons = (ve, 0)
            t_vcons_pe = (pe, 0)
            t_vcons_sc = (sc, 0)
            t_xtm2_cons = [0, 0]
            t_stf = 0
            for tb in range(NQB):
                tl = dset(dgx[tb % 2], E("gpsimd", lambda g, tb=tb, s=s:
                                         g.dma_start(
                    out=x3b[:], in_=xi[s][:, :, tb * QB:(tb + 1) * QB].rearrange(
                        "c p t -> p c t")),
                    waits=[t_x3b_cons] + dch(dgx[tb % 2]),
                    inc=dgx[tb % 2], inc_by=16))
                t_v = [0] * NCH
                for c in range(NCH):
                    t_v[c] = E("vector", lambda v, c=c, tb=tb: v.tensor_tensor(
                        v_blk[:, c, :], x3b[:, c, :],
                        facc[:, c, tb * QB:(tb + 1) * QB], ALU.add),
                        waits=[(dgx[tb % 2], tl), t_facc_cons, t_vcons_pe, t_vcons_sc,
                               (dgss[0], t_xtm2_cons[0]), (dgss[1], t_xtm2_cons[1])],
                        inc=ve)
                t_x3b_cons = (ve, t_v[NCH - 1])
                mm1 = None
                for c in range(NCH):
                    mm1 = E("tensor", lambda t, c=c: t.matmul(
                        p_sum[:], cs["c_128x32"][:], v_blk[:, c, :],
                        start=(c == 0), stop=(c == NCH - 1)),
                        waits=[(ve, t_v[c]), t_psum_free], inc=pe)
                mm2 = 0
                tsq = 0
                for c in range(NCH):
                    tsq = E("scalar", lambda sce, c=c: sce.activation(
                        r2f[:], v_blk[:, c, :], AF.Square),
                        waits=[(ve, t_v[c]), (pe, mm2),
                               hinv[0]["src_done"]], inc=sc)
                    mm2 = E("tensor", lambda t, c=c: t.matmul(
                        p_sq[:], cs["c_128x32"][:], r2f[:],
                        start=(c == 0), stop=(c == NCH - 1)),
                        waits=[(sc, tsq), t_psq_free], inc=pe)
                a1 = E("scalar", lambda sce: sce.mul(row_mu[:], p_sum[0:1, :], 1.0 / D),
                       waits=[(pe, mm1), t_rows_free], inc=sc)
                a2 = E("scalar", lambda sce: sce.activation(row_mu2[:], row_mu[:],
                                                            AF.Square),
                       waits=[(sc, a1)], inc=sc)
                a3 = E("vector", lambda v: v.scalar_tensor_tensor(
                    row_s2[:], p_sq[0:1, :], 1.0 / D, row_mu2[:], ALU.mult,
                    ALU.subtract), waits=[(pe, mm2), (sc, a2)], inc=ve)
                a4 = E("scalar", lambda sce: sce.activation(
                    row_mu2[:], row_s2[:], AF.Ln, bias=epsB[:], scale=1.0),
                    waits=[(ve, a3)], inc=sc)
                a5 = E("scalar", lambda sce: sce.activation(
                    row_rstd[:], row_mu2[:], AF.Exp, bias=0.0, scale=-0.5),
                    waits=[(sc, a4)], inc=sc)
                t_psum_free = (sc, a1)
                t_psq_free = (ve, a3)
                c3 = E("tensor", lambda t: t.matmul(
                    p_r1[:], cs["c_1x128f"][:], row_mu[:], start=True, stop=True),
                    waits=[(sc, a5), (ve, t_r1_cons(0))] if False else [(sc, a5)],
                    inc=pe)
                c4 = E("tensor", lambda t: t.matmul(
                    p_r2[:], cs["c_1x128f"][:], row_rstd[:], start=True, stop=True),
                    waits=[(sc, a5)], inc=pe)
                t_rows_free = (pe, c4)
                t_p4 = [0] * NCH
                for c in range(NCH):
                    t_p4[c] = E("vector", lambda v, c=c: v.tensor_tensor(
                        t_blk[:, c, :], v_blk[:, c, :], p_r1[:], ALU.subtract),
                        waits=[(ve, t_v[c]), (pe, max(c3, mm2))], inc=ve)
                t_p5 = [0] * NCH
                for c in range(NCH):
                    t_p5[c] = E("vector", lambda v, c=c: v.tensor_tensor(
                        v_blk[:, c, :], t_blk[:, c, :], p_r2[:], ALU.mult),
                        waits=[(ve, t_p4[c]), (pe, c4)], inc=ve)
                t_vcons_pe = (pe, mm1)
                t_vcons_sc = (sc, tsq)
                for j in range(4):
                    i = tb * 4 + j
                    sl = j % 2
                    ttr = None
                    for c in range(NCH):
                        ttr = dset(dst_[sl], E("sync", lambda sy, j=j, sl=sl,
                                               c=c: sy.dma_start(
                            out=xtm[:, sl, c * 128:(c + 1) * 128],
                            in_=v_blk[:, c, j * 128:(j + 1) * 128],
                            transpose=True),
                            waits=[(ve, t_p5[NCH - 1]), (dgss[sl], t_xtm2_cons[sl]),
                                   xinv[sl]["src_done"]] + dch(dst_[sl]),
                            inc=dst_[sl], inc_by=16))
                    t_trx[sl] = ttr
                    t_xtm2_cons[sl] = dset(dgss[sl], E("gpsimd", lambda g, i=i,
                                           s=s, sl=sl: g.dma_start(
                        out=xout[s, i * 128:(i + 1) * 128, :], in_=xtm[:, sl, :]),
                        waits=[(dst_[sl], ttr)] + dch(dgss[sl]),
                        inc=dgss[sl], inc_by=16))
        pr.barrier_all(allsems)
        E("gpsimd", None, waits=[(dgss[0], dgss[0].n), (dgss[1], dgss[1].n)])


def ds0_v(x, d):
    return x


def t_pg_cons(x):
    return x


def t_sg_cons(x):
    return x


def t_hqtr(a, b):
    return 0


def t_r1_cons(x):
    return x


def prep_d(inputs, n_cores=8, xin_override=None):
    w_gate = np.asarray(inputs["L1_ffn_w_gate"])
    w_up = np.asarray(inputs["L1_ffn_w_up"])
    w_down = np.asarray(inputs["L1_ffn_w_down"])
    shared = dict(
        wg=np.ascontiguousarray(w_gate.reshape(NNC, 128, D)),
        wu=np.ascontiguousarray(w_up.reshape(NNC, 128, D)),
        wd=np.ascontiguousarray(w_down.reshape(NCH, 128, IH)),
        **consts_d())
    in_maps = []
    for core in range(n_cores):
        m = dict(shared)
        m["xi"] = np.ascontiguousarray(xin_override[core])
        in_maps.append(m)
    return in_maps


ALL_SEMS = ("pe", "sc", "ve", "gs", "dgp", "dgw0", "dgw1", "dgx", "dgx0", "dgx1",
            "ds0", "ds1", "dgs", "dgs1")


def build_all():
    nc = bass.Bass("TRN2", target_bir_lowering=False, debug=False)
    T = globals()["T"]
    dram = {}

    def din(name, shape, dt):
        dram[name] = nc.dram_tensor(name, shape, dt, kind="ExternalInput").ap()

    def dint(name, shape, dt):
        dram[name] = nc.dram_tensor(name, shape, dt, kind="Internal").ap()

    din("a_xin", [NSEQ, T, D], F32)
    for n in ("a_wq", "a_wk", "a_wv"):
        din(n, [NCH, 128, D], F32)
    din("a_bq", [128, NCH], F32)
    din("a_bk", [128, NCH], F32)
    din("a_bv", [1, D], F32)
    din("a_gcol", [128, NCH], F32)
    din("a_gbrow", [2, D], F32)
    HS = 2048
    din("b_sw1", [NCH, 128, HS], F32)
    din("b_sw3", [NCH, 128, HS], F32)
    din("b_sw2", [HS // 128, 128, D], F32)
    din("b_w1", [8, NCH, 128, D], F32)
    din("b_w2", [8, NCH, 128, D], F32)
    din("b_sb1c", [128, HS // 128], F32)
    din("b_sb3c", [128, HS // 128], F32)
    din("b_sb2c", [128, NCH], F32)
    din("b_b1c", [8, 128, NCH], F32)
    din("b_b2r", [8, 1, D], F32)
    din("b_temb", [4, D], F32)
    for n in ("c_wq", "c_wk", "c_wv", "c_wo"):
        din(n, [NCH, 128, D], F32)
    din("c_bo", [128, NCH], F32)
    din("d_wg", [NNC, 128, D], F32)
    din("d_wu", [NNC, 128, D], F32)
    din("d_wd", [NCH, 128, IH], F32)
    seen = {}
    for shapes in (CONST_SHAPES_A, CONST_SHAPES_B, CONST_SHAPES_C, CONST_SHAPES_D):
        for k, (sh, dt) in shapes.items():
            if k not in seen:
                seen[k] = True
                din(k, sh, dt)
    dint("xab", [NSEQ, NCH, 128, T], BF16)
    dint("xbc", [NSEQ, NCH, 128, T], BF16)
    dint("xcd", [NSEQ, NCH, 128, T], BF16)
    dint("d_gT", [NCH, 128, IH], BF16)
    dint("d_uT", [NCH, 128, IH], BF16)
    dint("d_dT", [NNC, 128, D], BF16)
    dram["xout"] = nc.dram_tensor("xout", [NSEQ, T, D], F32,
                                  kind="ExternalOutput").ap()

    with ExitStack() as octx:
        sems = {n: Sem(octx.enter_context(nc.semaphore(n))) for n in ALL_SEMS}
        pr = Prog()
        with ExitStack() as c1:
            emit_a(nc, pr, sems, c1, dram)
        with ExitStack() as c2:
            emit_b(nc, pr, sems, c2, dram)
        with ExitStack() as c3:
            emit_c(nc, pr, sems, c3, dram)
        with ExitStack() as c4:
            emit_d(nc, pr, sems, c4, dram)
        with nc.Block() as block:
            pr.build(block)
    return nc


def const_union():
    c = {}
    c.update(consts_a())
    c.update(consts_b())
    c.update(consts_c())
    c.update(consts_d())
    return c


def prep_all(inputs, n_cores=8):
    x = np.asarray(inputs["L0_attn_x"])
    shared = {}
    shared["a_wq"] = np.ascontiguousarray(np.asarray(inputs["L0_attn_Wq"]).reshape(NCH, 128, D))
    shared["a_wk"] = np.ascontiguousarray(np.asarray(inputs["L0_attn_Wk"]).reshape(NCH, 128, D))
    shared["a_wv"] = np.ascontiguousarray(np.asarray(inputs["L0_attn_Wv"]).reshape(NCH, 128, D))
    shared["a_bq"] = np.ascontiguousarray(np.asarray(inputs["L0_attn_bq"]).reshape(NCH, 128).T)
    shared["a_bk"] = np.ascontiguousarray(np.asarray(inputs["L0_attn_bk"]).reshape(NCH, 128).T)
    shared["a_bv"] = np.asarray(inputs["L0_attn_bv"]).reshape(1, D)
    shared["a_gcol"] = np.ascontiguousarray(np.asarray(inputs["L0_attn_gamma"]).reshape(NCH, 128).T)
    shared["a_gbrow"] = np.stack([np.asarray(inputs["L0_attn_gamma"]),
                                  np.asarray(inputs["L0_attn_beta"])], axis=0)
    HS = 2048
    shared["b_sw1"] = np.ascontiguousarray(np.asarray(inputs["L0_ffn_sw1"]).reshape(NCH, 128, HS))
    shared["b_sw3"] = np.ascontiguousarray(np.asarray(inputs["L0_ffn_sw3"]).reshape(NCH, 128, HS))
    shared["b_sw2"] = np.ascontiguousarray(np.asarray(inputs["L0_ffn_sw2"]).reshape(HS // 128, 128, D))
    shared["b_w1"] = np.ascontiguousarray(np.asarray(inputs["L0_ffn_w1"]).reshape(8, NCH, 128, D))
    shared["b_w2"] = np.ascontiguousarray(np.asarray(inputs["L0_ffn_w2"]).reshape(8, NCH, 128, D))
    shared["b_sb1c"] = np.ascontiguousarray(np.asarray(inputs["L0_ffn_sb1"]).reshape(HS // 128, 128).T)
    shared["b_sb3c"] = np.ascontiguousarray(np.asarray(inputs["L0_ffn_sb3"]).reshape(HS // 128, 128).T)
    shared["b_sb2c"] = np.ascontiguousarray(np.asarray(inputs["L0_ffn_sb2"]).reshape(NCH, 128).T)
    shared["b_b1c"] = np.ascontiguousarray(np.asarray(inputs["L0_ffn_b1"]).reshape(8, NCH, 128).transpose(0, 2, 1))
    shared["b_b2r"] = np.ascontiguousarray(np.asarray(inputs["L0_ffn_b2"]).reshape(8, 1, D))
    shared["b_temb"] = np.ascontiguousarray(np.asarray(inputs["L0_ffn_t_emb"]))
    shared["c_wq"] = np.ascontiguousarray(np.asarray(inputs["L1_attn_Wq"]).reshape(NCH, 128, D))
    shared["c_wk"] = np.ascontiguousarray(np.asarray(inputs["L1_attn_Wk"]).reshape(NCH, 128, D))
    shared["c_wv"] = np.ascontiguousarray(np.asarray(inputs["L1_attn_Wv"]).reshape(NCH, 128, D))
    shared["c_wo"] = np.ascontiguousarray(np.asarray(inputs["L1_attn_Wo"]).reshape(NCH, 128, D))
    shared["c_bo"] = np.ascontiguousarray(np.asarray(inputs["L1_attn_bo"]).reshape(NCH, 128).T)
    shared["d_wg"] = np.ascontiguousarray(np.asarray(inputs["L1_ffn_w_gate"]).reshape(NNC, 128, D))
    shared["d_wu"] = np.ascontiguousarray(np.asarray(inputs["L1_ffn_w_up"]).reshape(NNC, 128, D))
    shared["d_wd"] = np.ascontiguousarray(np.asarray(inputs["L1_ffn_w_down"]).reshape(NCH, 128, IH))
    shared.update(const_union())
    in_maps = []
    for core in range(n_cores):
        m = dict(shared)
        m["a_xin"] = np.ascontiguousarray(x[core * NSEQ:(core + 1) * NSEQ])
        in_maps.append(m)
    return in_maps


def kernel(**inputs):
    n_cores = 8
    nc = build_all()
    in_maps = prep_all(inputs, n_cores=n_cores)
    res = run_bass_kernel_spmd(nc, in_maps, core_ids=list(range(n_cores)))
    out = np.empty((16, T, D), dtype=np.float32)
    for core in range(n_cores):
        xo = np.asarray(res.results[core]["xout"])
        out[core * 2] = xo[0]
        out[core * 2 + 1] = xo[1]
    return out


# version 13
# speedup vs baseline: 1.3763x; 1.0611x over previous
import numpy as np
import ml_dtypes
from contextlib import ExitStack
import concourse.bass as bass
import concourse.mybir as mybir
from concourse.bass_utils import run_bass_kernel_spmd

F32 = mybir.dt.float32
BF16 = mybir.dt.bfloat16
AF = mybir.ActivationFunctionType
ALU = mybir.AluOpType
AX = mybir.AxisListType
BF = ml_dtypes.bfloat16

ENGINES = ("tensor", "scalar", "vector", "gpsimd", "sync")

T, D, NCH, H, DH = 2048, 1024, 8, 16, 64
QB, NQB, NT, NSEQ = 512, 4, 16, 2
VW = H * 65


class Sem:
    def __init__(self, handle):
        self.h = handle
        self.n = 0


class Prog:
    def __init__(self):
        self.q = {e: [] for e in ENGINES}
        self._lastwait = {e: {} for e in ENGINES}

    def emit(self, eng, fn, waits=(), inc=None, inc_by=1):
        lw = self._lastwait[eng]
        wl = []
        for s, v in waits:
            if v is None or v <= 0:
                continue
            if lw.get(id(s), -1) >= v:
                continue
            wl.append((s, v))
            lw[id(s)] = v
        tok = None
        if inc is not None:
            assert fn is not None, "cannot inc a semaphore without an instruction"
            inc.n += inc_by
            tok = inc.n
        self.q[eng].append((fn, wl, inc, inc_by))
        return tok

    def barrier_all(self, sems):
        snap = [(s, s.n) for s in sems if s.n > 0]
        for e in ENGINES:
            for s, v in snap:
                self.emit(e, None, waits=[(s, v)])

    def build(self, block):
        def runner(engq):
            def _(eng):
                for fn, wl, inc, inc_by in engq:
                    for s, v in wl:
                        eng.wait_ge(s.h, v)
                    if fn is None:
                        continue
                    instr = fn(eng)
                    if inc is not None:
                        instr.then_inc(inc.h, inc_by)
            return _
        block.tensor(runner(self.q["tensor"]))
        block.scalar(runner(self.q["scalar"]))
        block.vector(runner(self.q["vector"]))
        block.gpsimd(runner(self.q["gpsimd"]))
        block.sync(runner(self.q["sync"]))


def consts_a():
    c = {}
    c["c_128x32"] = np.ones((128, 32), dtype=BF)
    c["c_128x32f"] = np.ones((128, 32), dtype=np.float32)
    c["c_1x32"] = np.ones((1, 32), dtype=BF)
    c["c_32x128"] = np.ones((32, 128), dtype=BF)
    c["c_1x64"] = np.ones((1, 64), dtype=BF)
    c["c_1x128"] = np.ones((1, 128), dtype=BF)
    k = np.arange(128)[:, None]
    u = np.arange(QB + 384)[None, :]
    c["tri01"] = (u >= k + 384).astype(BF)
    c["c_32x128d"] = (np.ones((32, 128)) / 32.0).astype(BF)
    sel = np.zeros((16, NCH * 128), dtype=BF)
    for hc in range(NCH):
        sel[2 * hc, hc * 128:hc * 128 + 64] = 1
        sel[2 * hc + 1, hc * 128 + 64:(hc + 1) * 128] = 1
    c["sel16"] = sel
    return c


CONST_SHAPES_A = {
    "c_128x32": ([128, 32], BF16), "c_128x32f": ([128, 32], F32),
    "c_1x32": ([1, 32], BF16), "c_32x128": ([32, 128], BF16),
    "c_1x64": ([1, 64], BF16), "c_1x128": ([1, 128], BF16),
    "tri01": ([128, QB + 384], BF16),
    "c_32x128d": ([32, 128], BF16), "sel16": ([16, NCH * 128], BF16),
}


def emit_a(nc, pr, sems, ctx, dram):
    T, NQB, NT = (globals()["T"], globals()["NQB"], globals()["NT"])
    xin = dram["a_xin"]
    wq, wk, wv = dram["a_wq"], dram["a_wk"], dram["a_wv"]
    bq, bk, bv = dram["a_bq"], dram["a_bk"], dram["a_bv"]
    gcol, gbrow = dram["a_gcol"], dram["a_gbrow"]
    cin = {k: dram[k] for k in CONST_SHAPES_A}
    xa = dram["xab"]
    if True:
        sb = lambda name, shape, dt=BF16: ctx.enter_context(nc.sbuf_tensor("a_" + name, shape, dt))
        ps = lambda name, shape, dt=F32: ctx.enter_context(nc.psum_tensor("a_" + name, shape, dt))
        wq_sb = sb("wq_sb", [128, NCH, D])
        wk_sb = sb("wk_sb", [128, NCH, D])
        wv_sb = sb("wv_sb", [128, NCH, D])
        bq_sb = sb("bq_sb", [128, NCH], F32)
        bk_sb = sb("bk_sb", [128, NCH], F32)
        bv_row = sb("bv_row", [1, D])
        bvrep = sb("bvrep", [32, D])
        gcol_sb = sb("gcol_sb", [128, NCH], F32)
        gb_sb = sb("gb_sb", [2, D])
        cs = {k: sb(k + "_s", sh, dt) for k, (sh, dt) in CONST_SHAPES_A.items()}
        neg50 = sb("neg50", [128, 1], F32)
        epsA = sb("epsA", [32, 1], F32)
        epsB = sb("epsB", [32, 1], F32)

        x_tm = sb("x_tm", [128, 2, D])
        x_blk = sb("x_blk", [128, 2, NCH, QB])
        q_blk = sb("q_blk", [128, NCH, QB])
        k_fm = sb("k_fm", [128, NCH, T])
        v_tm = sb("v_tm", [128, NT * VW])
        attn_blk = sb("attn_blk", [128, NCH, QB])
        praw = sb("praw", [128, 2, QB])
        pbuf = sb("pbuf", [128, 2, QB])
        r_blk = sb("r_blk", [128, NCH, QB])
        t1_blk = attn_blk
        x1_blk = sb("x1_blk", [128, NCH, QB])
        r2f = sb("r2f", [128, QB])
        msum_col = sb("msum_col", [128, NT], F32)
        m_colv = sb("m_colv", [128, NT], F32)
        m_colv2 = sb("m_colv2", [128, NT], F32)
        m_raw = sb("m_raw", [1, T])
        m_row = m_raw

        rdm_row = sb("rdm_row", [1, QB])
        den_tmp = sb("den_tmp", [1, 2, QB], F32)
        den16 = sb("den16", [16, QB], F32)
        rdm16b = sb("rdm16b", [16, QB])
        mrow16 = sb("mrow16", [16, QB])
        row_mu = sb("row_mu", [32, QB], F32)
        row_mu2 = sb("row_mu2", [32, QB], F32)
        row_s2 = sb("row_s2", [32, QB], F32)
        row_rstd = sb("row_rstd", [32, QB], F32)
        row_murstd = sb("row_murstd", [32, QB], F32)
        nurows = sb("nurows", [2, QB])
        mu_bf = sb("mu_bf", [32, QB])
        rstd_bf = sb("rstd_bf", [32, QB])

        p_s = [ps("p_s0", [128, QB]), ps("p_s1", [128, QB])]
        p_o = ps("p_o", [65, QB])
        p_v = ps("p_v", [128, D])
        p_rs = ps("p_rs", [128, QB])
        p_sum = ps("p_sum", [32, QB])
        p_sq = ps("p_sq", [32, QB])
        p_nu = [p_v[:, 0:QB], p_v[:, QB:D]]

        pe, sc, ve, dgp, dgs = (sems[k] for k in ("pe", "sc", "ve", "dgp", "dgs"))
        dgx = [sems["dgx0"], sems["dgx1"]]
        dst_ = [sems["ds0"], sems["ds1"]]
        allsems = list(sems.values())
        E = pr.emit

        for dst, src_ in [(wq_sb, wq), (wk_sb, wk), (wv_sb, wv)]:
            E("gpsimd", lambda g, d=dst, s_=src_: g.dma_start(
                out=d[:], in_=s_.rearrange("c p d -> p c d")), inc=dgp, inc_by=16)
        for dst, src_ in [(bq_sb, bq), (bk_sb, bk), (gcol_sb, gcol),
                          (bv_row, bv), (gb_sb, gbrow)] + \
                         [(cs[k], cin[k]) for k in CONST_SHAPES_A]:
            E("gpsimd", lambda g, d=dst, s_=src_: g.dma_start(out=d[:], in_=s_),
              inc=dgp, inc_by=16)
        LDG = dgp.n
        LDS = LDG
        t_neg50 = E("vector", lambda v: v.memset(neg50[:], -50.0), inc=ve)
        E("vector", lambda v: v.memset(epsA[:], 1e-8), inc=ve)
        t_epsm = E("vector", lambda v: v.memset(epsB[:], 1e-5), inc=ve)
        E("vector", lambda v: v.memset(nurows[:], -1.0), inc=ve)
        t_vones = E("vector", lambda v: v.memset(
            v_tm[:].rearrange("p (t h e) -> p t h e", t=NT, h=H)[:, :, :, 64:65], 1.0),
            inc=ve)
        t_bvev = 0
        for half in range(2):
            w = [(dgp, LDG)] + ([(ve, t_bvev)] if half else [])
            tmm = E("tensor", lambda t, h=half: t.matmul(
                p_rs[0:32, :], cs["c_1x32"][:], bv_row[:, h * QB:(h + 1) * QB],
                start=True, stop=True), waits=w, inc=pe)
            t_bvev = E("vector", lambda v, h=half: v.tensor_scalar(
                bvrep[:, h * QB:(h + 1) * QB], p_rs[0:32, :], 1.0 / 32.0, None, ALU.mult),
                waits=[(pe, tmm)], inc=ve)

        def load_xblk(s, b, slot, t_prev_cons, do_msum):
            for j in range(4):
                i = b * 4 + j
                tsl = j % 2
                w = [t_prev_cons, (dst_[tsl], t_tr_prev[tsl]), (ve, t_ms_prev[tsl])]
                tl = E("gpsimd", lambda g, i=i, tsl=tsl, s=s: g.dma_start(
                    out=x_tm[:, tsl, :], in_=xin[s, i * 128:(i + 1) * 128, :]),
                    waits=w, inc=dgx[tsl], inc_by=16)
                t_tr_prev[tsl] = E("sync", lambda sy, j=j, tsl=tsl, slot=slot: sy.dma_start(
                    out=x_blk[:, slot, :, j * 128:(j + 1) * 128],
                    in_=x_tm[:, tsl, :].rearrange("p (c f) -> p c f", c=NCH),
                    transpose=True), waits=[(dgx[tsl], tl), t_prev_cons],
                    inc=dst_[tsl], inc_by=16)
                if do_msum:
                    t_ms_prev[tsl] = E("vector", lambda v, i=i, tsl=tsl: v.reduce_sum(
                        msum_col[:, i:i + 1], x_tm[:, tsl, :], axis=AX.X),
                        waits=[(dgx[tsl], tl)], inc=ve)
            return [(dst_[0], t_tr_prev[0]), (dst_[1], t_tr_prev[1])]

        for s in range(NSEQ):
            pr.barrier_all(allsems)
            t_tr_prev = [0, 0]
            t_ms_prev = [0, 0]
            t_mrow_ev = 0
            t_bank_free = [0, 0]
            t_vev = 0
            t_psum_free0 = 0
            t_xblk_cons = [(pe, 0), (pe, 0)]
            t_xblk_rdy = [0, 0]
            for b in range(NQB):
                slot = b % 2
                wrdy = load_xblk(s, b, slot, t_xblk_cons[slot], do_msum=True)
                mm = None
                for c in range(NCH):
                    mm = E("tensor", lambda t, c=c, slot=slot: t.matmul(
                        p_sum[:], cs["c_128x32"][:], x_blk[:, slot, c, :],
                        start=(c == 0), stop=(c == NCH - 1)),
                        waits=wrdy + [(dgp, LDG), (ve, t_mrow_ev)], inc=pe)
                t_mrow_ev = E("vector", lambda v, b=b: v.tensor_copy(
                    m_raw[0:1, b * QB:(b + 1) * QB], p_sum[0:1, :]),
                    waits=[(pe, mm)], inc=ve)
                for c in range(NCH):
                    bi = c % 2
                    mm = None
                    for kc in range(NCH):
                        mm = E("tensor", lambda t, c=c, kc=kc, bi=bi, slot=slot: t.matmul(
                            p_s[bi][:], wk_sb[:, kc, c * 128:(c + 1) * 128],
                            x_blk[:, slot, kc, :],
                            start=(kc == 0), stop=(kc == NCH - 1)),
                            waits=[(sc, t_bank_free[bi])] + wrdy, inc=pe)
                    t_bank_free[bi] = E("scalar", lambda sce, c=c, b=b, bi=bi: sce.activation(
                        k_fm[:, c, b * QB:(b + 1) * QB], p_s[bi][:], AF.Relu,
                        bias=bk_sb[:, c:c + 1], scale=1.0),
                        waits=[(pe, mm), (dgp, LDS)], inc=sc)
                for j in range(4):
                    i = b * 4 + j
                    mm = None
                    for kc in range(NCH):
                        for half in range(2):
                            mm = E("tensor", lambda t, j=j, kc=kc, hf=half, slot=slot: t.matmul(
                                p_v[:, hf * QB:(hf + 1) * QB],
                                x_blk[:, slot, kc, j * 128:(j + 1) * 128],
                                wv_sb[:, kc, hf * QB:(hf + 1) * QB],
                                start=(kc == 0), stop=False),
                                waits=[(sc, t_vev), (ve, t_bvev)], inc=pe)
                    for half in range(2):
                        mm = E("tensor", lambda t, hf=half: t.matmul(
                            p_v[:, hf * QB:(hf + 1) * QB], cs["c_32x128"][:],
                            bvrep[:, hf * QB:(hf + 1) * QB], start=False, stop=True), inc=pe)
                    t_vev = E("scalar", lambda sce, i=i: sce.activation(
                        v_tm[:, i * VW:(i + 1) * VW].rearrange(
                            "p (h e) -> p h e", h=H)[:, :, 0:64],
                        p_v[:].rearrange("p (h e) -> p h e", h=H), AF.Relu),
                        waits=[(pe, mm), (ve, t_vones)], inc=sc)
                t_xblk_cons[slot] = (pe, mm)
            t_psum_free0 = t_mrow_ev
            tmp = E("scalar", lambda c: c.activation(m_colv2[:], msum_col[:], AF.Sign),
                    waits=[(ve, t_ms_prev[0]), (ve, t_ms_prev[1])], inc=sc)
            t_mcol = E("scalar", lambda c: c.activation(m_colv[:], m_colv2[:], AF.Abs),
                       waits=[(sc, tmp)], inc=sc)
            t_mb = E("vector", lambda v: v.tensor_scalar(
                m_colv2[:], m_colv[:], 100.0, -150.0, ALU.mult, ALU.add),
                waits=[(sc, t_mcol)], inc=ve)
            t_mrow_s = 0
            for b in range(NQB):
                tmp = E("scalar", lambda c, b=b: c.activation(
                    rdm_row[:], m_raw[0:1, b * QB:(b + 1) * QB], AF.Sign),
                    waits=[(ve, t_mrow_ev), (sc, t_mrow_s)], inc=sc)
                t_mrow_s = E("scalar", lambda c, b=b: c.activation(
                    m_raw[0:1, b * QB:(b + 1) * QB], rdm_row[:], AF.Abs),
                    waits=[(sc, tmp)], inc=sc)
            t_mrow = t_mrow_s

            t_attn_cons = 0
            t_praw_cons = [(ve, 0), (ve, 0)]
            t_pbuf_cons = [0, 0]
            t_oev = 0
            t_prs_free = (ve, t_bvev)
            t_den16_free = (ve, 0)
            t_dtmp = [0, 0]
            t_dprev = 0
            t_rdm16_free = (ve, 0)
            t_m16_free = (ve, 0)
            t_nu_free = [0, 0]
            t_x1_store = 0
            t_nurows_cons = 0
            t_mubf_cons = 0
            t_rstdbf_cons = 0
            t_r_cons_pe = (pe, 0)
            t_r_cons_sc = (sc, 0)
            t_psum_free = (ve, t_psum_free0)
            t_psq_free = (ve, 0)
            t_rows_free = (ve, 0)
            t_qev = 0
            t_q_cons = 0
            t_am = 0
            for qb in range(NQB):
                slot = qb % 2
                wrdy = load_xblk(s, qb, slot, t_xblk_cons[slot], do_msum=False)
                pm = E("tensor", lambda t, qb=qb: t.matmul(
                    p_rs[0:64, :], cs["c_1x64"][:],
                    m_raw[0:1, qb * QB:(qb + 1) * QB], start=True, stop=True),
                    waits=[(sc, t_mrow), t_prs_free], inc=pe)
                t_m16 = E("vector", lambda v: v.tensor_copy(
                    mrow16[0:16, :], p_rs[0:16, :]),
                    waits=[(pe, pm), t_m16_free], inc=ve)
                t_prs_free = (ve, t_m16)
                for c in range(NCH):
                    bi = c % 2
                    mm = None
                    for kc in range(NCH):
                        mm = E("tensor", lambda t, c=c, kc=kc, bi=bi, slot=slot: t.matmul(
                            p_s[bi][:], wq_sb[:, kc, c * 128:(c + 1) * 128],
                            x_blk[:, slot, kc, :],
                            start=(kc == 0), stop=(kc == NCH - 1)),
                            waits=[(sc, t_bank_free[bi]), (pe, t_q_cons)] + wrdy, inc=pe)
                    t_bank_free[bi] = E("scalar", lambda sce, c=c, bi=bi: sce.activation(
                        q_blk[:, c, :], p_s[bi][:], AF.Relu,
                        bias=bq_sb[:, c:c + 1], scale=1.0),
                        waits=[(pe, mm), (dgp, LDS)], inc=sc)
                    t_qev = t_bank_free[bi]
                nkt = 4 * qb + 4
                w_po = [(ve, t_oev)]
                for h in range(H):
                    hb = (h % 2) * 64
                    hc = h // 2
                    t_omm = None
                    for kt in range(nkt):
                        bi = kt % 2
                        smm = E("tensor", lambda t, hb=hb, hc=hc, kt=kt, bi=bi: t.matmul(
                            p_s[bi][:], k_fm[hb:hb + 64, hc, kt * 128:(kt + 1) * 128],
                            q_blk[hb:hb + 64, hc, :],
                            start=True, stop=True),
                            waits=[(sc, max(t_bank_free[bi], t_qev))], inc=pe)
                        t_q_cons = smm
                        texp = E("scalar", lambda sce, bi=bi, kt=kt: sce.activation(
                            praw[:, bi, :], p_s[bi][:], AF.Exp,
                            bias=m_colv2[:, kt:kt + 1], scale=0.125),
                            waits=[(pe, smm), t_praw_cons[bi], (ve, t_mb)], inc=sc)
                        t_bank_free[bi] = texp
                        off = kt - 4 * qb
                        if off >= 0:
                            ttri = E("vector", lambda v, bi=bi, off=off: v.tensor_tensor(
                                pbuf[:, bi, :], praw[:, bi, :],
                                cs["tri01"][:, 384 - off * 128:384 - off * 128 + QB],
                                ALU.mult),
                                waits=[(sc, texp), (pe, t_pbuf_cons[bi])], inc=ve)
                            t_praw_cons[bi] = (ve, ttri)
                            rhs = pbuf[:, bi, :]
                            w_omm = [(ve, ttri)]
                        else:
                            rhs = praw[:, bi, :]
                            w_omm = [(sc, texp)]
                        t_omm = E("tensor", lambda t, h=h, kt=kt, rhs=rhs,
                                  first=(kt == 0), last=(kt == nkt - 1): t.matmul(
                            p_o[:], v_tm[:, kt * VW + h * 65: kt * VW + h * 65 + 65],
                            rhs, start=first, stop=last),
                            waits=w_omm + w_po + [(sc, t_vev)], inc=pe)
                        w_po = []
                        if off >= 0:
                            t_pbuf_cons[bi] = t_omm
                        else:
                            t_praw_cons[bi] = (pe, t_omm)
                    d0a = E("vector", lambda v, h=h: v.tensor_copy(
                        den_tmp[0:1, h % 2, :], p_o[64:65, :]),
                        waits=[(pe, t_omm), (dgp, t_dtmp[h % 2])], inc=ve)
                    t_dtmp[h % 2] = E("gpsimd", lambda g, h=h: g.dma_start(
                        out=den16[h:h + 1, :], in_=den_tmp[0:1, h % 2, :]),
                        waits=[(ve, d0a), t_den16_free, (dgp, t_dprev)],
                        inc=dgp, inc_by=16)
                    t_dprev = t_dtmp[h % 2]
                    t_oev = E("vector", lambda v, hb=hb, hc=hc: v.tensor_copy(
                        attn_blk[hb:hb + 64, hc, :], p_o[0:64, :]),
                        waits=[(ve, max(d0a, t_attn_cons, t_am))], inc=ve)
                    w_po = [(ve, t_oev)]
                tse = E("vector", lambda v: v.tensor_scalar(
                    den16[0:16, :], den16[0:16, :], 1.0, 1e-30, ALU.mult, ALU.add),
                    waits=[(dgp, t_dprev), t_rdm16_free], inc=ve)
                rc = E("vector", lambda v: v.reciprocal(
                    den16[0:16, :], den16[0:16, :]),
                    waits=[(ve, tse)], inc=ve)
                t_den16_free = (ve, rc)
                rm = E("vector", lambda v: v.tensor_tensor(
                    rdm16b[0:16, :], den16[0:16, :], mrow16[0:16, :], ALU.mult),
                    waits=[(ve, rc)], inc=ve)
                t_rdm16_free = (ve, rm)
                t_m16_free = (ve, rm)
                bm = None
                for hc2 in range(NCH):
                    bm = E("tensor", lambda t, hc2=hc2: t.matmul(
                        p_rs[:], cs["sel16"][:, hc2 * 128:(hc2 + 1) * 128],
                        rdm16b[0:16, :], start=True, stop=True),
                        waits=[(ve, rm), t_prs_free], inc=pe)
                    t_am = E("vector", lambda v, hc2=hc2: v.tensor_tensor(
                        r_blk[:, hc2, :], attn_blk[:, hc2, :], p_rs[:], ALU.mult),
                        waits=[(pe, bm), t_r_cons_pe, t_r_cons_sc], inc=ve)
                    t_prs_free = (ve, t_am)
                t_r = [0] * NCH
                for c in range(NCH):
                    t_r[c] = E("vector", lambda v, c=c, slot=slot: v.tensor_tensor(
                        r_blk[:, c, :], r_blk[:, c, :],
                        x_blk[:, slot, c, :], ALU.add),
                        waits=[(ve, t_am)] + wrdy, inc=ve)


                def ln_stats(src_blk, t_src):
                    mm1 = None
                    for c in range(NCH):
                        mm1 = E("tensor", lambda t, c=c, sbk=src_blk: t.matmul(
                            p_sum[:], cs["c_128x32"][:], sbk[:, c, :],
                            start=(c == 0), stop=(c == NCH - 1)),
                            waits=[(ve, t_src[c]), t_psum_free], inc=pe)
                    mm2 = 0
                    tsq = 0
                    for c in range(NCH):
                        tsq = E("scalar", lambda sce, c=c, sbk=src_blk: sce.activation(
                            r2f[:], sbk[:, c, :], AF.Square),
                            waits=[(ve, t_src[c]), (pe, mm2)], inc=sc)
                        mm2 = E("tensor", lambda t, c=c: t.matmul(
                            p_sq[:], cs["c_128x32"][:], r2f[:],
                            start=(c == 0), stop=(c == NCH - 1)),
                            waits=[(sc, tsq), t_psq_free], inc=pe)
                    return mm1, mm2, tsq

                def ln_rows(mm1, mm2, eps_ap):
                    a1 = E("scalar", lambda sce: sce.mul(row_mu[:], p_sum[0:32, :], 1.0 / D),
                           waits=[(pe, mm1), t_rows_free], inc=sc)
                    a2 = E("scalar", lambda sce: sce.activation(row_mu2[:], row_mu[:], AF.Square),
                           waits=[(sc, a1)], inc=sc)
                    a3 = E("vector", lambda v: v.scalar_tensor_tensor(
                        row_s2[:], p_sq[0:32, :], 1.0 / D, row_mu2[:], ALU.mult, ALU.subtract),
                        waits=[(pe, mm2), (sc, a2)], inc=ve)
                    a4 = E("scalar", lambda sce, e=eps_ap: sce.activation(
                        row_mu2[:], row_s2[:], AF.Ln, bias=e[:], scale=1.0),
                        waits=[(ve, max(a3, t_epsm))], inc=sc)
                    a5 = E("scalar", lambda sce: sce.activation(
                        row_rstd[:], row_mu2[:], AF.Exp, bias=0.0, scale=-0.5),
                        waits=[(sc, a4)], inc=sc)
                    return a1, a3, a5

                mm1, mm2, tsq1 = ln_stats(r_blk, t_r)
                a_mu, a_s2, a_rstd = ln_rows(mm1, mm2, epsA)
                t_psum_free = (sc, a_mu)
                t_psq_free = (ve, a_s2)
                b1 = E("vector", lambda v: v.tensor_tensor(
                    row_murstd[:], row_mu[:], row_rstd[:], ALU.mult),
                    waits=[(sc, a_rstd)], inc=ve)
                b2 = E("vector", lambda v: v.tensor_copy(nurows[0:1, :], row_murstd[0:1, :]),
                       waits=[(ve, b1), (pe, t_nurows_cons)], inc=ve)
                b3 = E("vector", lambda v: v.tensor_copy(rstd_bf[:], row_rstd[:]),
                       waits=[(sc, a_rstd), (pe, t_rstdbf_cons)], inc=ve)
                b4 = E("tensor", lambda t: t.matmul(
                    p_rs[:], cs["c_32x128d"][:], rstd_bf[0:32, :],
                    start=True, stop=True),
                    waits=[(ve, b3), t_prs_free], inc=pe)
                t_rows_free = (ve, b3)
                t_pass2 = [0] * NCH
                nmm = 0
                for c in range(NCH):
                    nuh = c % 2
                    nmm = E("tensor", lambda t, c=c, nuh=nuh: t.matmul(
                        p_nu[nuh], gb_sb[:, c * 128:(c + 1) * 128], nurows[:],
                        start=True, stop=True),
                        waits=[(ve, max(b2, t_nu_free[nuh])), (sc, t_vev)], inc=pe)
                    t_nurows_cons = nmm
                    tp1 = E("vector", lambda v, c=c: v.scalar_tensor_tensor(
                        t1_blk[:, c, :], r_blk[:, c, :], gcol_sb[:, c:c + 1],
                        p_rs[:], ALU.mult, ALU.mult),
                        waits=[(ve, t_r[c]), (pe, b4)], inc=ve)
                    t_pass2[c] = E("vector", lambda v, c=c, nuh=nuh: v.tensor_tensor(
                        x1_blk[:, c, :], t1_blk[:, c, :], p_nu[nuh], ALU.subtract),
                        waits=[(ve, tp1), (pe, nmm), (dgs, t_x1_store)], inc=ve)
                    t_nu_free[nuh] = t_pass2[c]
                    t_prs_free = (ve, tp1)
                t_v = [0] * NCH
                for c in range(NCH):
                    t_v[c] = E("vector", lambda v, c=c, slot=slot: v.tensor_tensor(
                        r_blk[:, c, :], x1_blk[:, c, :],
                        x_blk[:, slot, c, :], ALU.add),
                        waits=[(ve, t_pass2[c]), (pe, mm1), (sc, tsq1)], inc=ve)
                t_xblk_cons[slot] = (ve, t_v[NCH - 1])
                mm1b, mm2b, tsq2 = ln_stats(r_blk, t_v)
                t_r_cons_pe = (pe, mm1b)
                t_r_cons_sc = (sc, tsq2)
                a_mu2, a_s2b, a_rstd2 = ln_rows(mm1b, mm2b, epsB)
                t_psum_free = (sc, a_mu2)
                t_psq_free = (ve, a_s2b)
                c1 = E("vector", lambda v: v.tensor_copy(mu_bf[:], row_mu[:]),
                       waits=[(sc, a_mu2), (pe, t_mubf_cons)], inc=ve)
                c2 = E("vector", lambda v: v.tensor_copy(rstd_bf[:], row_rstd[:]),
                       waits=[(sc, a_rstd2), (pe, b4)], inc=ve)
                t_rows_free = (ve, c2)
                c3 = E("tensor", lambda t: t.matmul(
                    p_nu[0], cs["c_32x128d"][:], mu_bf[0:32, :],
                    start=True, stop=True),
                    waits=[(ve, max(c1, t_nu_free[0]))], inc=pe)
                c4 = E("tensor", lambda t: t.matmul(
                    p_nu[1], cs["c_32x128d"][:], rstd_bf[0:32, :],
                    start=True, stop=True),
                    waits=[(ve, max(c2, t_nu_free[1]))], inc=pe)
                t_mubf_cons = c3
                t_rstdbf_cons = c4
                t_p4 = [0] * NCH
                for c in range(NCH):
                    t_p4[c] = E("vector", lambda v, c=c: v.tensor_tensor(
                        t1_blk[:, c, :], r_blk[:, c, :], p_nu[0], ALU.subtract),
                        waits=[(ve, t_v[c]), (pe, max(c3, mm2b))], inc=ve)
                t_p5 = [0] * NCH
                for c in range(NCH):
                    t_p5[c] = E("vector", lambda v, c=c: v.tensor_tensor(
                        x1_blk[:, c, :], t1_blk[:, c, :], p_nu[1], ALU.mult),
                        waits=[(ve, t_p4[c]), (pe, c4)], inc=ve)
                t_nu_free[0] = t_p4[NCH - 1]
                t_nu_free[1] = t_p5[NCH - 1]
                t_attn_cons = t_p5[NCH - 1]
                t_x1_store = E("gpsimd", lambda g, qb=qb, s=s: g.dma_start(
                    out=xa[s][:, :, qb * QB:(qb + 1) * QB].rearrange("c p t -> p c t"),
                    in_=x1_blk[:]), waits=[(ve, t_p5[NCH - 1])], inc=dgs, inc_by=16)

        pr.barrier_all(allsems)


def prep_a(inputs, n_cores=8):
    x = np.asarray(inputs["L0_attn_x"])
    Wq = np.asarray(inputs["L0_attn_Wq"])
    Wk = np.asarray(inputs["L0_attn_Wk"])
    Wv = np.asarray(inputs["L0_attn_Wv"])
    bq = np.asarray(inputs["L0_attn_bq"])
    bk = np.asarray(inputs["L0_attn_bk"])
    bv = np.asarray(inputs["L0_attn_bv"])
    gamma = np.asarray(inputs["L0_attn_gamma"])
    beta = np.asarray(inputs["L0_attn_beta"])
    consts = consts_a()
    wq_l = np.ascontiguousarray(Wq.reshape(NCH, 128, D))
    wk_l = np.ascontiguousarray(Wk.reshape(NCH, 128, D))
    wv_l = np.ascontiguousarray(Wv.reshape(NCH, 128, D))
    bq_l = np.ascontiguousarray(bq.reshape(NCH, 128).T)
    bk_l = np.ascontiguousarray(bk.reshape(NCH, 128).T)
    bv_l = bv.reshape(1, D)
    gcol_l = np.ascontiguousarray(gamma.reshape(NCH, 128).T)
    gbrow_l = np.stack([gamma, beta], axis=0)
    shared = dict(wq=wq_l, wk=wk_l, wv=wv_l, bq=bq_l, bk=bk_l, bv=bv_l,
                  gcol=gcol_l, gbrow=gbrow_l, **consts)
    in_maps = []
    for core in range(n_cores):
        m = dict(shared)
        m["xin"] = np.ascontiguousarray(x[core * NSEQ:(core + 1) * NSEQ])
        in_maps.append(m)
    return in_maps


CONST_SHAPES_B = {
    "c_128x32": ([128, 32], BF16), "c_1x32": ([1, 32], BF16),
    "c_32x256": ([32, 256], BF16), "c_1x128f": ([1, 128], F32),
}


def consts_b():
    c = consts_a()
    out = {k: c[k] for k in CONST_SHAPES_B if k in c}
    out["c_32x256"] = np.ones((32, 256), dtype=BF)
    out["c_1x128f"] = np.ones((1, 128), dtype=np.float32)
    return out


def emit_b(nc, pr, sems, ctx, dram):
    T, NQB, NT = (globals()["T"], globals()["NQB"], globals()["NT"])
    HS = 2048
    NHC = HS // 128
    RG = T // 8
    xi = dram["xab"]
    sw1, sw3, sw2 = dram["b_sw1"], dram["b_sw3"], dram["b_sw2"]
    w1, w2 = dram["b_w1"], dram["b_w2"]
    sb1c, sb3c, sb2c = dram["b_sb1c"], dram["b_sb3c"], dram["b_sb2c"]
    b1c, b2r = dram["b_b1c"], dram["b_b2r"]
    cin = {k: dram[k] for k in CONST_SHAPES_B}
    xo = dram["xbc"]
    temb = dram["b_temb"]
    if True:
        sb = lambda name, shape, dt=BF16: ctx.enter_context(nc.sbuf_tensor("b_" + name, shape, dt))
        ps = lambda name, shape, dt=F32: ctx.enter_context(nc.psum_tensor("b_" + name, shape, dt))
        arena = sb("arena", [128, 3 * NCH * HS])
        sw1_v = arena[:, 0:NCH * HS].rearrange("p (c d) -> p c d", c=NCH)
        sw3_v = arena[:, NCH * HS:2 * NCH * HS].rearrange("p (c d) -> p c d", c=NCH)
        sw2_v = arena[:, 2 * NCH * HS:2 * NCH * HS + NHC * D].rearrange(
            "p (c d) -> p c d", c=NHC)
        EW = NCH * D
        wexp = [None, None]
        for es in range(2):
            wexp[es] = (arena[:, es * 2 * EW:es * 2 * EW + EW].rearrange(
                "p (c d) -> p c d", c=NCH),
                arena[:, es * 2 * EW + EW:(es + 1) * 2 * EW].rearrange(
                "p (c d) -> p c d", c=NCH))
        x1_s = sb("x1_s", [128, NCH, T])
        facc = sb("facc", [128, NCH, T])
        h_s = sb("h_s", [128, NHC, QB])
        he_sb = sb("he_sb", [128, NCH, RG])
        silu_sb = sb("silu_sb", [128, 1, QB])
        v_blk = h_s[:, 8:16, :]
        t_blk = h_s[:, 0:8, :]
        r2f = sb("r2f", [128, QB])
        b2rep = sb("b2rep", [32, D])
        b2row = sb("b2row", [1, D])
        sb1_sb = sb("sb1_sb", [128, NHC], F32)
        sb3_sb = sb("sb3_sb", [128, NHC], F32)
        sb2_sb = sb("sb2_sb", [128, NCH], F32)
        sb2h = sb("sb2h", [128, NCH], F32)
        b1_sb = sb("b1_sb", [128, 2, NCH], F32)
        cs = {k: sb(k + "_s", sh, dt) for k, (sh, dt) in CONST_SHAPES_B.items()}
        epsB = sb("epsB", [1, 1], F32)
        row_mu = sb("row_mu", [1, QB], F32)
        rec_rowA = row_mu
        row_mu2 = sb("row_mu2", [1, QB], F32)
        den_row = row_mu2
        row_s2 = sb("row_s2", [1, QB], F32)
        rec_rowB = row_s2
        row_rstd = sb("row_rstd", [1, QB], F32)
        pA = ps("pA", [128, QB])
        pB = ps("pB", [128, QB])
        pC = ps("pC", [128, QB])
        pD = ps("pD", [128, QB])
        p_sum = ps("p_sum", [32, QB])
        p_sq = ps("p_sq", [32, QB])
        p_r1 = ps("p_r1", [128, QB])
        p_r2 = ps("p_r2", [128, QB])

        pe, sc, ve, dgp, dgx, dgs = (sems[k] for k in
                                     ("pe", "sc", "ve", "dgp", "dgx", "dgs"))
        dgw = [sems["dgw0"], sems["dgw1"]]
        allsems = list(sems.values())
        E = pr.emit
        E("gpsimd", lambda g: g.dma_start(out=sb("temb_sb", [1, 16], F32)[:],
                                          in_=temb[0:1, 0:16]), inc=dgp, inc_by=16)

        for dst, src_ in [(sb1_sb, sb1c), (sb3_sb, sb3c), (sb2_sb, sb2c)] + \
                         [(cs[k], cin[k]) for k in CONST_SHAPES_B]:
            E("gpsimd", lambda g, d=dst, s_=src_: g.dma_start(out=d[:], in_=s_),
              inc=dgp, inc_by=16)
        LDGP = dgp.n
        E("vector", lambda v: v.memset(epsB[:], 1e-5), inc=ve)
        t_sb2h = E("scalar", lambda c: c.mul(sb2h[:], sb2_sb[:], 1.0),
                   waits=[(dgp, LDGP)], inc=sc)

        for s in range(NSEQ):
            pr.barrier_all(allsems)
            for dst, src_ in [(sw1_v, sw1), (sw3_v, sw3), (sw2_v, sw2)]:
                E("gpsimd", lambda g, d=dst, s_=src_: g.dma_start(
                    out=d, in_=s_.rearrange("c p d -> p c d")), inc=dgp, inc_by=16)
            LDGP = dgp.n
            t_x1 = E("gpsimd", lambda g, s=s: g.dma_start(
                out=x1_s[:], in_=xi[s].rearrange("c p t -> p c t")),
                inc=dgx, inc_by=16)
            XRDY = (dgx, t_x1)

            t_bankG = [0, 0]
            t_bankGs = [0, 0]
            t_bankY = [0, 0]
            t_hs_cons = 0
            t_facc_st = 0
            for tb in range(NQB):
                t_h = [0] * NHC
                for hc in range(NHC):
                    bk = [pA, pB][hc % 2]
                    mm = None
                    for kc in range(NCH):
                        mm = E("tensor", lambda t, hc=hc, kc=kc, tb=tb, bk=bk: t.matmul(
                            bk[:], sw1_v[:, kc, hc * 128:(hc + 1) * 128],
                            x1_s[:, kc, tb * QB:(tb + 1) * QB],
                            start=(kc == 0), stop=(kc == NCH - 1)),
                            waits=[XRDY, (dgp, LDGP), (ve, t_bankG[hc % 2]),
                                   (sc, t_bankGs[hc % 2])], inc=pe)
                    t_sil = E("scalar", lambda sce, hc=hc, bk=bk: sce.activation(
                        silu_sb[:, 0, :], bk[:], AF.Silu,
                        bias=sb1_sb[:, hc:hc + 1], scale=1.0),
                        waits=[(pe, mm), (ve, max(t_bankG[0], t_bankG[1]))], inc=sc)
                    t_bankGs[hc % 2] = t_sil
                    mm = None
                    for kc in range(NCH):
                        mm = E("tensor", lambda t, hc=hc, kc=kc, tb=tb, bk=bk: t.matmul(
                            bk[:], sw3_v[:, kc, hc * 128:(hc + 1) * 128],
                            x1_s[:, kc, tb * QB:(tb + 1) * QB],
                            start=(kc == 0), stop=(kc == NCH - 1)),
                            waits=[(sc, t_sil)], inc=pe)
                    t_h[hc] = E("vector", lambda v, hc=hc, bk=bk: v.scalar_tensor_tensor(
                        h_s[:, hc, :], bk[:], sb3_sb[:, hc:hc + 1],
                        silu_sb[:, 0, :], ALU.add, ALU.mult),
                        waits=[(pe, mm), (sc, t_sil), (pe, t_hs_cons)], inc=ve)
                    t_bankG[hc % 2] = t_h[hc]
                for oc in range(NCH):
                    bk = [pC, pD][oc % 2]
                    mm = None
                    for hc in range(NHC):
                        mm = E("tensor", lambda t, oc=oc, hc=hc, bk=bk: t.matmul(
                            bk[:], sw2_v[:, hc, oc * 128:(oc + 1) * 128],
                            h_s[:, hc, :],
                            start=(hc == 0), stop=(hc == NHC - 1)),
                            waits=[(ve, max(t_h[hc], t_bankY[oc % 2]))], inc=pe)
                    t_bankY[oc % 2] = E("vector", lambda v, oc=oc, tb=tb, bk=bk: v.tensor_scalar(
                        facc[:, oc, tb * QB:(tb + 1) * QB], bk[:],
                        sb2h[:, oc:oc + 1], 0.5, ALU.add, ALU.mult),
                        waits=[(pe, mm), (sc, t_sb2h)], inc=ve)
                t_hs_cons = mm

            t_wld = [0, 0]
            t_wcons = [t_hs_cons, t_hs_cons]
            t_b2ld = [0, 0]
            t_he_cons = 0
            t_bankE = [0, 0]
            t_acc = 0
            t_b2rep_ev = 0
            t_b2rep_cons = 0
            t_b2row_cons = 0
            t_ywar = [t_bankY[0], t_bankY[1]]
            for e in range(8):
                es = e % 2
                E("gpsimd", lambda g, e=e, es=es: g.dma_start(
                    out=wexp[es][0], in_=w1[e].rearrange("c p d -> p c d")),
                    waits=[(pe, t_wcons[es])], inc=dgw[es], inc_by=16)
                E("gpsimd", lambda g, e=e, es=es: g.dma_start(
                    out=wexp[es][1], in_=w2[e].rearrange("c p d -> p c d")),
                    inc=dgw[es], inc_by=16)
                E("gpsimd", lambda g, e=e: g.dma_start(
                    out=b2row[:], in_=b2r[e]),
                    waits=[(pe, t_b2row_cons)], inc=dgw[es], inc_by=16)
                t_wall = E("gpsimd", lambda g, e=e, es=es: g.dma_start(
                    out=b1_sb[:, es, :], in_=b1c[e]),
                    waits=[(sc, t_bankE[es])], inc=dgw[es], inc_by=16)
                t_wld[es] = t_wall
                t_b2ld[es] = t_wall
                t_b1ld = t_wall
                t_bc = t_b2rep_ev
                for half in range(2):
                    mmb = E("tensor", lambda t, h=half: t.matmul(
                        p_r1[0:32, :], cs["c_1x32"][:], b2row[:, h * QB:(h + 1) * QB],
                        start=True, stop=True),
                        waits=[(dgw[es], t_b2ld[es]), (ve, t_bc),
                               (pe, t_b2rep_cons)], inc=pe)
                    t_b2row_cons = mmb
                    t_bc = E("vector", lambda v, h=half: v.tensor_scalar(
                        b2rep[:, h * QB:(h + 1) * QB], p_r1[0:32, :], 1.0 / 32.0,
                        None, ALU.mult), waits=[(pe, mmb)], inc=ve)
                t_b2rep_ev = t_bc
                for r in ((3 * e) % 8, (3 * e + 3) % 8):
                    for hr in range(NCH):
                        bk = [pA, pB][hr % 2]
                        mm = None
                        for kc in range(NCH):
                            mm = E("tensor", lambda t, hr=hr, kc=kc, r=r, es=es, bk=bk: t.matmul(
                                bk[:, 0:RG],
                                wexp[es][0][:, kc, hr * 128:(hr + 1) * 128],
                                x1_s[:, kc, :].rearrange(
                                    "p (n eight) -> p n eight", eight=8)[:, :, r],
                                start=(kc == 0), stop=(kc == NCH - 1)),
                                waits=[(dgw[es], t_wld[es]), XRDY,
                                       (sc, t_bankE[hr % 2]), (ve, t_bankG[hr % 2])],
                                inc=pe)
                        t_bankE[hr % 2] = E("scalar", lambda sce, hr=hr, es=es, bk=bk:
                                            sce.activation(
                            he_sb[:, hr, :], bk[:, 0:RG], AF.Gelu,
                            bias=b1_sb[:, es, hr:hr + 1], scale=1.0),
                            waits=[(pe, mm), (dgw[es], t_b1ld), (pe, t_he_cons)], inc=sc)
                        t_bankG[hr % 2] = 0
                    for oc in range(NCH):
                        bk = [pC, pD][oc % 2]
                        mm = None
                        for hr in range(NCH):
                            mm = E("tensor", lambda t, oc=oc, hr=hr, es=es, bk=bk: t.matmul(
                                bk[:, 0:RG],
                                wexp[es][1][:, hr, oc * 128:(oc + 1) * 128],
                                he_sb[:, hr, :],
                                start=(hr == 0), stop=False),
                                waits=[(sc, t_bankE[hr % 2]), (ve, max(t_ywar[oc % 2], t_acc))],
                                inc=pe)
                        mm = E("tensor", lambda t, oc=oc, bk=bk: t.matmul(
                            bk[:, 0:RG], b2rep[:, oc * 128:(oc + 1) * 128],
                            cs["c_32x256"][:, 0:RG], start=False, stop=True),
                            waits=[(ve, t_b2rep_ev)], inc=pe)
                        t_b2rep_cons = mm
                        t_acc = E("vector", lambda v, oc=oc, r=r, bk=bk: v.scalar_tensor_tensor(
                            facc[:, oc, :].rearrange(
                                "p (n eight) -> p n eight", eight=8)[:, :, r],
                            bk[:, 0:RG], 0.25,
                            facc[:, oc, :].rearrange(
                                "p (n eight) -> p n eight", eight=8)[:, :, r],
                            ALU.mult, ALU.add),
                            waits=[(pe, mm), (ve, t_bankY[oc % 2])], inc=ve)
                        t_bankY[oc % 2] = 0
                        t_ywar[oc % 2] = 0
                        t_he_cons = mm
                t_wcons[es] = mm

            t_psum_free = (ve, 0)
            t_psq_free = (ve, 0)
            t_rows_free = (ve, 0)
            t_mubf_cons = 0
            t_rstdbf_cons = 0
            t_v_cons_pe = (pe, 0)
            t_v_cons_sc = (sc, 0)
            t_st = 0
            for tb in range(NQB):
                t_v = [0] * NCH
                for c in range(NCH):
                    t_v[c] = E("vector", lambda v, c=c, tb=tb: v.tensor_tensor(
                        v_blk[:, c, :], x1_s[:, c, tb * QB:(tb + 1) * QB],
                        facc[:, c, tb * QB:(tb + 1) * QB], ALU.add),
                        waits=[(ve, t_acc), t_v_cons_pe, t_v_cons_sc,
                               (dgs, t_st)], inc=ve)
                mm1 = None
                for c in range(NCH):
                    mm1 = E("tensor", lambda t, c=c: t.matmul(
                        p_sum[:], cs["c_128x32"][:], v_blk[:, c, :],
                        start=(c == 0), stop=(c == NCH - 1)),
                        waits=[(ve, t_v[c]), t_psum_free], inc=pe)
                mm2 = 0
                tsq = 0
                for c in range(NCH):
                    tsq = E("scalar", lambda sce, c=c: sce.activation(
                        r2f[:], v_blk[:, c, :], AF.Square),
                        waits=[(ve, t_v[c]), (pe, mm2)], inc=sc)
                    mm2 = E("tensor", lambda t, c=c: t.matmul(
                        p_sq[:], cs["c_128x32"][:], r2f[:],
                        start=(c == 0), stop=(c == NCH - 1)),
                        waits=[(sc, tsq), t_psq_free], inc=pe)
                a1 = E("scalar", lambda sce: sce.mul(row_mu[:], p_sum[0:1, :], 1.0 / D),
                       waits=[(pe, mm1), t_rows_free], inc=sc)
                a2 = E("scalar", lambda sce: sce.activation(row_mu2[:], row_mu[:], AF.Square),
                       waits=[(sc, a1)], inc=sc)
                a3 = E("vector", lambda v: v.scalar_tensor_tensor(
                    row_s2[:], p_sq[0:1, :], 1.0 / D, row_mu2[:], ALU.mult, ALU.subtract),
                    waits=[(pe, mm2), (sc, a2)], inc=ve)
                a4 = E("scalar", lambda sce: sce.activation(
                    row_mu2[:], row_s2[:], AF.Ln, bias=epsB[:], scale=1.0),
                    waits=[(ve, a3)], inc=sc)
                a5 = E("scalar", lambda sce: sce.activation(
                    row_rstd[:], row_mu2[:], AF.Exp, bias=0.0, scale=-0.5),
                    waits=[(sc, a4)], inc=sc)
                t_psum_free = (sc, a1)
                t_psq_free = (ve, a3)
                c3 = E("tensor", lambda t: t.matmul(
                    p_r1[:], cs["c_1x128f"][:], row_mu[:], start=True, stop=True),
                    waits=[(sc, a5)], inc=pe)
                c4 = E("tensor", lambda t: t.matmul(
                    p_r2[:], cs["c_1x128f"][:], row_rstd[:], start=True, stop=True),
                    waits=[(sc, a5)], inc=pe)
                t_rows_free = (pe, c4)
                t_p4 = [0] * NCH
                for c in range(NCH):
                    t_p4[c] = E("vector", lambda v, c=c: v.tensor_tensor(
                        t_blk[:, c, :], v_blk[:, c, :], p_r1[:], ALU.subtract),
                        waits=[(ve, t_v[c]), (pe, max(c3, mm2))], inc=ve)
                t_p5 = [0] * NCH
                for c in range(NCH):
                    t_p5[c] = E("vector", lambda v, c=c: v.tensor_tensor(
                        v_blk[:, c, :], t_blk[:, c, :], p_r2[:], ALU.mult),
                        waits=[(ve, t_p4[c]), (pe, c4)], inc=ve)
                t_v_cons_pe = (pe, mm1)
                t_v_cons_sc = (sc, tsq)
                t_st = E("gpsimd", lambda g, tb=tb, s=s: g.dma_start(
                    out=xo[s][:, :, tb * QB:(tb + 1) * QB].rearrange("c p t -> p c t"),
                    in_=v_blk[:]), waits=[(ve, t_p5[NCH - 1])], inc=dgs, inc_by=16)
                t_v_cons_ve_store = t_st
                t_v_cons_pe = (pe, mm1)
        pr.barrier_all(allsems)


def prep_b(inputs, n_cores=8, xin_override=None):
    sw1 = np.asarray(inputs["L0_ffn_sw1"])
    sw3 = np.asarray(inputs["L0_ffn_sw3"])
    sw2 = np.asarray(inputs["L0_ffn_sw2"])
    sb1 = np.asarray(inputs["L0_ffn_sb1"])
    sb3 = np.asarray(inputs["L0_ffn_sb3"])
    sb2 = np.asarray(inputs["L0_ffn_sb2"])
    w1 = np.asarray(inputs["L0_ffn_w1"])
    w2 = np.asarray(inputs["L0_ffn_w2"])
    b1 = np.asarray(inputs["L0_ffn_b1"])
    b2 = np.asarray(inputs["L0_ffn_b2"])
    HS = 2048
    shared = dict(
        sw1=np.ascontiguousarray(sw1.reshape(NCH, 128, HS)),
        sw3=np.ascontiguousarray(sw3.reshape(NCH, 128, HS)),
        sw2=np.ascontiguousarray(sw2.reshape(HS // 128, 128, D)),
        w1=np.ascontiguousarray(w1.reshape(8, NCH, 128, D)),
        w2=np.ascontiguousarray(w2.reshape(8, NCH, 128, D)),
        sb1c=np.ascontiguousarray(sb1.reshape(HS // 128, 128).T),
        sb3c=np.ascontiguousarray(sb3.reshape(HS // 128, 128).T),
        sb2c=np.ascontiguousarray(sb2.reshape(NCH, 128).T),
        b1c=np.ascontiguousarray(b1.reshape(8, NCH, 128).transpose(0, 2, 1)),
        b2r=np.ascontiguousarray(b2.reshape(8, 1, D)),
        **consts_b())
    in_maps = []
    for core in range(n_cores):
        m = dict(shared)
        m["xi"] = np.ascontiguousarray(xin_override[core])
        in_maps.append(m)
    return in_maps


CONST_SHAPES_C = {
    "c_128x32": ([128, 32], BF16), "c_1x128f": ([1, 128], F32),
    "c_tri128": ([128, 128], BF16),
    "c_E2": ([128, 2], BF16),
}


def consts_c():
    c = {}
    c["c_128x32"] = np.ones((128, 32), dtype=BF)
    c["c_1x128f"] = np.ones((1, 128), dtype=np.float32)
    k = np.arange(128)[:, None]
    j = np.arange(128)[None, :]
    c["c_tri128"] = (j >= k).astype(BF)
    e2 = np.zeros((128, 2), dtype=BF)
    e2[0:64, 0] = 1
    e2[64:128, 1] = 1
    c["c_E2"] = e2
    return c


def emit_c(nc, pr, sems, ctx, dram):
    T, NQB, NT = (globals()["T"], globals()["NQB"], globals()["NT"])
    NU = T // 128
    NCK = T // 64
    xi = dram["xbc"]
    wq, wk, wv, wo = dram["c_wq"], dram["c_wk"], dram["c_wv"], dram["c_wo"]
    bo = dram["c_bo"]
    cin = {k: dram[k] for k in CONST_SHAPES_C}
    xo = dram["xcd"]
    if True:
        sb = lambda name, shape, dt=BF16: ctx.enter_context(nc.sbuf_tensor("c_" + name, shape, dt))
        ps = lambda name, shape, dt=F32: ctx.enter_context(nc.psum_tensor("c_" + name, shape, dt))
        arena = sb("arena", [128, 2 * NCH * D])
        wslot = [arena[:, 0:NCH * D].rearrange("p (c d) -> p c d", c=NCH),
                 arena[:, NCH * D:2 * NCH * D].rearrange("p (c d) -> p c d", c=NCH)]
        x_tm = sb("x_tm", [128, D])
        x2b = sb("x2b", [128, NCH, QB])
        q_fm = sb("q_fm", [128, NCH, T])
        kl_fm = sb("kl_fm", [128, 2, T])
        kg_tm = sb("kg_tm", [128, NT, 768])
        v_tm = sb("v_tm", [128, NT * 1024])
        eq = sb("eq", [128, 6, T])
        q2f = eq[:].rearrange("p c t -> p (c t)")[:, 6 * T - QB:6 * T]
        elu2 = q2f
        attn_fm = sb("attn_fm", [128, NCH, T])
        ctx_sb = sb("ctx_sb", [128, 6, 64])
        rdm_sb = ctx_sb[0:64, 0:2, :].rearrange("p a b -> p (a b)")
        NCKl = T // 64
        scr_f32 = sb("scr_f32", [128, max(12 * NCKl, 384)], F32)
        ctx_acc1 = scr_f32[:, 0:384].rearrange("p (g c) -> p g c", g=6)
        ctx_acc = [ctx_acc1, ctx_acc1]
        bkc = scr_f32[:, 0:6 * NCKl].rearrange("p (g c) -> p g c", g=6)
        bkp = scr_f32[:, 6 * NCKl:12 * NCKl].rearrange("p (g c) -> p g c", g=6)
        rbk = sb("rbk", [128, 6, NCKl + 1])
        elu1 = sb("elu1", [128, QB])
        bacc = sb("bacc", [1, 64])
        rowb1 = sb("rowb1", [1, 1], F32)
        rowb2 = sb("rowb2", [1, 1], F32)
        rowb3 = sb("rowb3", [1, 1], F32)
        bias_loc = sb("bias_loc", [128, 1], F32)
        bias_glob = sb("bias_glob", [128, 1], F32)
        bo_sb = sb("bo_sb", [128, NCH], F32)

        cs = {k: sb(k + "_s", sh, dt) for k, (sh, dt) in CONST_SHAPES_C.items()}
        epsB = sb("epsB", [1, 1], F32)
        row_mu = sb("row_mu", [1, QB], F32)
        rec_rowA = row_mu
        row_mu2 = sb("row_mu2", [1, QB], F32)
        den_row = row_mu2
        row_s2 = sb("row_s2", [1, QB], F32)
        rec_rowB = row_s2
        row_rstd = sb("row_rstd", [1, QB], F32)
        r2f = elu1
        if T >= 2048:
            eqf = eq[:].rearrange("p c t -> p (c t)")
            r_blk = eqf[:, 0:NCH * QB].rearrange("p (c t) -> p c t", c=NCH)
            t_blk = eqf[:, NCH * QB:2 * NCH * QB].rearrange("p (c t) -> p c t", c=NCH)
        else:
            r_blk = sb("r_blk", [128, NCH, QB])
            t_blk = sb("t_blk", [128, NCH, QB])

        p_s = [ps("p_s0", [128, QB]), ps("p_s1", [128, QB])]
        p_big = ps("p_big", [128, 1024])
        p_ctx = p_big[:, 0:384]
        p_at = p_big[:, 512:896]
        p_bk = p_big[:, 896:908].rearrange("p (g two) -> p g two", two=2)
        p_loc = p_big[:, 0:128]
        p_rs = ps("p_rs", [128, QB])
        p_sum = ps("p_sum", [32, QB])
        p_sq = ps("p_sq", [32, QB])

        pe, sc, ve, dgp, dgs = (sems[k] for k in ("pe", "sc", "ve", "dgp", "dgs"))
        dgw = [sems["dgw0"], sems["dgw1"]]
        dgx = [sems["dgx0"], sems["dgx1"]]
        dst_ = [sems["ds0"], sems["ds1"]]
        allsems = list(sems.values())
        E = pr.emit

        for dst, src_ in [(bo_sb, bo)] + [(cs[k], cin[k]) for k in CONST_SHAPES_C]:
            E("gpsimd", lambda g, d=dst, s_=src_: g.dma_start(out=d[:], in_=s_),
              inc=dgp, inc_by=16)
        LDGP = dgp.n
        E("vector", lambda v: v.memset(epsB[:], 1e-5), inc=ve)
        t_bacc0 = E("vector", lambda v: v.memset(bacc[:], 0.0), inc=ve)

        t_w = [0, 0]
        t_wcons = [0, 0]
        t_tr_prev = [0, 0]
        t_ms_prev = [0, 0]

        def load_w(slot, mat):
            t_w[slot] = E("gpsimd", lambda g, sl=slot, m=mat: g.dma_start(
                out=wslot[sl], in_=m.rearrange("c p d -> p c d")),
                waits=[(pe, t_wcons[slot])], inc=dgw[slot], inc_by=16)
            return t_w[slot]

        def load_xb(s, tb, t_prev_cons):
            tl = E("gpsimd", lambda g, tb=tb, s=s: g.dma_start(
                out=x2b[:], in_=xi[s][:, :, tb * QB:(tb + 1) * QB].rearrange(
                    "c p t -> p c t")),
                waits=[t_prev_cons], inc=dgx[tb % 2], inc_by=16)
            return [(dgx[tb % 2], tl)]

        for s in range(NSEQ):
            pr.barrier_all(allsems)
            t_tr_prev = [0, 0]
            load_w(0, wq)
            load_w(1, wk)
            t_bank = [0, 0]
            t_bank_ve = [0, 0]
            t_xcons = (pe, 0)
            t_qcons = 0
            t_klcons = 0
            t_kgcons = 0
            t_vcons = 0
            for tb in range(NQB):
                wrdy = load_xb(s, tb, t_xcons)
                mmlast = None
                for oc in range(NCH):
                    bi = oc % 2
                    mm = None
                    for kc in range(NCH):
                        mm = E("tensor", lambda t, oc=oc, kc=kc, bi=bi: t.matmul(
                            p_s[bi][:], wslot[0][:, kc, oc * 128:(oc + 1) * 128],
                            x2b[:, kc, :], start=(kc == 0), stop=(kc == NCH - 1)),
                            waits=wrdy + [(dgw[0], t_w[0]), (sc, t_bank[bi]),
                                          (ve, t_bank_ve[bi])], inc=pe)
                    t_bank[bi] = E("scalar", lambda sce, oc=oc, tb=tb, bi=bi: sce.copy(
                        q_fm[:, oc, tb * QB:(tb + 1) * QB], p_s[bi][:]),
                        waits=[(pe, mm)], inc=sc)
                    t_lastq = mm
                for oc in range(2):
                    bi = oc % 2
                    mm = None
                    for kc in range(NCH):
                        mm = E("tensor", lambda t, oc=oc, kc=kc, bi=bi: t.matmul(
                            p_s[bi][:], wslot[1][:, kc, oc * 128:(oc + 1) * 128],
                            x2b[:, kc, :], start=(kc == 0), stop=(kc == NCH - 1)),
                            waits=[(dgw[1], t_w[1]), (sc, t_bank[bi]),
                                   (ve, t_bank_ve[bi])], inc=pe)
                    t_bank[bi] = E("scalar", lambda sce, oc=oc, tb=tb, bi=bi: sce.copy(
                        kl_fm[:, oc, tb * QB:(tb + 1) * QB], p_s[bi][:]),
                        waits=[(pe, mm)], inc=sc)
                for j in range(4):
                    i = tb * 4 + j
                    mm0 = mm1 = None
                    for kc in range(NCH):
                        mm0 = E("tensor", lambda t, j=j, kc=kc: t.matmul(
                            p_s[0][:], x2b[:, kc, j * 128:(j + 1) * 128],
                            wslot[1][:, kc, 256:768],
                            start=(kc == 0), stop=(kc == NCH - 1)),
                            waits=[(sc, t_bank[0]), (ve, t_bank_ve[0])], inc=pe)
                        mm1 = E("tensor", lambda t, j=j, kc=kc: t.matmul(
                            p_s[1][:, 0:256], x2b[:, kc, j * 128:(j + 1) * 128],
                            wslot[1][:, kc, 768:1024],
                            start=(kc == 0), stop=(kc == NCH - 1)),
                            waits=[(sc, t_bank[1]), (ve, t_bank_ve[1])], inc=pe)
                    for gsl, (bank, width, c0) in enumerate(
                            [(p_s[0], QB, 0), (p_s[1], 256, QB)]):
                        e1 = E("vector", lambda v, bank=bank, w=width: v.tensor_scalar(
                            elu1[:, 0:w], bank[:, 0:w], 0.0, None, ALU.min),
                            waits=[(pe, mm1)], inc=ve)
                        e2 = E("scalar", lambda sce, w=width: sce.activation(
                            elu2[:, 0:w], elu1[:, 0:w], AF.Exp, bias=0.0, scale=1.0),
                            waits=[(ve, e1)], inc=sc)
                        tok = E("vector", lambda v, i=i, bank=bank, w=width, c0=c0:
                                v.scalar_tensor_tensor(
                            kg_tm[:, i, c0:c0 + w], bank[:, 0:w], 0.0,
                            elu2[:, 0:w], ALU.max, ALU.add),
                            waits=[(sc, e2), (pe, t_kgcons)], inc=ve)
                        t_bank_ve[gsl] = tok
                        t_bank[gsl] = e2
                    mmlast = mm1
                t_xcons = (pe, mmlast if mmlast is not None else 0)
            t_wcons[0] = t_lastq
            t_wcons[1] = mmlast
            t_nq = 0
            idx = 0
            t_red = t_bacc0
            for srcbuf, nch_, tok_rdy in [(q_fm, NCH, None), (kl_fm, 2, None)]:
                for c in range(nch_):
                    for tb in range(NQB):
                        tsq = E("scalar", lambda sce, c=c, tb=tb, sbuf=srcbuf: sce.activation(
                            q2f[:], sbuf[:, c, tb * QB:(tb + 1) * QB], AF.Square),
                            waits=[(sc, t_bank[0]), (sc, t_bank[1]), (pe, t_nq),
                                   (ve, t_bank_ve[0]), (ve, t_bank_ve[1])], inc=sc)
                        mm = E("tensor", lambda t: t.matmul(
                            p_sum[:], cs["c_128x32"][:], q2f[:],
                            start=True, stop=True),
                            waits=[(sc, tsq), (ve, t_red)], inc=pe)
                        t_nq = mm
                        t_red = E("vector", lambda v, idx=idx: v.reduce_max(
                            bacc[0:1, idx:idx + 1], p_sum[0:1, :], axis=AX.X),
                            waits=[(pe, mm)], inc=ve)
                        idx += 1
            nq_total = idx
            b1 = E("vector", lambda v: v.reduce_max(
                rowb1[:], bacc[0:1, 0:NCH * NQB], axis=AX.X),
                waits=[(ve, t_red)], inc=ve)
            b2 = E("vector", lambda v: v.reduce_max(
                rowb2[:], bacc[0:1, NCH * NQB:nq_total], axis=AX.X),
                waits=[(ve, t_red)], inc=ve)
            l1 = E("scalar", lambda sce: sce.activation(rowb1[:], rowb1[:], AF.Ln)
                   if False else sce.activation(rowb3[:], rowb1[:], AF.Ln),
                   waits=[(ve, b1)], inc=sc)
            l2 = E("scalar", lambda sce: sce.activation(
                rec_rowB[0:1, 0:1], rowb2[:], AF.Ln), waits=[(ve, b2)], inc=sc)
            l3 = E("vector", lambda v: v.tensor_tensor(
                rowb1[:], rowb3[:], rec_rowB[0:1, 0:1], ALU.add),
                waits=[(sc, l1), (sc, l2)], inc=ve)
            l4 = E("scalar", lambda sce: sce.activation(rowb1[:], rowb1[:], AF.Exp)
                   if False else sce.activation(rowb2[:], rowb1[:], AF.Exp,
                                                bias=0.0, scale=0.5),
                   waits=[(ve, l3)], inc=sc)
            l5 = E("vector", lambda v: v.tensor_scalar(
                rowb2[:], rowb2[:], -0.125, None, ALU.mult)
                if False else v.tensor_scalar(rowb1[:], rowb2[:], -0.125, None, ALU.mult),
                waits=[(sc, l4)], inc=ve)
            gq = E("scalar", lambda sce: sce.activation(rowb2[:], rowb3[:], AF.Exp,
                                                        bias=0.0, scale=0.5),
                   waits=[(ve, l5)], inc=sc)
            gq2 = E("vector", lambda v: v.tensor_scalar(
                rowb3[:], rowb2[:], -1.0, None, ALU.mult),
                waits=[(sc, gq), (ve, l5)], inc=ve)
            bm1 = E("tensor", lambda t: t.matmul(
                p_rs[:, 0:1], cs["c_1x128f"][:], rowb1[:], start=True, stop=True),
                waits=[(ve, l5)], inc=pe)
            bl = E("vector", lambda v: v.tensor_copy(bias_loc[:], p_rs[:, 0:1]),
                   waits=[(pe, bm1)], inc=ve)
            bm2 = E("tensor", lambda t: t.matmul(
                p_rs[:, 0:1], cs["c_1x128f"][:], rowb3[:], start=True, stop=True),
                waits=[(ve, max(gq2, bl))], inc=pe)
            bg = E("vector", lambda v: v.tensor_copy(bias_glob[:], p_rs[:, 0:1]),
                   waits=[(pe, bm2)], inc=ve)
            t_prs_free = bg

            load_w(0, wv)
            t_vev = 0
            t_xcons2 = t_xcons
            for tb in range(NQB):
                wrdy = load_xb(s, tb, t_xcons2)
                for j in range(4):
                    i = tb * 4 + j
                    mm = None
                    for kc in range(NCH):
                        for half in range(2):
                            mm = E("tensor", lambda t, j=j, kc=kc, hf=half: t.matmul(
                                p_big[:, hf * QB:(hf + 1) * QB],
                                x2b[:, kc, j * 128:(j + 1) * 128],
                                wslot[0][:, kc, hf * QB:(hf + 1) * QB],
                                start=(kc == 0), stop=(kc == NCH - 1)),
                                waits=wrdy + [(dgw[0], t_w[0]), (sc, t_vev),
                                              (ve, 0)], inc=pe)
                    t_vev = E("scalar", lambda sce, i=i: sce.copy(
                        v_tm[:, i * 1024:(i + 1) * 1024], p_big[:]),
                        waits=[(pe, mm)], inc=sc)
                t_xcons2 = (pe, mm)
                t_wcons[0] = mm
            load_w(1, wo)

            t_praw_cons = 0
            t_pb_cons = 0
            t_elu2_cons = 0
            t_rdm_cons = 0
            t_attnw = 0
            for h in range(4):
                hb = (h % 2) * 64
                hcl = h // 2
                for u in range(NU):
                    parts = ([u - 1] if u > 0 else []) + [u]
                    olast = None
                    dlast = None
                    for pi, kt in enumerate(parts):
                        bi = kt % 2
                        smm = E("tensor", lambda t, hb=hb, hcl=hcl, kt=kt, u=u, bi=bi:
                                t.matmul(
                            p_s[bi][:, 0:128],
                            kl_fm[hb:hb + 64, hcl, kt * 128:(kt + 1) * 128],
                            q_fm[hb:hb + 64, hcl, u * 128:(u + 1) * 128],
                            start=True, stop=True),
                            waits=[(sc, t_bank[bi]), (ve, t_bank_ve[bi])], inc=pe)
                        t_qcons = smm
                        ex = E("scalar", lambda sce, bi=bi: sce.activation(
                            elu1[:, 0:128], p_s[bi][:, 0:128], AF.Exp,
                            bias=bias_loc[:], scale=0.125),
                            waits=[(pe, smm), (ve, max(t_praw_cons, bl))], inc=sc)
                        t_bank[bi] = ex
                        t_bank_ve[bi] = 0
                        if kt == u:
                            pm = E("vector", lambda v: v.tensor_tensor(
                                elu2[:, 0:128], elu1[:, 0:128], cs["c_tri128"][:],
                                ALU.mult), waits=[(sc, ex), (pe, t_elu2_cons)], inc=ve)
                            psrc = elu2
                            t_praw_cons = pm
                            pwait = [(ve, pm)]
                        else:
                            psrc = elu1
                            pwait = [(sc, ex)]
                        first, last = (pi == 0), (pi == len(parts) - 1)
                        dlast = E("tensor", lambda t, psrc=psrc, first=first,
                                  last=last: t.matmul(
                            p_big[0:32, 512:640], cs["c_128x32"][:],
                            psrc[:, 0:128], start=first, stop=last),
                            waits=pwait + [(sc, t_vev), (ve, t_pb_cons)], inc=pe)
                        olast = E("tensor", lambda t, h=h, kt=kt, psrc=psrc,
                                  first=first, last=last: t.matmul(
                            p_loc[0:64, :], v_tm[:, kt * 1024 + h * 64:
                                                 kt * 1024 + h * 64 + 64],
                            psrc[:, 0:128], start=first, stop=last), inc=pe)
                        if psrc is elu2:
                            t_elu2_cons = dlast
                    d1 = E("vector", lambda v: v.tensor_scalar(
                        den_row[0:1, 0:128], p_big[0:1, 512:640], 1e-30, None, ALU.add),
                        waits=[(pe, olast)], inc=ve)
                    d2 = E("vector", lambda v: v.reciprocal(
                        rec_rowA[0:1, 0:128], den_row[0:1, 0:128]),
                        waits=[(ve, d1)], inc=ve)
                    d4 = E("tensor", lambda t: t.matmul(
                        p_rs[0:64, 0:128], cs["c_1x128f"][:, 0:64], rec_rowA[0:1, 0:128],
                        start=True, stop=True),
                        waits=[(ve, d2), (ve, t_prs_free), (pe, t_rdm_cons)], inc=pe)
                    d5 = E("vector", lambda v: v.tensor_copy(
                        rdm_sb[:], p_rs[0:64, 0:128]), waits=[(pe, d4)], inc=ve)
                    t_rdm_cons = d4
                    t_attnw = E("vector", lambda v, hb=hb, hcl=hcl, u=u: v.tensor_tensor(
                        attn_fm[hb:hb + 64, hcl, u * 128:(u + 1) * 128],
                        p_loc[0:64, :], rdm_sb[:], ALU.mult),
                        waits=[(ve, d5)], inc=ve)
                    t_pb_cons = t_attnw
                    t_prs_free = d5

            t_eq = [0] * 6
            for pg in range(6):
                tk = 0
                for tb in range(NQB):
                    tk = E("scalar", lambda sce, pg=pg, tb=tb: sce.activation(
                        eq[:, pg, tb * QB:(tb + 1) * QB],
                        q_fm[:, 2 + pg, tb * QB:(tb + 1) * QB],
                        AF.Exp, bias=bias_glob[:], scale=1.0),
                        waits=[(ve, max(bg, t_attnw)), (pe, t_qcons),
                               (pe, t_elu2_cons)], inc=sc)
                t_eq[pg] = tk
            t_bkev = 0
            for i in range(NT):
                mm = None
                for pg in range(6):
                    for rr in range(2):
                        g = 4 + pg * 2 + rr
                        mm = E("tensor", lambda t, i=i, pg=pg, rr=rr, g=g: t.matmul(
                            p_bk[rr * 64:rr * 64 + 64, pg, :],
                            kg_tm[:, i, (g - 4) * 64:(g - 4) * 64 + 64],
                            cs["c_E2"][:], start=True, stop=True),
                            waits=[(ve, t_bank_ve[0]), (ve, t_bank_ve[1]),
                                   (ve, t_bkev)], inc=pe)
                t_bkev = E("vector", lambda v, i=i: v.tensor_copy(
                    bkc[:, :, 2 * i:2 * i + 2], p_bk[:]),
                    waits=[(pe, mm)], inc=ve)
            cur, oth = bkc, bkp
            t_cs = t_bkev
            sh = 1
            while sh < NCK:
                t1 = E("vector", lambda v, cur=cur, oth=oth, sh=sh: v.tensor_tensor(
                    oth[:, :, sh:NCK], cur[:, :, sh:NCK], cur[:, :, 0:NCK - sh],
                    ALU.add), waits=[(ve, t_cs)], inc=ve)
                t2 = E("vector", lambda v, cur=cur, oth=oth, sh=sh: v.tensor_copy(
                    oth[:, :, 0:sh], cur[:, :, 0:sh]),
                    waits=[(ve, t_cs)], inc=ve)
                t_cs = max(t1, t2)
                cur, oth = oth, cur
                sh *= 2
            tnv = E("vector", lambda v: v.memset(rbk[:, :, 0:1], 1.0), waits=[], inc=ve)
            scratch = oth
            ta = E("vector", lambda v, cur=cur, scratch=scratch: v.tensor_scalar(
                scratch[:, :, :], cur[:, :, :], 1e-6, None, ALU.add),
                waits=[(ve, t_cs)], inc=ve)
            tb_ = E("vector", lambda v, scratch=scratch, cur=cur: v.reciprocal(
                cur[:, :, :], scratch[:, :, :]), waits=[(ve, ta)], inc=ve)
            tb2_ = E("vector", lambda v, cur=cur: v.tensor_copy(
                rbk[:, :, 1:NCK + 1], cur[:, :, :]), waits=[(ve, tb_)], inc=ve)
            t_rbk = max(tnv, tb2_)
            t_qp = [0] * 6
            for pg in range(6):
                t_qp[pg] = E("vector", lambda v, pg=pg: v.tensor_tensor(
                    attn_fm[:, 2 + pg, :].rearrange("p (c n) -> p c n", n=64),
                    eq[:, pg, :].rearrange("p (c n) -> p c n", n=64),
                    rbk[:, pg, 0:NCK].broadcast_to([128, NCK, 64]), ALU.mult),
                    waits=[(ve, t_rbk), (sc, t_eq[pg]), (ve, t_attnw)], inc=ve)
            t_den_mm = 0
            t_qprime = [0] * 6
            for pg in range(6):
                tlast = 0
                for tb in range(NQB):
                    mmA = E("tensor", lambda t, pg=pg, tb=tb: t.matmul(
                        p_sum[:], cs["c_128x32"][0:64, :],
                        eq[0:64, pg, tb * QB:(tb + 1) * QB], start=True, stop=True),
                        waits=[(sc, t_eq[pg]), (ve, t_den_ve(0))] if False else
                              [(sc, t_eq[pg])], inc=pe)
                    mmB = E("tensor", lambda t, pg=pg, tb=tb: t.matmul(
                        p_sq[:], cs["c_128x32"][64:128, :],
                        eq[64:128, pg, tb * QB:(tb + 1) * QB], start=True, stop=True),
                        inc=pe)
                    dA = E("vector", lambda v: v.tensor_scalar(
                        den_row[:], p_sum[0:1, :], 1e-30, None, ALU.add),
                        waits=[(pe, mmA)], inc=ve)
                    rA = E("vector", lambda v: v.reciprocal(rec_rowA[:], den_row[:]),
                           waits=[(ve, dA)], inc=ve)
                    dB = E("vector", lambda v: v.tensor_scalar(
                        den_row[:], p_sq[0:1, :], 1e-30, None, ALU.add),
                        waits=[(pe, mmB), (ve, rA)], inc=ve)
                    rB = E("vector", lambda v: v.reciprocal(rec_rowB[:], den_row[:]),
                           waits=[(ve, dB)], inc=ve)
                    bA = E("tensor", lambda t: t.matmul(
                        p_rs[0:64, :], cs["c_1x128f"][:, 0:64], rec_rowA[:],
                        start=True, stop=True),
                        waits=[(ve, rA), (ve, t_prs_free), (pe, t_rdm_cons)], inc=pe)
                    bB = E("tensor", lambda t: t.matmul(
                        p_rs[64:128, :], cs["c_1x128f"][:, 0:64], rec_rowB[:],
                        start=True, stop=True), waits=[(ve, rB)], inc=pe)
                    t_rdm_cons = bB
                    tlast = E("vector", lambda v, pg=pg, tb=tb: v.tensor_tensor(
                        eq[:, pg, tb * QB:(tb + 1) * QB],
                        attn_fm[:, 2 + pg, tb * QB:(tb + 1) * QB],
                        p_rs[:], ALU.mult),
                        waits=[(pe, bB), (ve, t_qp[pg])], inc=ve)
                    t_prs_free = tlast
                    t_den_mm = mmB
                t_qprime[pg] = tlast
            t_ctx0 = E("vector", lambda v: v.memset(ctx_sb[:], 0.0),
                       waits=[(ve, t_attnw)], inc=ve)
            t_ctx1 = E("vector", lambda v: v.memset(ctx_acc[1][:], 0.0),
                       waits=[(ve, t_rbk)], inc=ve)
            t_snap = max(t_ctx0, t_ctx1)
            t_atev = 0
            t_ctxmm = 0
            t_accr = 0
            for c in range(NCK):
                atl = None
                for pg in range(6):
                    for rr in range(2):
                        gb = rr * 64
                        atl = E("tensor", lambda t, pg=pg, gb=gb, c=c: t.matmul(
                            p_at[gb:gb + 64, :].rearrange(
                                "p (g n) -> p g n", n=64)[:, pg, :],
                            ctx_sb[gb:gb + 64, pg, :],
                            eq[gb:gb + 64, pg, c * 64:(c + 1) * 64],
                            start=True, stop=True),
                            waits=[(ve, max(t_snap, t_qprime[pg], t_atev))], inc=pe)
                t_atev = E("vector", lambda v, c=c: v.tensor_copy(
                    attn_fm[:, 2:8, c * 64:(c + 1) * 64],
                    p_at[:].rearrange("p (g n) -> p g n", n=64)),
                    waits=[(pe, atl), (pe, t_ctxmm)], inc=ve)
                if c < NCK - 1:
                    mm = None
                    for pg in range(6):
                        for rr in range(2):
                            g = 4 + pg * 2 + rr
                            i, half = c // 2, (c % 2) * 64
                            mm = E("tensor", lambda t, pg=pg, rr=rr, g=g, i=i,
                                   half=half: t.matmul(
                                p_ctx[rr * 64:rr * 64 + 64, :].rearrange(
                                    "p (g n) -> p g n", n=64)[:, pg, :],
                                kg_tm[half:half + 64, i,
                                      (g - 4) * 64:(g - 4) * 64 + 64],
                                v_tm[half:half + 64, i * 1024 + g * 64:
                                     i * 1024 + g * 64 + 64],
                                start=True, stop=True),
                                waits=[(ve, t_accr), (sc, t_vev)], inc=pe)
                    t_ctxmm = mm
                    cur, prv = ctx_acc[c % 2], ctx_acc[(c + 1) % 2]
                    t_accr = E("vector", lambda v, cur=cur, prv=prv: v.tensor_tensor(
                        cur[:], prv[:], p_ctx[:].rearrange("p (g n) -> p g n", n=64),
                        ALU.add), waits=[(pe, mm), (ve, t_snap)], inc=ve)
                    t_snap = E("vector", lambda v, cur=cur: v.tensor_copy(
                        ctx_sb[:], cur[:]), waits=[(ve, t_accr)], inc=ve)
            t_attn_done = t_atev

            t_psum_free = (ve, 0)
            t_psq_free = (ve, 0)
            t_rows_free = (ve, 0)
            t_v_cons_pe = (pe, 0)
            t_v_cons_sc = (sc, 0)
            t_st = 0
            t_xc = t_xcons2
            for tb in range(NQB):
                wrdy = load_xb(s, tb, t_xc)
                t_r = [0] * NCH
                for oc in range(NCH):
                    bi = oc % 2
                    mm = None
                    for kc in range(NCH):
                        mm = E("tensor", lambda t, oc=oc, kc=kc, tb=tb, bi=bi: t.matmul(
                            p_s[bi][:], wslot[1][:, kc, oc * 128:(oc + 1) * 128],
                            attn_fm[:, kc, tb * QB:(tb + 1) * QB],
                            start=(kc == 0), stop=(kc == NCH - 1)),
                            waits=[(dgw[1], t_w[1]), (ve, t_attn_done),
                                   (sc, t_bank[bi]), (ve, t_bank_ve[bi]),
                                   (ve, t_rblk_war(0))] if False else
                                  [(dgw[1], t_w[1]), (ve, t_attn_done),
                                   (sc, t_bank[bi]), (ve, t_bank_ve[bi])], inc=pe)
                    t_r[oc] = E("vector", lambda v, oc=oc, bi=bi: v.scalar_tensor_tensor(
                        r_blk[:, oc, :], p_s[bi][:], bo_sb[:, oc:oc + 1],
                        x2b[:, oc, :], ALU.add, ALU.add),
                        waits=[(pe, mm), (dgp, LDGP), t_v_cons_pe, t_v_cons_sc,
                               (dgs, t_st)] + wrdy,
                        inc=ve)
                    t_bank_ve[bi] = t_r[oc]
                    t_bank[bi] = 0
                t_xc = (ve, t_r[NCH - 1])
                mm1 = None
                for c in range(NCH):
                    mm1 = E("tensor", lambda t, c=c: t.matmul(
                        p_sum[:], cs["c_128x32"][:], r_blk[:, c, :],
                        start=(c == 0), stop=(c == NCH - 1)),
                        waits=[(ve, t_r[c]), t_psum_free], inc=pe)
                mm2 = 0
                tsq = 0
                for c in range(NCH):
                    tsq = E("scalar", lambda sce, c=c: sce.activation(
                        r2f[:], r_blk[:, c, :], AF.Square),
                        waits=[(ve, t_r[c]), (pe, mm2)], inc=sc)
                    mm2 = E("tensor", lambda t, c=c: t.matmul(
                        p_sq[:], cs["c_128x32"][:], r2f[:],
                        start=(c == 0), stop=(c == NCH - 1)),
                        waits=[(sc, tsq), t_psq_free], inc=pe)
                a1 = E("scalar", lambda sce: sce.mul(row_mu[:], p_sum[0:1, :], 1.0 / D),
                       waits=[(pe, mm1), t_rows_free], inc=sc)
                a2 = E("scalar", lambda sce: sce.activation(row_mu2[:], row_mu[:],
                                                            AF.Square),
                       waits=[(sc, a1)], inc=sc)
                a3 = E("vector", lambda v: v.scalar_tensor_tensor(
                    row_s2[:], p_sq[0:1, :], 1.0 / D, row_mu2[:], ALU.mult,
                    ALU.subtract), waits=[(pe, mm2), (sc, a2)], inc=ve)
                a4 = E("scalar", lambda sce: sce.activation(
                    row_mu2[:], row_s2[:], AF.Ln, bias=epsB[:], scale=1.0),
                    waits=[(ve, a3)], inc=sc)
                a5 = E("scalar", lambda sce: sce.activation(
                    row_rstd[:], row_mu2[:], AF.Exp, bias=0.0, scale=-0.5),
                    waits=[(sc, a4)], inc=sc)
                t_psum_free = (sc, a1)
                t_psq_free = (ve, a3)
                c3 = E("tensor", lambda t: t.matmul(
                    p_rs[:], cs["c_1x128f"][:], row_mu[:], start=True, stop=True),
                    waits=[(sc, a5), (ve, t_prs_free), (pe, t_rdm_cons)], inc=pe)
                c4 = E("tensor", lambda t: t.matmul(
                    p_big[:, 0:QB], cs["c_1x128f"][:], row_rstd[:],
                    start=True, stop=True),
                    waits=[(sc, a5), (ve, t_snap), (ve, t_c4war(0))] if False else
                          [(sc, a5), (ve, t_snap)], inc=pe)
                t_rows_free = (pe, c4)
                t_p4 = [0] * NCH
                for c in range(NCH):
                    t_p4[c] = E("vector", lambda v, c=c: v.tensor_tensor(
                        t_blk[:, c, :], r_blk[:, c, :], p_rs[:], ALU.subtract),
                        waits=[(ve, t_r[c]), (pe, max(c3, mm2))], inc=ve)
                t_prs_free = t_p4[NCH - 1]
                t_p5 = [0] * NCH
                for c in range(NCH):
                    t_p5[c] = E("vector", lambda v, c=c: v.tensor_tensor(
                        r_blk[:, c, :], t_blk[:, c, :], p_big[:, 0:QB], ALU.mult),
                        waits=[(ve, t_p4[c]), (pe, c4), (dgs, t_st)], inc=ve)
                t_snap = t_p5[NCH - 1]
                t_v_cons_pe = (pe, mm1)
                t_v_cons_sc = (sc, tsq)
                t_st = E("gpsimd", lambda g, tb=tb, s=s: g.dma_start(
                    out=xo[s][:, :, tb * QB:(tb + 1) * QB].rearrange("c p t -> p c t"),
                    in_=r_blk[:]), waits=[(ve, t_p5[NCH - 1])], inc=dgs, inc_by=16)
        pr.barrier_all(allsems)


def t_pb_cons_v(x):
    return x


def t_den_cons_s(x):
    return x


def t_rblk_war(x):
    return x


def t_c4war(x):
    return x


def prep_c(inputs, n_cores=8, xin_override=None):
    Wq = np.asarray(inputs["L1_attn_Wq"])
    Wk = np.asarray(inputs["L1_attn_Wk"])
    Wv = np.asarray(inputs["L1_attn_Wv"])
    Wo = np.asarray(inputs["L1_attn_Wo"])
    bo = np.asarray(inputs["L1_attn_bo"])
    shared = dict(
        wq=np.ascontiguousarray(Wq.reshape(NCH, 128, D)),
        wk=np.ascontiguousarray(Wk.reshape(NCH, 128, D)),
        wv=np.ascontiguousarray(Wv.reshape(NCH, 128, D)),
        wo=np.ascontiguousarray(Wo.reshape(NCH, 128, D)),
        bo=np.ascontiguousarray(bo.reshape(NCH, 128).T),
        **consts_c())
    in_maps = []
    for core in range(n_cores):
        m = dict(shared)
        m["xi"] = np.ascontiguousarray(xin_override[core])
        in_maps.append(m)
    return in_maps


IH = 2816
NNC = IH // 128

CONST_SHAPES_D = {
    "c_128x32": ([128, 32], BF16), "c_1x128f": ([1, 128], F32),
}


def consts_d():
    return {"c_128x32": np.ones((128, 32), dtype=BF),
            "c_1x128f": np.ones((1, 128), dtype=np.float32)}


def emit_d(nc, pr, sems, ctx, dram):
    T, NQB, NT = (globals()["T"], globals()["NQB"], globals()["NT"])
    xi = dram["xcd"]
    wg, wu, wd = dram["d_wg"], dram["d_wu"], dram["d_wd"]
    cin = {k: dram[k] for k in CONST_SHAPES_D}
    gT, uT, dT = dram["d_gT"], dram["d_uT"], dram["d_dT"]
    xout = dram["xout"]
    NBW = [512] * 5 + [256]
    NBO = [0, 512, 1024, 1536, 2048, 2560]
    if True:
        sb = lambda name, shape, dt=BF16: ctx.enter_context(nc.sbuf_tensor("d_" + name, shape, dt))
        ps = lambda name, shape, dt=F32: ctx.enter_context(nc.psum_tensor("d_" + name, shape, dt))
        ctx0 = ExitStack()
        sb0 = lambda name, shape, dt=BF16: ctx0.enter_context(nc.sbuf_tensor("d_" + name, shape, dt))
        qsets = []
        for sid in range(3):
            qsets.append({
                "f1": sb("q%df1" % sid, [128, D], F32),
                "f2": sb("q%df2" % sid, [128, D], F32),
                "f3": sb("q%df3" % sid, [128, D], F32),
                "scl": sb("q%dscl" % sid, [128, D // 16], F32),
                "rscl": sb("q%drscl" % sid, [128, D // 16], F32),
                "rln": sb("q%drln" % sid, [128, D // 16], F32),
                "sgb": sb("q%dsgb" % sid, [128, D])})
        qi = sb("qi", [128, D], mybir.dt.int32)
        qb1 = sb("qb1", [128, D])
        qb2 = sb("qb2", [128, D])
        m4c = sb("m4c", [128, 1], F32)
        m6c = sb("m6c", [128, 1], F32)
        p2c = sb("p2c", [128, 1], F32)
        p4c = sb("p4c", [128, 1], F32)
        phc = sb("phc", [128, 1], F32)
        cs = {k: sb(k + "_s", sh, dt) for k, (sh, dt) in CONST_SHAPES_D.items()}
        epsB = sb("epsB", [1, 1], F32)
        wraw = sb0("wraw", [128, 2, D], F32)
        wqb = sb0("wqb", [128, 2, D])
        wqT = sb0("wqT", [128, 2, D])

        p_g = ps("p_g", [128, QB])
        p_u = ps("p_u", [128, QB])
        p_d = [ps("p_d0", [128, QB]), ps("p_d1", [128, QB])]
        p_sum = ps("p_sum", [32, QB])
        p_sq = ps("p_sq", [32, QB])
        p_r1 = ps("p_r1", [128, QB])
        p_r2 = ps("p_r2", [128, QB])

        pe, sc, ve, dgp, dgs = (sems[k] for k in ("pe", "sc", "ve", "dgp", "dgs"))
        gs = sems["gs"]
        dgw = [sems["dgw0"], sems["dgw1"]]
        dgx = [sems["dgx0"], sems["dgx1"]]
        dst_ = [sems["ds0"], sems["ds1"]]
        dgss = [sems["dgs"], sems["dgs1"]]
        allsems = list(sems.values())
        E = pr.emit
        _dma_last = {}

        def dch(sem):
            t = _dma_last.get(id(sem))
            return [(sem, t)] if t else []

        def dset(sem, tok):
            _dma_last[id(sem)] = tok
            return tok

        for dst, src_ in [(cs[k], cin[k]) for k in CONST_SHAPES_D]:
            E("gpsimd", lambda g, d=dst, s_=src_: g.dma_start(out=d[:], in_=s_),
              inc=dgp, inc_by=16)
        LDGP = dgp.n
        E("vector", lambda v: v.memset(epsB[:], 1e-5), inc=ve)
        E("vector", lambda v: v.memset(m4c[:], -4.0), inc=ve)
        E("vector", lambda v: v.memset(m6c[:], -6.0), inc=ve)
        E("vector", lambda v: v.memset(p2c[:], 2.0), inc=ve)
        E("vector", lambda v: v.memset(p4c[:], 4.0), inc=ve)
        t_cst = (ve, E("vector", lambda v: v.memset(phc[:], 0.5 - 2.0 ** -16),
                       inc=ve))

        qshared = {"qi": (sc, 0), "qb": (gs, 0)}
        for S in qsets:
            S.update({"f1_free": (gs, 0), "f2_free": (ve, 0), "f3_free": (ve, 0),
                      "scl_free": (gs, 0), "rscl_free": (gs, 0),
                      "rln_free": (sc, 0), "sgb_free": (gs, 0)})
        qp = {"p1": None, "p2": None, "n": 0}

        def _q_fha(inv):
            S = inv["S"]; W = inv["W"]; NB16 = W // 16
            f2, f3 = S["f2"], S["f3"]
            t0 = E("scalar", lambda c, src=inv["src"], f2=f2, W=W: c.activation(
                f2[:, 0:W], src, AF.Abs), waits=inv["wsrc"] + [S["f2_free"]], inc=sc)
            sgn = E("scalar", lambda c, src=inv["src"], sg_=S["sgb"], W=W:
                    c.activation(sg_[:, 0:W], src, AF.Sign),
                    waits=[S["sgb_free"]], inc=sc)
            cur_off, cur_n = 0, W
            tprev = (sc, t0)
            lev = 0
            while cur_n > NB16:
                nxt = cur_n // 2
                src_lo = (f2 if lev == 0 else f3)
                so = 0 if lev == 0 else cur_off
                do = 0 if lev % 2 == 0 else W // 2
                w = [tprev] + ([S["f3_free"]] if lev == 0 else [])
                tt = E("vector", lambda v, so=so, do=do, nxt=nxt, src_lo=src_lo,
                       f3=f3: v.tensor_tensor(
                    f3[:, do:do + nxt],
                    src_lo[:, so:so + 2 * nxt].rearrange(
                        "p (n two) -> p n two", two=2)[:, :, 0],
                    src_lo[:, so:so + 2 * nxt].rearrange(
                        "p (n two) -> p n two", two=2)[:, :, 1], ALU.max),
                    waits=w, inc=ve)
                tprev = (ve, tt)
                cur_off, cur_n = do, nxt
                lev += 1
            tsc = E("vector", lambda v, cur_off=cur_off, NB16=NB16, f3=f3,
                    sl_=S["scl"]: v.tensor_scalar(
                sl_[:, 0:NB16], f3[:, cur_off:cur_off + NB16],
                1.0 / 6.0, 1e-12, ALU.mult, ALU.max),
                waits=[tprev, S["scl_free"]], inc=ve)
            lnr = E("scalar", lambda c, NB16=NB16, sl_=S["scl"], rl_=S["rln"]:
                    c.activation(rl_[:, 0:NB16], sl_[:, 0:NB16], AF.Ln),
                    waits=[(ve, tsc), S["rln_free"]], inc=sc)
            exr = E("scalar", lambda c, NB16=NB16, rl_=S["rln"], rs_=S["rscl"]:
                    c.activation(rs_[:, 0:NB16], rl_[:, 0:NB16], AF.Exp,
                                 bias=0.0, scale=-1.0),
                    waits=[(sc, lnr), S["rscl_free"]], inc=sc)
            xnv = E("gpsimd", lambda g, W=W, NB16=NB16, f2=f2, f1=S["f1"],
                    rs_=S["rscl"]: g.tensor_tensor(
                f1[:, 0:W].rearrange("p (n b) -> p n b", b=16),
                f2[:, 0:W].rearrange("p (n b) -> p n b", b=16),
                rs_[:, 0:NB16].broadcast_to([128, NB16, 16]), ALU.mult),
                waits=[(sc, exr), S["f1_free"]], inc=gs)
            S["rscl_free"] = (gs, xnv)
            S["rln_free"] = (sc, exr)
            inv["tsc"], inv["xnv"] = tsc, xnv
            inv["src_done"] = (sc, sgn)

        def _q_fhb(inv):
            S, W = inv["S"], inv["W"]
            f1, f2, f3 = S["f1"], S["f2"], S["f3"]
            m1 = E("scalar", lambda c, W=W, f1=f1, f3=f3: c.activation(
                f3[:, 0:W], f1[:, 0:W], AF.Relu, bias=p2c[:], scale=1.0),
                waits=[(gs, inv["xnv"]), (ve, inv["tsc"]), t_cst], inc=sc)
            u1 = E("vector", lambda v, W=W, f1=f1, f2=f2, f3=f3:
                   v.scalar_tensor_tensor(
                f2[:, 0:W], f1[:, 0:W], 2.0, f3[:, 0:W], ALU.mult, ALU.min),
                waits=[(sc, m1)], inc=ve)
            inv["u1"] = u1

        def _q_sh1(inv):
            S, W = inv["S"], inv["W"]
            f1, f2, f3 = S["f1"], S["f2"], S["f3"]
            m2 = E("vector", lambda v, W=W, f1=f1, f3=f3: v.tensor_scalar(
                f3[:, 0:W], f1[:, 0:W], 0.5, 4.0, ALU.mult, ALU.add),
                waits=[(ve, inv["u1"])], inc=ve)
            u2 = E("vector", lambda v, W=W, f1=f1, f2=f2, f3=f3: v.tensor_tensor(
                f1[:, 0:W], f2[:, 0:W], f3[:, 0:W], ALU.min),
                waits=[(ve, m2)], inc=ve)
            tadd = E("vector", lambda v, W=W, f1=f1, f2=f2: v.tensor_scalar(
                f2[:, 0:W], f1[:, 0:W], 1.0, 0.5 - 2.0 ** -16, ALU.mult, ALU.add),
                waits=[(ve, u2)], inc=ve)
            ic = E("scalar", lambda c, W=W, f2=f2: c.activation(
                qi[:, 0:W], f2[:, 0:W], AF.Copy),
                waits=[(ve, tadd), qshared["qi"]], inc=sc)
            fi = E("scalar", lambda c, W=W, f3=f3: c.activation(
                f3[:, 0:W], qi[:, 0:W], AF.Copy), waits=[(sc, ic)], inc=sc)
            qshared["qi"] = (sc, fi)
            gt = E("vector", lambda v, W=W, f1=f1, f2=f2, f3=f3: v.tensor_tensor(
                f1[:, 0:W], f3[:, 0:W], f2[:, 0:W], ALU.is_gt),
                waits=[(sc, fi)], inc=ve)
            isub = E("vector", lambda v, W=W, f1=f1, f3=f3: v.tensor_tensor(
                f3[:, 0:W], f3[:, 0:W], f1[:, 0:W], ALU.subtract),
                waits=[(ve, gt)], inc=ve)
            inv["isub"] = isub

        def _q_sh2(inv):
            S, W = inv["S"], inv["W"]
            NB16 = W // 16
            f1, f2, f3 = S["f1"], S["f2"], S["f3"]
            r1 = E("scalar", lambda c, W=W, f1=f1, f3=f3: c.activation(
                f1[:, 0:W], f3[:, 0:W], AF.Relu, bias=m4c[:], scale=1.0),
                waits=[(ve, inv["isub"]), t_cst], inc=sc)
            r2 = E("scalar", lambda c, W=W, f2=f2, f3=f3: c.activation(
                f2[:, 0:W], f3[:, 0:W], AF.Relu, bias=m6c[:], scale=1.0),
                waits=[(sc, r1)], inc=sc)
            s1 = E("vector", lambda v, W=W, f1=f1, f2=f2: v.scalar_tensor_tensor(
                qb1[:, 0:W], f1[:, 0:W], 0.5, f2[:, 0:W], ALU.mult, ALU.add),
                waits=[(sc, r2), qshared["qb"]], inc=ve)
            s2 = E("vector", lambda v, W=W, f3=f3: v.scalar_tensor_tensor(
                qb2[:, 0:W], f3[:, 0:W], 0.5, qb1[:, 0:W], ALU.mult, ALU.add),
                waits=[(ve, s1)], inc=ve)
            t2 = E("gpsimd", lambda g, W=W, f1=f1, sg_=S["sgb"]: g.tensor_tensor(
                f1[:, 0:W], qb2[:, 0:W], sg_[:, 0:W], ALU.mult),
                waits=[(ve, s2)], inc=gs)
            qshared["qb"] = (gs, t2)
            qv = E("gpsimd", lambda g, W=W, NB16=NB16, dst=inv["dst"], f1=f1,
                   sl_=S["scl"]: g.tensor_tensor(
                dst.rearrange("p (n b) -> p n b", b=16),
                f1[:, 0:W].rearrange("p (n b) -> p n b", b=16),
                sl_[:, 0:NB16].broadcast_to([128, NB16, 16]), ALU.mult),
                waits=[(gs, t2)] + inv["wdst_fn"](), inc=gs)
            S["f1_free"] = (gs, qv)
            S["f2_free"] = (ve, s1)
            S["f3_free"] = (ve, s2)
            S["scl_free"] = (gs, qv)
            S["sgb_free"] = (gs, t2)
            inv["done"] = (gs, qv)

        def q_feed(src_ap, dst_ap, F, wsrc, wdst_fn, meta=None):
            inv = {"S": qsets[qp["n"] % 3], "src": src_ap, "dst": dst_ap, "W": F,
                   "wsrc": list(wsrc), "wdst_fn": wdst_fn, "meta": meta}
            qp["n"] += 1
            done = qp["p2"]
            if done is not None:
                _q_sh2(done)
            _q_fha(inv)
            if qp["p1"] is not None:
                _q_sh1(qp["p1"])
            _q_fhb(inv)
            qp["p2"] = qp["p1"]
            qp["p1"] = inv
            return done

        def q_flush(cb):
            if qp["p1"] is not None:
                _q_sh1(qp["p1"])
            if qp["p2"] is not None:
                _q_sh2(qp["p2"])
                cb(qp["p2"])
            if qp["p1"] is not None:
                _q_sh2(qp["p1"])
                cb(qp["p1"])
            qp["p1"] = qp["p2"] = None

        t_qbf_cons = [0, 0]
        t_trd_cons = [0, 0]
        winv = [None, None]
        chunks = []
        for ci in range(NNC):
            chunks.append(("g", wg[ci], gT, ci, D, 0))
        for ci in range(NNC):
            chunks.append(("u", wu[ci], uT, ci, D, 0))
        for ci in range(NCH):
            for seg in range(3):
                W = 1024 if seg < 2 else 768
                chunks.append(("d", wd[ci], dT, ci, W, seg * 1024))

        def emit_wtr(pi):
            kind, dstT, ci, W, soff, slw = pi["meta"]
            NKC = W // 128
            tt = dset(dst_[slw], E("sync", lambda sy, slw=slw, W=W, NKC=NKC:
                                    sy.dma_start(
                out=wqT[:, slw, 0:W].rearrange("p (c f) -> p c f", c=NKC),
                in_=wqb[:, slw, 0:W].rearrange("p (c f) -> p c f", c=NKC),
                transpose=True),
                waits=[pi["done"], (dgss[slw], t_trd_cons[slw])] + dch(dst_[slw]),
                inc=dst_[slw], inc_by=16))
            t_qbf_cons[slw] = tt
            if kind in ("g", "u"):
                st = dset(dgss[slw], E("gpsimd", lambda g, dstT=dstT, ci=ci,
                                       slw=slw: g.dma_start(
                    out=dstT[:, :, ci * 128:(ci + 1) * 128].rearrange(
                        "c k n -> k c n"),
                    in_=wqT[:, slw, :].rearrange("p (c f) -> p c f", c=NCH)),
                    waits=[(dst_[slw], tt)] + dch(dgss[slw]),
                    inc=dgss[slw], inc_by=16))
            else:
                st = dset(dgss[slw], E("gpsimd", lambda g, ci=ci, slw=slw,
                                       soff=soff, NKC=NKC: g.dma_start(
                    out=dT[soff // 128:soff // 128 + NKC, :,
                           ci * 128:(ci + 1) * 128].rearrange("c k n -> k c n"),
                    in_=wqT[:, slw, 0:NKC * 128].rearrange(
                        "p (c f) -> p c f", c=NKC)),
                    waits=[(dst_[slw], tt)] + dch(dgss[slw]),
                    inc=dgss[slw], inc_by=16))
            t_trd_cons[slw] = st

        for n, (kind, src_ap, dstT, ci, W, soff) in enumerate(chunks):
            slw = n % 2
            wld_waits = [winv[slw]["src_done"]] if winv[slw] else []
            tl = dset(dgw[slw], E("gpsimd", lambda g, src_ap=src_ap, slw=slw,
                                  W=W, soff=soff:
                   g.dma_start(out=wraw[:, slw, 0:W], in_=src_ap[:, soff:soff + W]),
                   waits=wld_waits + dch(dgw[slw]), inc=dgw[slw], inc_by=16))
            prev = q_feed(wraw[:, slw, 0:W], wqb[:, slw, 0:W], W,
                          [(dgw[slw], tl)],
                          (lambda slw=slw: [(dst_[slw], t_qbf_cons[slw])]),
                          meta=(kind, dstT, ci, W, soff, slw))
            winv[slw] = qp["p1"]
            if prev is not None:
                emit_wtr(prev)
        q_flush(emit_wtr)
        WSD0, WSD1 = dgss[0].n, dgss[1].n
        ctx0.close()
        x3q = sb("x3q", [128, NCH, T])
        xtm = sb("xtm", [128, 2, D])
        xqtm = sb("xqtm", [128, 1, D])
        wg_nb = sb("wg_nb", [128, 2, NCH, QB])
        wu_nb = sb("wu_nb", [128, 2, NCH, QB])
        wd_nb = sb("wd_nb", [128, 1, 4, D])
        h_tm = sb("h_tm", [128, 4 * QB])
        hq_tm = sb("hq_tm", [128, 4 * QB])
        hqf = sb("hq_fm", [128, max(4 * T, 2 * NCH * QB)])
        hq_fm = hqf[:, 0:4 * T].rearrange("p (c t) -> p c t", c=4)
        v_blk = hqf[:, 0:NCH * QB].rearrange("p (c t) -> p c t", c=NCH)
        t_blk = hqf[:, NCH * QB:2 * NCH * QB].rearrange("p (c t) -> p c t", c=NCH)
        sg_sb = sb("sg_sb", [128, 2, QB])
        facc = sb("facc", [128, NCH, T])
        xfmt = sb("xfmt", [128, 1, D])
        r2f = h_tm[:, 0:QB]
        row_mu = sb("row_mu", [1, QB], F32)
        row_mu2 = sb("row_mu2", [1, QB], F32)
        row_s2 = sb("row_s2", [1, QB], F32)
        row_rstd = sb("row_rstd", [1, QB], F32)
        x3b = wg_nb[:, 0]

        xinv = [None, None]
        hinv = [None, None]
        t_hqc = [[], []]
        t_trx = [0, 0]
        for s in range(NSEQ):
            pr.barrier_all(allsems)
            t_xq_cons = [0, 0]
            t_xfmt_last = (dst_[0], 0)

            def emit_xtr(pi):
                i = pi["meta"]
                sl = i % 2
                tt = dset(dst_[sl], E("sync", lambda sy, i=i: sy.dma_start(
                    out=x3q[:, :, i * 128:(i + 1) * 128],
                    in_=xqtm[:, 0, :].rearrange("p (c f) -> p c f", c=NCH),
                    transpose=True),
                    waits=[pi["done"]] + dch(dst_[sl]),
                    inc=dst_[sl], inc_by=16))
                t_xq_cons[sl] = tt
                t_trx[sl] = tt

            for i in range(NT):
                sl = i % 2
                tld = dset(dgx[sl], E("gpsimd", lambda g, i=i, s=s: g.dma_start(
                    out=xfmt[:, 0, :].rearrange("p (c f) -> p c f", c=NCH),
                    in_=xi[s][:, :, i * 128:(i + 1) * 128].rearrange(
                        "c p f -> p c f")),
                    waits=[t_xfmt_last] + dch(dgx[sl]), inc=dgx[sl], inc_by=16))
                xw = [xinv[sl]["src_done"]] if xinv[sl] else []
                tl = dset(dst_[sl], E("sync", lambda sy, i=i, sl=sl: sy.dma_start(
                    out=xtm[:, sl, :].rearrange("p (c f) -> p c f", c=NCH),
                    in_=xfmt[:, 0, :].rearrange("p (c f) -> p c f", c=NCH),
                    transpose=True),
                    waits=xw + [(dgx[sl], tld)] + dch(dst_[sl]),
                    inc=dst_[sl], inc_by=16))
                t_xfmt_last = (dst_[sl], tl)
                prev = q_feed(xtm[:, sl, :], xqtm[:, 0, :], D, [(dst_[sl], tl)],
                              (lambda: [(dst_[0], t_xq_cons[0]),
                                        (dst_[1], t_xq_cons[1])]), meta=i)
                xinv[sl] = qp["p1"]
                if prev is not None:
                    emit_xtr(prev)
            q_flush(emit_xtr)
            XQRDY = [(dst_[0], t_trx[0]), (dst_[1], t_trx[1])]

            t_wnb = [0, 0]
            t_wnb_cons = [0, 0]
            t_facc_cons = (ve, 0)
            t_hfm_cons = 0
            t_wd_cons = 0
            for nb in range(6):
                W = NBW[nb]
                NKC = W // 128
                slw = nb % 2
                ld1 = dset(dgw[slw], E("gpsimd", lambda g, nb=nb, slw=slw, W=W:
                                       g.dma_start(
                    out=wg_nb[:, slw, :, 0:W], in_=gT[:, :, NBO[nb]:NBO[nb] + W]
                    .rearrange("c k n -> k c n")),
                    waits=[(pe, t_wnb_cons[slw]), (dgss[0], WSD0), (dgss[1], WSD1)]
                    + dch(dgw[slw]), inc=dgw[slw], inc_by=16))
                ld2 = dset(dgw[slw], E("gpsimd", lambda g, nb=nb, slw=slw, W=W:
                                       g.dma_start(
                    out=wu_nb[:, slw, :, 0:W], in_=uT[:, :, NBO[nb]:NBO[nb] + W]
                    .rearrange("c k n -> k c n")), waits=dch(dgw[slw]),
                    inc=dgw[slw], inc_by=16))
                ld3 = dset(dgw[slw], E("gpsimd", lambda g, nb=nb, slw=slw, NKC=NKC:
                                       g.dma_start(
                    out=wd_nb[:, 0, 0:NKC, :],
                    in_=dT[NBO[nb] // 128:NBO[nb] // 128 + NKC].rearrange(
                        "c k n -> k c n")),
                    waits=[(pe, t_wd_cons)] + dch(dgw[slw]),
                    inc=dgw[slw], inc_by=16))
                t_wnb[slw] = ld3
                t_hq_war = (pe, t_hfm_cons)
                t_pg_free = (sc, 0)
                t_pu_free = (ve, 0)
                t_sg_free = [(ve, 0), (ve, 0)]

                def emit_htr(pi):
                    p_, W_, NKC_, hb_ = pi["meta"]
                    tts = []
                    for j in (0, 1):
                        i = 2 * p_ + j
                        tts.append(dset(dst_[j], E("sync", lambda sy, i=i, j=j,
                                        W_=W_, NKC_=NKC_, hb_=hb_: sy.dma_start(
                            out=hq_fm[:, 0:NKC_, i * 128:(i + 1) * 128],
                            in_=hq_tm[:, hb_ + j * W_:hb_ + (j + 1) * W_]
                            .rearrange("p (c f) -> p c f", c=NKC_),
                            transpose=True),
                            waits=[pi["done"], pi["hq_war"]] + dch(dst_[j]),
                            inc=dst_[j], inc_by=16)))
                    t_hqc[p_ % 2] = [(dst_[0], tts[0]), (dst_[1], tts[1])]
                    t_trx[0], t_trx[1] = tts[0], tts[1]

                for p in range(NT // 2):
                    par = p % 2
                    hb = par * 2 * QB
                    ths = []
                    for j in (0, 1):
                        i = 2 * p + j
                        mmg = mmu = None
                        for kc in range(NCH):
                            mmg = E("tensor", lambda t, i=i, kc=kc, slw=slw, W=W:
                                    t.matmul(
                                p_g[:, 0:W], x3q[:, kc, i * 128:(i + 1) * 128],
                                wg_nb[:, slw, kc, 0:W], start=(kc == 0),
                                stop=(kc == NCH - 1)),
                                waits=XQRDY + [(dgw[slw], t_wnb[slw]), t_pg_free,
                                               t_pu_free], inc=pe)
                            mmu = E("tensor", lambda t, i=i, kc=kc, slw=slw, W=W:
                                    t.matmul(
                                p_u[:, 0:W], x3q[:, kc, i * 128:(i + 1) * 128],
                                wu_nb[:, slw, kc, 0:W], start=(kc == 0),
                                stop=(kc == NCH - 1)), inc=pe)
                        tsg = E("scalar", lambda c, j=j, W=W: c.activation(
                            sg_sb[:, j, 0:W], p_g[:, 0:W], AF.Silu),
                            waits=[(pe, mmu), t_sg_free[j]], inc=sc)
                        t_pg_free = (sc, tsg)
                        hw = [hinv[par]["src_done"]] if hinv[par] else []
                        th = E("vector", lambda v, j=j, W=W, hb=hb: v.tensor_tensor(
                            h_tm[:, hb + j * W:hb + (j + 1) * W], sg_sb[:, j, 0:W],
                            p_u[:, 0:W], ALU.mult),
                            waits=[(sc, tsg)] + hw, inc=ve)
                        t_pu_free = (ve, th)
                        t_sg_free[j] = (ve, th)
                        ths.append(th)
                    prev = q_feed(h_tm[:, hb:hb + 2 * W], hq_tm[:, hb:hb + 2 * W],
                                  2 * W, [(ve, ths[0]), (ve, ths[1])],
                                  (lambda par=par: list(t_hqc[par])),
                                  meta=(p, W, NKC, hb))
                    qp["p1"]["hq_war"] = t_hq_war
                    hinv[par] = qp["p1"]
                    if prev is not None:
                        emit_htr(prev)
                q_flush(emit_htr)
                HQRDY = [(dst_[0], t_trx[0]), (dst_[1], t_trx[1])]
                first_nb = (nb == 0)
                for oc in range(NCH):
                    bi = oc % 2
                    for tb in range(NQB):
                        mm = None
                        for kk in range(NKC):
                            mm = E("tensor", lambda t, oc=oc, tb=tb, kk=kk, slw=slw,
                                   bi=bi, NKC=NKC: t.matmul(
                                p_d[bi][:], wd_nb[:, 0, kk, oc * 128:(oc + 1) * 128],
                                hq_fm[:, kk, tb * QB:(tb + 1) * QB],
                                start=(kk == 0), stop=(kk == NKC - 1)),
                                waits=HQRDY + [(dgw[slw], t_wnb[slw]),
                                               t_facc_cons], inc=pe)
                        if first_nb:
                            tf = E("vector", lambda v, oc=oc, tb=tb, bi=bi: v.tensor_scalar(
                                facc[:, oc, tb * QB:(tb + 1) * QB], p_d[bi][:],
                                1.0, None, ALU.mult),
                                waits=[(pe, mm)], inc=ve)
                        else:
                            tf = E("vector", lambda v, oc=oc, tb=tb, bi=bi:
                                   v.scalar_tensor_tensor(
                                facc[:, oc, tb * QB:(tb + 1) * QB], p_d[bi][:], 1.0,
                                facc[:, oc, tb * QB:(tb + 1) * QB], ALU.mult, ALU.add),
                                waits=[(pe, mm)], inc=ve)
                        t_facc_cons = (ve, tf)
                t_hfm_cons = mm
                t_wnb_cons[slw] = mm
                t_wd_cons = mm

            t_psum_free = (ve, 0)
            t_psq_free = (ve, 0)
            t_rows_free = (ve, 0)
            t_x3b_cons = (ve, 0)
            t_vcons_pe = (pe, 0)
            t_vcons_sc = (sc, 0)
            t_xtm2_cons = [0, 0]
            t_stf = 0
            for tb in range(NQB):
                tl = dset(dgx[tb % 2], E("gpsimd", lambda g, tb=tb, s=s:
                                         g.dma_start(
                    out=x3b[:], in_=xi[s][:, :, tb * QB:(tb + 1) * QB].rearrange(
                        "c p t -> p c t")),
                    waits=[t_x3b_cons] + dch(dgx[tb % 2]),
                    inc=dgx[tb % 2], inc_by=16))
                t_v = [0] * NCH
                for c in range(NCH):
                    t_v[c] = E("vector", lambda v, c=c, tb=tb: v.tensor_tensor(
                        v_blk[:, c, :], x3b[:, c, :],
                        facc[:, c, tb * QB:(tb + 1) * QB], ALU.add),
                        waits=[(dgx[tb % 2], tl), t_facc_cons, t_vcons_pe, t_vcons_sc,
                               (dgss[0], t_xtm2_cons[0]), (dgss[1], t_xtm2_cons[1])],
                        inc=ve)
                t_x3b_cons = (ve, t_v[NCH - 1])
                mm1 = None
                for c in range(NCH):
                    mm1 = E("tensor", lambda t, c=c: t.matmul(
                        p_sum[:], cs["c_128x32"][:], v_blk[:, c, :],
                        start=(c == 0), stop=(c == NCH - 1)),
                        waits=[(ve, t_v[c]), t_psum_free], inc=pe)
                mm2 = 0
                tsq = 0
                for c in range(NCH):
                    tsq = E("scalar", lambda sce, c=c: sce.activation(
                        r2f[:], v_blk[:, c, :], AF.Square),
                        waits=[(ve, t_v[c]), (pe, mm2),
                               hinv[0]["src_done"]], inc=sc)
                    mm2 = E("tensor", lambda t, c=c: t.matmul(
                        p_sq[:], cs["c_128x32"][:], r2f[:],
                        start=(c == 0), stop=(c == NCH - 1)),
                        waits=[(sc, tsq), t_psq_free], inc=pe)
                a1 = E("scalar", lambda sce: sce.mul(row_mu[:], p_sum[0:1, :], 1.0 / D),
                       waits=[(pe, mm1), t_rows_free], inc=sc)
                a2 = E("scalar", lambda sce: sce.activation(row_mu2[:], row_mu[:],
                                                            AF.Square),
                       waits=[(sc, a1)], inc=sc)
                a3 = E("vector", lambda v: v.scalar_tensor_tensor(
                    row_s2[:], p_sq[0:1, :], 1.0 / D, row_mu2[:], ALU.mult,
                    ALU.subtract), waits=[(pe, mm2), (sc, a2)], inc=ve)
                a4 = E("scalar", lambda sce: sce.activation(
                    row_mu2[:], row_s2[:], AF.Ln, bias=epsB[:], scale=1.0),
                    waits=[(ve, a3)], inc=sc)
                a5 = E("scalar", lambda sce: sce.activation(
                    row_rstd[:], row_mu2[:], AF.Exp, bias=0.0, scale=-0.5),
                    waits=[(sc, a4)], inc=sc)
                t_psum_free = (sc, a1)
                t_psq_free = (ve, a3)
                c3 = E("tensor", lambda t: t.matmul(
                    p_r1[:], cs["c_1x128f"][:], row_mu[:], start=True, stop=True),
                    waits=[(sc, a5), (ve, t_r1_cons(0))] if False else [(sc, a5)],
                    inc=pe)
                c4 = E("tensor", lambda t: t.matmul(
                    p_r2[:], cs["c_1x128f"][:], row_rstd[:], start=True, stop=True),
                    waits=[(sc, a5)], inc=pe)
                t_rows_free = (pe, c4)
                t_p4 = [0] * NCH
                for c in range(NCH):
                    t_p4[c] = E("vector", lambda v, c=c: v.tensor_tensor(
                        t_blk[:, c, :], v_blk[:, c, :], p_r1[:], ALU.subtract),
                        waits=[(ve, t_v[c]), (pe, max(c3, mm2))], inc=ve)
                t_p5 = [0] * NCH
                for c in range(NCH):
                    t_p5[c] = E("vector", lambda v, c=c: v.tensor_tensor(
                        v_blk[:, c, :], t_blk[:, c, :], p_r2[:], ALU.mult),
                        waits=[(ve, t_p4[c]), (pe, c4)], inc=ve)
                t_vcons_pe = (pe, mm1)
                t_vcons_sc = (sc, tsq)
                for j in range(4):
                    i = tb * 4 + j
                    sl = j % 2
                    ttr = None
                    for c in range(NCH):
                        ttr = dset(dst_[sl], E("sync", lambda sy, j=j, sl=sl,
                                               c=c: sy.dma_start(
                            out=xtm[:, sl, c * 128:(c + 1) * 128],
                            in_=v_blk[:, c, j * 128:(j + 1) * 128],
                            transpose=True),
                            waits=[(ve, t_p5[NCH - 1]), (dgss[sl], t_xtm2_cons[sl]),
                                   xinv[sl]["src_done"]] + dch(dst_[sl]),
                            inc=dst_[sl], inc_by=16))
                    t_trx[sl] = ttr
                    t_xtm2_cons[sl] = dset(dgss[sl], E("gpsimd", lambda g, i=i,
                                           s=s, sl=sl: g.dma_start(
                        out=xout[s, i * 128:(i + 1) * 128, :], in_=xtm[:, sl, :]),
                        waits=[(dst_[sl], ttr)] + dch(dgss[sl]),
                        inc=dgss[sl], inc_by=16))
        pr.barrier_all(allsems)
        E("gpsimd", None, waits=[(dgss[0], dgss[0].n), (dgss[1], dgss[1].n)])


def ds0_v(x, d):
    return x


def t_pg_cons(x):
    return x


def t_sg_cons(x):
    return x


def t_hqtr(a, b):
    return 0


def t_r1_cons(x):
    return x


def prep_d(inputs, n_cores=8, xin_override=None):
    w_gate = np.asarray(inputs["L1_ffn_w_gate"])
    w_up = np.asarray(inputs["L1_ffn_w_up"])
    w_down = np.asarray(inputs["L1_ffn_w_down"])
    shared = dict(
        wg=np.ascontiguousarray(w_gate.reshape(NNC, 128, D)),
        wu=np.ascontiguousarray(w_up.reshape(NNC, 128, D)),
        wd=np.ascontiguousarray(w_down.reshape(NCH, 128, IH)),
        **consts_d())
    in_maps = []
    for core in range(n_cores):
        m = dict(shared)
        m["xi"] = np.ascontiguousarray(xin_override[core])
        in_maps.append(m)
    return in_maps


ALL_SEMS = ("pe", "sc", "ve", "gs", "dgp", "dgw0", "dgw1", "dgx", "dgx0", "dgx1",
            "ds0", "ds1", "dgs", "dgs1")


def build_all():
    nc = bass.Bass("TRN2", target_bir_lowering=False, debug=False)
    T = globals()["T"]
    dram = {}

    def din(name, shape, dt):
        dram[name] = nc.dram_tensor(name, shape, dt, kind="ExternalInput").ap()

    def dint(name, shape, dt):
        dram[name] = nc.dram_tensor(name, shape, dt, kind="Internal").ap()

    din("a_xin", [NSEQ, T, D], F32)
    for n in ("a_wq", "a_wk", "a_wv"):
        din(n, [NCH, 128, D], F32)
    din("a_bq", [128, NCH], F32)
    din("a_bk", [128, NCH], F32)
    din("a_bv", [1, D], F32)
    din("a_gcol", [128, NCH], F32)
    din("a_gbrow", [2, D], F32)
    HS = 2048
    din("b_sw1", [NCH, 128, HS], F32)
    din("b_sw3", [NCH, 128, HS], F32)
    din("b_sw2", [HS // 128, 128, D], F32)
    din("b_w1", [8, NCH, 128, D], F32)
    din("b_w2", [8, NCH, 128, D], F32)
    din("b_sb1c", [128, HS // 128], F32)
    din("b_sb3c", [128, HS // 128], F32)
    din("b_sb2c", [128, NCH], F32)
    din("b_b1c", [8, 128, NCH], F32)
    din("b_b2r", [8, 1, D], F32)
    din("b_temb", [4, D], F32)
    for n in ("c_wq", "c_wk", "c_wv", "c_wo"):
        din(n, [NCH, 128, D], F32)
    din("c_bo", [128, NCH], F32)
    din("d_wg", [NNC, 128, D], F32)
    din("d_wu", [NNC, 128, D], F32)
    din("d_wd", [NCH, 128, IH], F32)
    seen = {}
    for shapes in (CONST_SHAPES_A, CONST_SHAPES_B, CONST_SHAPES_C, CONST_SHAPES_D):
        for k, (sh, dt) in shapes.items():
            if k not in seen:
                seen[k] = True
                din(k, sh, dt)
    dint("xab", [NSEQ, NCH, 128, T], BF16)
    dint("xbc", [NSEQ, NCH, 128, T], BF16)
    dint("xcd", [NSEQ, NCH, 128, T], BF16)
    dint("d_gT", [NCH, 128, IH], BF16)
    dint("d_uT", [NCH, 128, IH], BF16)
    dint("d_dT", [NNC, 128, D], BF16)
    dram["xout"] = nc.dram_tensor("xout", [NSEQ, T, D], F32,
                                  kind="ExternalOutput").ap()

    with ExitStack() as octx:
        sems = {n: Sem(octx.enter_context(nc.semaphore(n))) for n in ALL_SEMS}
        pr = Prog()
        with ExitStack() as c1:
            emit_a(nc, pr, sems, c1, dram)
        with ExitStack() as c2:
            emit_b(nc, pr, sems, c2, dram)
        with ExitStack() as c3:
            emit_c(nc, pr, sems, c3, dram)
        with ExitStack() as c4:
            emit_d(nc, pr, sems, c4, dram)
        with nc.Block() as block:
            pr.build(block)
    return nc


def const_union():
    c = {}
    c.update(consts_a())
    c.update(consts_b())
    c.update(consts_c())
    c.update(consts_d())
    return c


def prep_all(inputs, n_cores=8):
    x = np.asarray(inputs["L0_attn_x"])
    shared = {}
    shared["a_wq"] = np.ascontiguousarray(np.asarray(inputs["L0_attn_Wq"]).reshape(NCH, 128, D))
    shared["a_wk"] = np.ascontiguousarray(np.asarray(inputs["L0_attn_Wk"]).reshape(NCH, 128, D))
    shared["a_wv"] = np.ascontiguousarray(np.asarray(inputs["L0_attn_Wv"]).reshape(NCH, 128, D))
    shared["a_bq"] = np.ascontiguousarray(np.asarray(inputs["L0_attn_bq"]).reshape(NCH, 128).T)
    shared["a_bk"] = np.ascontiguousarray(np.asarray(inputs["L0_attn_bk"]).reshape(NCH, 128).T)
    shared["a_bv"] = np.asarray(inputs["L0_attn_bv"]).reshape(1, D)
    shared["a_gcol"] = np.ascontiguousarray(np.asarray(inputs["L0_attn_gamma"]).reshape(NCH, 128).T)
    shared["a_gbrow"] = np.stack([np.asarray(inputs["L0_attn_gamma"]),
                                  np.asarray(inputs["L0_attn_beta"])], axis=0)
    HS = 2048
    shared["b_sw1"] = np.ascontiguousarray(np.asarray(inputs["L0_ffn_sw1"]).reshape(NCH, 128, HS))
    shared["b_sw3"] = np.ascontiguousarray(np.asarray(inputs["L0_ffn_sw3"]).reshape(NCH, 128, HS))
    shared["b_sw2"] = np.ascontiguousarray(np.asarray(inputs["L0_ffn_sw2"]).reshape(HS // 128, 128, D))
    shared["b_w1"] = np.ascontiguousarray(np.asarray(inputs["L0_ffn_w1"]).reshape(8, NCH, 128, D))
    shared["b_w2"] = np.ascontiguousarray(np.asarray(inputs["L0_ffn_w2"]).reshape(8, NCH, 128, D))
    shared["b_sb1c"] = np.ascontiguousarray(np.asarray(inputs["L0_ffn_sb1"]).reshape(HS // 128, 128).T)
    shared["b_sb3c"] = np.ascontiguousarray(np.asarray(inputs["L0_ffn_sb3"]).reshape(HS // 128, 128).T)
    shared["b_sb2c"] = np.ascontiguousarray(np.asarray(inputs["L0_ffn_sb2"]).reshape(NCH, 128).T)
    shared["b_b1c"] = np.ascontiguousarray(np.asarray(inputs["L0_ffn_b1"]).reshape(8, NCH, 128).transpose(0, 2, 1))
    shared["b_b2r"] = np.ascontiguousarray(np.asarray(inputs["L0_ffn_b2"]).reshape(8, 1, D))
    shared["b_temb"] = np.ascontiguousarray(np.asarray(inputs["L0_ffn_t_emb"]))
    shared["c_wq"] = np.ascontiguousarray(np.asarray(inputs["L1_attn_Wq"]).reshape(NCH, 128, D))
    shared["c_wk"] = np.ascontiguousarray(np.asarray(inputs["L1_attn_Wk"]).reshape(NCH, 128, D))
    shared["c_wv"] = np.ascontiguousarray(np.asarray(inputs["L1_attn_Wv"]).reshape(NCH, 128, D))
    shared["c_wo"] = np.ascontiguousarray(np.asarray(inputs["L1_attn_Wo"]).reshape(NCH, 128, D))
    shared["c_bo"] = np.ascontiguousarray(np.asarray(inputs["L1_attn_bo"]).reshape(NCH, 128).T)
    shared["d_wg"] = np.ascontiguousarray(np.asarray(inputs["L1_ffn_w_gate"]).reshape(NNC, 128, D))
    shared["d_wu"] = np.ascontiguousarray(np.asarray(inputs["L1_ffn_w_up"]).reshape(NNC, 128, D))
    shared["d_wd"] = np.ascontiguousarray(np.asarray(inputs["L1_ffn_w_down"]).reshape(NCH, 128, IH))
    shared.update(const_union())
    in_maps = []
    for core in range(n_cores):
        m = dict(shared)
        m["a_xin"] = np.ascontiguousarray(x[core * NSEQ:(core + 1) * NSEQ])
        in_maps.append(m)
    return in_maps


def kernel(**inputs):
    n_cores = 8
    nc = build_all()
    in_maps = prep_all(inputs, n_cores=n_cores)
    res = run_bass_kernel_spmd(nc, in_maps, core_ids=list(range(n_cores)))
    out = np.empty((16, T, D), dtype=np.float32)
    for core in range(n_cores):
        xo = np.asarray(res.results[core]["xout"])
        out[core * 2] = xo[0]
        out[core * 2 + 1] = xo[1]
    return out


# version 14
# speedup vs baseline: 1.4587x; 1.0599x over previous
import numpy as np
import ml_dtypes
from contextlib import ExitStack
import concourse.bass as bass
import concourse.mybir as mybir
from concourse.bass_utils import run_bass_kernel_spmd

F32 = mybir.dt.float32
BF16 = mybir.dt.bfloat16
AF = mybir.ActivationFunctionType
ALU = mybir.AluOpType
AX = mybir.AxisListType
BF = ml_dtypes.bfloat16

ENGINES = ("tensor", "scalar", "vector", "gpsimd", "sync")

T, D, NCH, H, DH = 2048, 1024, 8, 16, 64
QB, NQB, NT, NSEQ = 512, 4, 16, 2
VW = H * 65


class Sem:
    def __init__(self, handle):
        self.h = handle
        self.n = 0


class Prog:
    def __init__(self):
        self.q = {e: [] for e in ENGINES}
        self._lastwait = {e: {} for e in ENGINES}

    def emit(self, eng, fn, waits=(), inc=None, inc_by=1):
        lw = self._lastwait[eng]
        wl = []
        for s, v in waits:
            if v is None or v <= 0:
                continue
            if lw.get(id(s), -1) >= v:
                continue
            wl.append((s, v))
            lw[id(s)] = v
        tok = None
        if inc is not None:
            assert fn is not None, "cannot inc a semaphore without an instruction"
            inc.n += inc_by
            tok = inc.n
        self.q[eng].append((fn, wl, inc, inc_by))
        return tok

    def barrier_all(self, sems):
        snap = [(s, s.n) for s in sems if s.n > 0]
        for e in ENGINES:
            for s, v in snap:
                self.emit(e, None, waits=[(s, v)])

    def build(self, block):
        def runner(engq):
            def _(eng):
                for fn, wl, inc, inc_by in engq:
                    for s, v in wl:
                        eng.wait_ge(s.h, v)
                    if fn is None:
                        continue
                    instr = fn(eng)
                    if inc is not None:
                        instr.then_inc(inc.h, inc_by)
            return _
        block.tensor(runner(self.q["tensor"]))
        block.scalar(runner(self.q["scalar"]))
        block.vector(runner(self.q["vector"]))
        block.gpsimd(runner(self.q["gpsimd"]))
        block.sync(runner(self.q["sync"]))


def consts_a():
    c = {}
    c["c_128x32"] = np.ones((128, 32), dtype=BF)
    c["c_128x32f"] = np.ones((128, 32), dtype=np.float32)
    c["c_1x32"] = np.ones((1, 32), dtype=BF)
    c["c_32x128"] = np.ones((32, 128), dtype=BF)
    c["c_1x64"] = np.ones((1, 64), dtype=BF)
    c["c_1x128"] = np.ones((1, 128), dtype=BF)
    k = np.arange(128)[:, None]
    u = np.arange(QB + 384)[None, :]
    c["tri01"] = (u >= k + 384).astype(BF)
    c["c_32x128d"] = (np.ones((32, 128)) / 32.0).astype(BF)
    sel = np.zeros((16, NCH * 128), dtype=BF)
    for hc in range(NCH):
        sel[2 * hc, hc * 128:hc * 128 + 64] = 1
        sel[2 * hc + 1, hc * 128 + 64:(hc + 1) * 128] = 1
    c["sel16"] = sel
    return c


CONST_SHAPES_A = {
    "c_128x32": ([128, 32], BF16), "c_128x32f": ([128, 32], F32),
    "c_1x32": ([1, 32], BF16), "c_32x128": ([32, 128], BF16),
    "c_1x64": ([1, 64], BF16), "c_1x128": ([1, 128], BF16),
    "tri01": ([128, QB + 384], BF16),
    "c_32x128d": ([32, 128], BF16), "sel16": ([16, NCH * 128], BF16),
}


def emit_a(nc, pr, sems, ctx, dram):
    T, NQB, NT = (globals()["T"], globals()["NQB"], globals()["NT"])
    xin = dram["a_xin"]
    wq, wk, wv = dram["a_wq"], dram["a_wk"], dram["a_wv"]
    bq, bk, bv = dram["a_bq"], dram["a_bk"], dram["a_bv"]
    gcol, gbrow = dram["a_gcol"], dram["a_gbrow"]
    cin = {k: dram[k] for k in CONST_SHAPES_A}
    xa = dram["xab"]
    if True:
        sb = lambda name, shape, dt=BF16: ctx.enter_context(nc.sbuf_tensor("a_" + name, shape, dt))
        ps = lambda name, shape, dt=F32: ctx.enter_context(nc.psum_tensor("a_" + name, shape, dt))
        wq_sb = sb("wq_sb", [128, NCH, D])
        wk_sb = sb("wk_sb", [128, NCH, D])
        wv_sb = sb("wv_sb", [128, NCH, D])
        bq_sb = sb("bq_sb", [128, NCH], F32)
        bk_sb = sb("bk_sb", [128, NCH], F32)
        bv_row = sb("bv_row", [1, D])
        bvrep = sb("bvrep", [32, D])
        gcol_sb = sb("gcol_sb", [128, NCH], F32)
        gb_sb = sb("gb_sb", [2, D])
        cs = {k: sb(k + "_s", sh, dt) for k, (sh, dt) in CONST_SHAPES_A.items()}
        neg50 = sb("neg50", [128, 1], F32)
        epsA = sb("epsA", [32, 1], F32)
        epsB = sb("epsB", [32, 1], F32)

        x_tm = sb("x_tm", [128, 2, D])
        x_blk = sb("x_blk", [128, 2, NCH, QB])
        q_blk = sb("q_blk", [128, NCH, QB])
        k_fm = sb("k_fm", [128, NCH, T])
        v_tm = sb("v_tm", [128, NT * VW])
        attn_blk = sb("attn_blk", [128, NCH, QB])
        praw = sb("praw", [128, 2, QB])
        pbuf = sb("pbuf", [128, 2, QB])
        r_blk = sb("r_blk", [128, NCH, QB])
        t1_blk = attn_blk
        x1_blk = sb("x1_blk", [128, NCH, QB])
        r2f = sb("r2f", [128, QB])
        msum_col = sb("msum_col", [128, NT], F32)
        m_colv = sb("m_colv", [128, NT], F32)
        m_colv2 = sb("m_colv2", [128, NT], F32)
        m_raw = sb("m_raw", [1, T])
        m_row = m_raw

        rdm_row = sb("rdm_row", [1, QB])
        den_tmp = sb("den_tmp", [1, 2, QB], F32)
        den16 = sb("den16", [16, QB], F32)
        rdm16b = sb("rdm16b", [16, QB])
        mrow16 = sb("mrow16", [16, QB])
        row_mu = sb("row_mu", [32, QB], F32)
        row_mu2 = sb("row_mu2", [32, QB], F32)
        row_s2 = sb("row_s2", [32, QB], F32)
        row_rstd = sb("row_rstd", [32, QB], F32)
        row_murstd = sb("row_murstd", [32, QB], F32)
        nurows = sb("nurows", [2, QB])
        mu_bf = sb("mu_bf", [32, QB])
        rstd_bf = sb("rstd_bf", [32, QB])

        p_s = [ps("p_s0", [128, QB]), ps("p_s1", [128, QB])]
        p_o = ps("p_o", [65, QB])
        p_v = ps("p_v", [128, D])
        p_rs = ps("p_rs", [128, QB])
        p_sum = ps("p_sum", [32, QB])
        p_sq = ps("p_sq", [32, QB])
        p_nu = [p_v[:, 0:QB], p_v[:, QB:D]]

        pe, sc, ve, dgp, dgs = (sems[k] for k in ("pe", "sc", "ve", "dgp", "dgs"))
        dgx = [sems["dgx0"], sems["dgx1"]]
        dst_ = [sems["ds0"], sems["ds1"]]
        allsems = list(sems.values())
        E = pr.emit

        for dst, src_ in [(wq_sb, wq), (wk_sb, wk), (wv_sb, wv)]:
            E("gpsimd", lambda g, d=dst, s_=src_: g.dma_start(
                out=d[:], in_=s_.rearrange("c p d -> p c d")), inc=dgp, inc_by=16)
        for dst, src_ in [(bq_sb, bq), (bk_sb, bk), (gcol_sb, gcol),
                          (bv_row, bv), (gb_sb, gbrow)] + \
                         [(cs[k], cin[k]) for k in CONST_SHAPES_A]:
            E("gpsimd", lambda g, d=dst, s_=src_: g.dma_start(out=d[:], in_=s_),
              inc=dgp, inc_by=16)
        LDG = dgp.n
        LDS = LDG
        t_neg50 = E("vector", lambda v: v.memset(neg50[:], -50.0), inc=ve)
        E("vector", lambda v: v.memset(epsA[:], 1e-8), inc=ve)
        t_epsm = E("vector", lambda v: v.memset(epsB[:], 1e-5), inc=ve)
        E("vector", lambda v: v.memset(nurows[:], -1.0), inc=ve)
        t_vones = E("vector", lambda v: v.memset(
            v_tm[:].rearrange("p (t h e) -> p t h e", t=NT, h=H)[:, :, :, 64:65], 1.0),
            inc=ve)
        t_bvev = 0
        for half in range(2):
            w = [(dgp, LDG)] + ([(ve, t_bvev)] if half else [])
            tmm = E("tensor", lambda t, h=half: t.matmul(
                p_rs[0:32, :], cs["c_1x32"][:], bv_row[:, h * QB:(h + 1) * QB],
                start=True, stop=True), waits=w, inc=pe)
            t_bvev = E("vector", lambda v, h=half: v.tensor_scalar(
                bvrep[:, h * QB:(h + 1) * QB], p_rs[0:32, :], 1.0 / 32.0, None, ALU.mult),
                waits=[(pe, tmm)], inc=ve)

        def load_xblk(s, b, slot, t_prev_cons, do_msum):
            for j in range(4):
                i = b * 4 + j
                tsl = j % 2
                w = [t_prev_cons, (dst_[tsl], t_tr_prev[tsl]), (ve, t_ms_prev[tsl])]
                tl = E("gpsimd", lambda g, i=i, tsl=tsl, s=s: g.dma_start(
                    out=x_tm[:, tsl, :], in_=xin[s, i * 128:(i + 1) * 128, :]),
                    waits=w, inc=dgx[tsl], inc_by=16)
                t_tr_prev[tsl] = E("sync", lambda sy, j=j, tsl=tsl, slot=slot: sy.dma_start(
                    out=x_blk[:, slot, :, j * 128:(j + 1) * 128],
                    in_=x_tm[:, tsl, :].rearrange("p (c f) -> p c f", c=NCH),
                    transpose=True), waits=[(dgx[tsl], tl), t_prev_cons],
                    inc=dst_[tsl], inc_by=16)
                if do_msum:
                    t_ms_prev[tsl] = E("vector", lambda v, i=i, tsl=tsl: v.reduce_sum(
                        msum_col[:, i:i + 1], x_tm[:, tsl, :], axis=AX.X),
                        waits=[(dgx[tsl], tl)], inc=ve)
            return [(dst_[0], t_tr_prev[0]), (dst_[1], t_tr_prev[1])]

        for s in range(NSEQ):
            pr.barrier_all(allsems)
            t_tr_prev = [0, 0]
            t_ms_prev = [0, 0]
            t_mrow_ev = 0
            t_bank_free = [0, 0]
            t_vev = 0
            t_psum_free0 = 0
            t_xblk_cons = [(pe, 0), (pe, 0)]
            t_xblk_rdy = [0, 0]
            for b in range(NQB):
                slot = b % 2
                wrdy = load_xblk(s, b, slot, t_xblk_cons[slot], do_msum=True)
                mm = None
                for c in range(NCH):
                    mm = E("tensor", lambda t, c=c, slot=slot: t.matmul(
                        p_sum[:], cs["c_128x32"][:], x_blk[:, slot, c, :],
                        start=(c == 0), stop=(c == NCH - 1)),
                        waits=wrdy + [(dgp, LDG), (ve, t_mrow_ev)], inc=pe)
                t_mrow_ev = E("vector", lambda v, b=b: v.tensor_copy(
                    m_raw[0:1, b * QB:(b + 1) * QB], p_sum[0:1, :]),
                    waits=[(pe, mm)], inc=ve)
                for c in range(NCH):
                    bi = c % 2
                    mm = None
                    for kc in range(NCH):
                        mm = E("tensor", lambda t, c=c, kc=kc, bi=bi, slot=slot: t.matmul(
                            p_s[bi][:], wk_sb[:, kc, c * 128:(c + 1) * 128],
                            x_blk[:, slot, kc, :],
                            start=(kc == 0), stop=(kc == NCH - 1)),
                            waits=[(sc, t_bank_free[bi])] + wrdy, inc=pe)
                    t_bank_free[bi] = E("scalar", lambda sce, c=c, b=b, bi=bi: sce.activation(
                        k_fm[:, c, b * QB:(b + 1) * QB], p_s[bi][:], AF.Relu,
                        bias=bk_sb[:, c:c + 1], scale=1.0),
                        waits=[(pe, mm), (dgp, LDS)], inc=sc)
                for j in range(4):
                    i = b * 4 + j
                    mm = None
                    for kc in range(NCH):
                        for half in range(2):
                            mm = E("tensor", lambda t, j=j, kc=kc, hf=half, slot=slot: t.matmul(
                                p_v[:, hf * QB:(hf + 1) * QB],
                                x_blk[:, slot, kc, j * 128:(j + 1) * 128],
                                wv_sb[:, kc, hf * QB:(hf + 1) * QB],
                                start=(kc == 0), stop=False),
                                waits=[(sc, t_vev), (ve, t_bvev)], inc=pe)
                    for half in range(2):
                        mm = E("tensor", lambda t, hf=half: t.matmul(
                            p_v[:, hf * QB:(hf + 1) * QB], cs["c_32x128"][:],
                            bvrep[:, hf * QB:(hf + 1) * QB], start=False, stop=True), inc=pe)
                    t_vev = E("scalar", lambda sce, i=i: sce.activation(
                        v_tm[:, i * VW:(i + 1) * VW].rearrange(
                            "p (h e) -> p h e", h=H)[:, :, 0:64],
                        p_v[:].rearrange("p (h e) -> p h e", h=H), AF.Relu),
                        waits=[(pe, mm), (ve, t_vones)], inc=sc)
                t_xblk_cons[slot] = (pe, mm)
            t_psum_free0 = t_mrow_ev
            tmp = E("scalar", lambda c: c.activation(m_colv2[:], msum_col[:], AF.Sign),
                    waits=[(ve, t_ms_prev[0]), (ve, t_ms_prev[1])], inc=sc)
            t_mcol = E("scalar", lambda c: c.activation(m_colv[:], m_colv2[:], AF.Abs),
                       waits=[(sc, tmp)], inc=sc)
            t_mb = E("vector", lambda v: v.tensor_scalar(
                m_colv2[:], m_colv[:], 100.0, -150.0, ALU.mult, ALU.add),
                waits=[(sc, t_mcol)], inc=ve)
            t_mrow_s = 0
            for b in range(NQB):
                tmp = E("scalar", lambda c, b=b: c.activation(
                    rdm_row[:], m_raw[0:1, b * QB:(b + 1) * QB], AF.Sign),
                    waits=[(ve, t_mrow_ev), (sc, t_mrow_s)], inc=sc)
                t_mrow_s = E("scalar", lambda c, b=b: c.activation(
                    m_raw[0:1, b * QB:(b + 1) * QB], rdm_row[:], AF.Abs),
                    waits=[(sc, tmp)], inc=sc)
            t_mrow = t_mrow_s

            t_attn_cons = 0
            t_praw_cons = [(ve, 0), (ve, 0)]
            t_pbuf_cons = [0, 0]
            t_oev = 0
            t_prs_free = (ve, t_bvev)
            t_den16_free = (ve, 0)
            t_dtmp = [0, 0]
            t_dprev = 0
            t_rdm16_free = (ve, 0)
            t_m16_free = (ve, 0)
            t_nu_free = [0, 0]
            t_x1_store = 0
            t_nurows_cons = 0
            t_mubf_cons = 0
            t_rstdbf_cons = 0
            t_r_cons_pe = (pe, 0)
            t_r_cons_sc = (sc, 0)
            t_psum_free = (ve, t_psum_free0)
            t_psq_free = (ve, 0)
            t_rows_free = (ve, 0)
            t_qev = 0
            t_q_cons = 0
            t_am = 0
            for qb in range(NQB):
                slot = qb % 2
                wrdy = load_xblk(s, qb, slot, t_xblk_cons[slot], do_msum=False)
                pm = E("tensor", lambda t, qb=qb: t.matmul(
                    p_rs[0:64, :], cs["c_1x64"][:],
                    m_raw[0:1, qb * QB:(qb + 1) * QB], start=True, stop=True),
                    waits=[(sc, t_mrow), t_prs_free], inc=pe)
                t_m16 = E("vector", lambda v: v.tensor_copy(
                    mrow16[0:16, :], p_rs[0:16, :]),
                    waits=[(pe, pm), t_m16_free], inc=ve)
                t_prs_free = (ve, t_m16)
                for c in range(NCH):
                    bi = c % 2
                    mm = None
                    for kc in range(NCH):
                        mm = E("tensor", lambda t, c=c, kc=kc, bi=bi, slot=slot: t.matmul(
                            p_s[bi][:], wq_sb[:, kc, c * 128:(c + 1) * 128],
                            x_blk[:, slot, kc, :],
                            start=(kc == 0), stop=(kc == NCH - 1)),
                            waits=[(sc, t_bank_free[bi]), (pe, t_q_cons)] + wrdy, inc=pe)
                    t_bank_free[bi] = E("scalar", lambda sce, c=c, bi=bi: sce.activation(
                        q_blk[:, c, :], p_s[bi][:], AF.Relu,
                        bias=bq_sb[:, c:c + 1], scale=1.0),
                        waits=[(pe, mm), (dgp, LDS)], inc=sc)
                    t_qev = t_bank_free[bi]
                nkt = 4 * qb + 4
                w_po = [(ve, t_oev)]
                for h in range(H):
                    hb = (h % 2) * 64
                    hc = h // 2
                    t_omm = None
                    for kt in range(nkt):
                        bi = kt % 2
                        smm = E("tensor", lambda t, hb=hb, hc=hc, kt=kt, bi=bi: t.matmul(
                            p_s[bi][:], k_fm[hb:hb + 64, hc, kt * 128:(kt + 1) * 128],
                            q_blk[hb:hb + 64, hc, :],
                            start=True, stop=True),
                            waits=[(sc, max(t_bank_free[bi], t_qev))], inc=pe)
                        t_q_cons = smm
                        texp = E("scalar", lambda sce, bi=bi, kt=kt: sce.activation(
                            praw[:, bi, :], p_s[bi][:], AF.Exp,
                            bias=m_colv2[:, kt:kt + 1], scale=0.125),
                            waits=[(pe, smm), t_praw_cons[bi], (ve, t_mb)], inc=sc)
                        t_bank_free[bi] = texp
                        off = kt - 4 * qb
                        if off >= 0:
                            ttri = E("vector", lambda v, bi=bi, off=off: v.tensor_tensor(
                                pbuf[:, bi, :], praw[:, bi, :],
                                cs["tri01"][:, 384 - off * 128:384 - off * 128 + QB],
                                ALU.mult),
                                waits=[(sc, texp), (pe, t_pbuf_cons[bi])], inc=ve)
                            t_praw_cons[bi] = (ve, ttri)
                            rhs = pbuf[:, bi, :]
                            w_omm = [(ve, ttri)]
                        else:
                            rhs = praw[:, bi, :]
                            w_omm = [(sc, texp)]
                        t_omm = E("tensor", lambda t, h=h, kt=kt, rhs=rhs,
                                  first=(kt == 0), last=(kt == nkt - 1): t.matmul(
                            p_o[:], v_tm[:, kt * VW + h * 65: kt * VW + h * 65 + 65],
                            rhs, start=first, stop=last),
                            waits=w_omm + w_po + [(sc, t_vev)], inc=pe)
                        w_po = []
                        if off >= 0:
                            t_pbuf_cons[bi] = t_omm
                        else:
                            t_praw_cons[bi] = (pe, t_omm)
                    d0a = E("vector", lambda v, h=h: v.tensor_copy(
                        den_tmp[0:1, h % 2, :], p_o[64:65, :]),
                        waits=[(pe, t_omm), (dgp, t_dtmp[h % 2])], inc=ve)
                    t_dtmp[h % 2] = E("gpsimd", lambda g, h=h: g.dma_start(
                        out=den16[h:h + 1, :], in_=den_tmp[0:1, h % 2, :]),
                        waits=[(ve, d0a), t_den16_free, (dgp, t_dprev)],
                        inc=dgp, inc_by=16)
                    t_dprev = t_dtmp[h % 2]
                    t_oev = E("vector", lambda v, hb=hb, hc=hc: v.tensor_copy(
                        attn_blk[hb:hb + 64, hc, :], p_o[0:64, :]),
                        waits=[(ve, max(d0a, t_attn_cons, t_am))], inc=ve)
                    w_po = [(ve, t_oev)]
                tse = E("vector", lambda v: v.tensor_scalar(
                    den16[0:16, :], den16[0:16, :], 1.0, 1e-30, ALU.mult, ALU.add),
                    waits=[(dgp, t_dprev), t_rdm16_free], inc=ve)
                rc = E("vector", lambda v: v.reciprocal(
                    den16[0:16, :], den16[0:16, :]),
                    waits=[(ve, tse)], inc=ve)
                t_den16_free = (ve, rc)
                rm = E("vector", lambda v: v.tensor_tensor(
                    rdm16b[0:16, :], den16[0:16, :], mrow16[0:16, :], ALU.mult),
                    waits=[(ve, rc)], inc=ve)
                t_rdm16_free = (ve, rm)
                t_m16_free = (ve, rm)
                bm = None
                for hc2 in range(NCH):
                    bm = E("tensor", lambda t, hc2=hc2: t.matmul(
                        p_rs[:], cs["sel16"][:, hc2 * 128:(hc2 + 1) * 128],
                        rdm16b[0:16, :], start=True, stop=True),
                        waits=[(ve, rm), t_prs_free], inc=pe)
                    t_am = E("vector", lambda v, hc2=hc2: v.tensor_tensor(
                        r_blk[:, hc2, :], attn_blk[:, hc2, :], p_rs[:], ALU.mult),
                        waits=[(pe, bm), t_r_cons_pe, t_r_cons_sc], inc=ve)
                    t_prs_free = (ve, t_am)
                t_r = [0] * NCH
                for c in range(NCH):
                    t_r[c] = E("vector", lambda v, c=c, slot=slot: v.tensor_tensor(
                        r_blk[:, c, :], r_blk[:, c, :],
                        x_blk[:, slot, c, :], ALU.add),
                        waits=[(ve, t_am)] + wrdy, inc=ve)


                def ln_stats(src_blk, t_src):
                    mm1 = None
                    for c in range(NCH):
                        mm1 = E("tensor", lambda t, c=c, sbk=src_blk: t.matmul(
                            p_sum[:], cs["c_128x32"][:], sbk[:, c, :],
                            start=(c == 0), stop=(c == NCH - 1)),
                            waits=[(ve, t_src[c]), t_psum_free], inc=pe)
                    mm2 = 0
                    tsq = 0
                    for c in range(NCH):
                        tsq = E("scalar", lambda sce, c=c, sbk=src_blk: sce.activation(
                            r2f[:], sbk[:, c, :], AF.Square),
                            waits=[(ve, t_src[c]), (pe, mm2)], inc=sc)
                        mm2 = E("tensor", lambda t, c=c: t.matmul(
                            p_sq[:], cs["c_128x32"][:], r2f[:],
                            start=(c == 0), stop=(c == NCH - 1)),
                            waits=[(sc, tsq), t_psq_free], inc=pe)
                    return mm1, mm2, tsq

                def ln_rows(mm1, mm2, eps_ap):
                    a1 = E("scalar", lambda sce: sce.mul(row_mu[:], p_sum[0:32, :], 1.0 / D),
                           waits=[(pe, mm1), t_rows_free], inc=sc)
                    a2 = E("scalar", lambda sce: sce.activation(row_mu2[:], row_mu[:], AF.Square),
                           waits=[(sc, a1)], inc=sc)
                    a3 = E("vector", lambda v: v.scalar_tensor_tensor(
                        row_s2[:], p_sq[0:32, :], 1.0 / D, row_mu2[:], ALU.mult, ALU.subtract),
                        waits=[(pe, mm2), (sc, a2)], inc=ve)
                    a4 = E("scalar", lambda sce, e=eps_ap: sce.activation(
                        row_mu2[:], row_s2[:], AF.Ln, bias=e[:], scale=1.0),
                        waits=[(ve, max(a3, t_epsm))], inc=sc)
                    a5 = E("scalar", lambda sce: sce.activation(
                        row_rstd[:], row_mu2[:], AF.Exp, bias=0.0, scale=-0.5),
                        waits=[(sc, a4)], inc=sc)
                    return a1, a3, a5

                mm1, mm2, tsq1 = ln_stats(r_blk, t_r)
                a_mu, a_s2, a_rstd = ln_rows(mm1, mm2, epsA)
                t_psum_free = (sc, a_mu)
                t_psq_free = (ve, a_s2)
                b1 = E("vector", lambda v: v.tensor_tensor(
                    row_murstd[:], row_mu[:], row_rstd[:], ALU.mult),
                    waits=[(sc, a_rstd)], inc=ve)
                b2 = E("vector", lambda v: v.tensor_copy(nurows[0:1, :], row_murstd[0:1, :]),
                       waits=[(ve, b1), (pe, t_nurows_cons)], inc=ve)
                b3 = E("vector", lambda v: v.tensor_copy(rstd_bf[:], row_rstd[:]),
                       waits=[(sc, a_rstd), (pe, t_rstdbf_cons)], inc=ve)
                b4 = E("tensor", lambda t: t.matmul(
                    p_rs[:], cs["c_32x128d"][:], rstd_bf[0:32, :],
                    start=True, stop=True),
                    waits=[(ve, b3), t_prs_free], inc=pe)
                t_rows_free = (ve, b3)
                t_pass2 = [0] * NCH
                nmm = 0
                for c in range(NCH):
                    nuh = c % 2
                    nmm = E("tensor", lambda t, c=c, nuh=nuh: t.matmul(
                        p_nu[nuh], gb_sb[:, c * 128:(c + 1) * 128], nurows[:],
                        start=True, stop=True),
                        waits=[(ve, max(b2, t_nu_free[nuh])), (sc, t_vev)], inc=pe)
                    t_nurows_cons = nmm
                    tp1 = E("vector", lambda v, c=c: v.scalar_tensor_tensor(
                        t1_blk[:, c, :], r_blk[:, c, :], gcol_sb[:, c:c + 1],
                        p_rs[:], ALU.mult, ALU.mult),
                        waits=[(ve, t_r[c]), (pe, b4)], inc=ve)
                    t_pass2[c] = E("vector", lambda v, c=c, nuh=nuh: v.tensor_tensor(
                        x1_blk[:, c, :], t1_blk[:, c, :], p_nu[nuh], ALU.subtract),
                        waits=[(ve, tp1), (pe, nmm), (dgs, t_x1_store)], inc=ve)
                    t_nu_free[nuh] = t_pass2[c]
                    t_prs_free = (ve, tp1)
                t_v = [0] * NCH
                for c in range(NCH):
                    t_v[c] = E("vector", lambda v, c=c, slot=slot: v.tensor_tensor(
                        r_blk[:, c, :], x1_blk[:, c, :],
                        x_blk[:, slot, c, :], ALU.add),
                        waits=[(ve, t_pass2[c]), (pe, mm1), (sc, tsq1)], inc=ve)
                t_xblk_cons[slot] = (ve, t_v[NCH - 1])
                mm1b, mm2b, tsq2 = ln_stats(r_blk, t_v)
                t_r_cons_pe = (pe, mm1b)
                t_r_cons_sc = (sc, tsq2)
                a_mu2, a_s2b, a_rstd2 = ln_rows(mm1b, mm2b, epsB)
                t_psum_free = (sc, a_mu2)
                t_psq_free = (ve, a_s2b)
                c1 = E("vector", lambda v: v.tensor_copy(mu_bf[:], row_mu[:]),
                       waits=[(sc, a_mu2), (pe, t_mubf_cons)], inc=ve)
                c2 = E("vector", lambda v: v.tensor_copy(rstd_bf[:], row_rstd[:]),
                       waits=[(sc, a_rstd2), (pe, b4)], inc=ve)
                t_rows_free = (ve, c2)
                c3 = E("tensor", lambda t: t.matmul(
                    p_nu[0], cs["c_32x128d"][:], mu_bf[0:32, :],
                    start=True, stop=True),
                    waits=[(ve, max(c1, t_nu_free[0]))], inc=pe)
                c4 = E("tensor", lambda t: t.matmul(
                    p_nu[1], cs["c_32x128d"][:], rstd_bf[0:32, :],
                    start=True, stop=True),
                    waits=[(ve, max(c2, t_nu_free[1]))], inc=pe)
                t_mubf_cons = c3
                t_rstdbf_cons = c4
                t_p4 = [0] * NCH
                for c in range(NCH):
                    t_p4[c] = E("vector", lambda v, c=c: v.tensor_tensor(
                        t1_blk[:, c, :], r_blk[:, c, :], p_nu[0], ALU.subtract),
                        waits=[(ve, t_v[c]), (pe, max(c3, mm2b))], inc=ve)
                t_p5 = [0] * NCH
                for c in range(NCH):
                    t_p5[c] = E("vector", lambda v, c=c: v.tensor_tensor(
                        x1_blk[:, c, :], t1_blk[:, c, :], p_nu[1], ALU.mult),
                        waits=[(ve, t_p4[c]), (pe, c4)], inc=ve)
                t_nu_free[0] = t_p4[NCH - 1]
                t_nu_free[1] = t_p5[NCH - 1]
                t_attn_cons = t_p5[NCH - 1]
                t_x1_store = E("gpsimd", lambda g, qb=qb, s=s: g.dma_start(
                    out=xa[s][:, :, qb * QB:(qb + 1) * QB].rearrange("c p t -> p c t"),
                    in_=x1_blk[:]), waits=[(ve, t_p5[NCH - 1])], inc=dgs, inc_by=16)

        pr.barrier_all(allsems)


def prep_a(inputs, n_cores=8):
    x = np.asarray(inputs["L0_attn_x"])
    Wq = np.asarray(inputs["L0_attn_Wq"])
    Wk = np.asarray(inputs["L0_attn_Wk"])
    Wv = np.asarray(inputs["L0_attn_Wv"])
    bq = np.asarray(inputs["L0_attn_bq"])
    bk = np.asarray(inputs["L0_attn_bk"])
    bv = np.asarray(inputs["L0_attn_bv"])
    gamma = np.asarray(inputs["L0_attn_gamma"])
    beta = np.asarray(inputs["L0_attn_beta"])
    consts = consts_a()
    wq_l = np.ascontiguousarray(Wq.reshape(NCH, 128, D))
    wk_l = np.ascontiguousarray(Wk.reshape(NCH, 128, D))
    wv_l = np.ascontiguousarray(Wv.reshape(NCH, 128, D))
    bq_l = np.ascontiguousarray(bq.reshape(NCH, 128).T)
    bk_l = np.ascontiguousarray(bk.reshape(NCH, 128).T)
    bv_l = bv.reshape(1, D)
    gcol_l = np.ascontiguousarray(gamma.reshape(NCH, 128).T)
    gbrow_l = np.stack([gamma, beta], axis=0)
    shared = dict(wq=wq_l, wk=wk_l, wv=wv_l, bq=bq_l, bk=bk_l, bv=bv_l,
                  gcol=gcol_l, gbrow=gbrow_l, **consts)
    in_maps = []
    for core in range(n_cores):
        m = dict(shared)
        m["xin"] = np.ascontiguousarray(x[core * NSEQ:(core + 1) * NSEQ])
        in_maps.append(m)
    return in_maps


CONST_SHAPES_B = {
    "c_128x32": ([128, 32], BF16), "c_1x32": ([1, 32], BF16),
    "c_32x256": ([32, 512], BF16), "c_1x128f": ([1, 128], F32),
}


def consts_b():
    c = consts_a()
    out = {k: c[k] for k in CONST_SHAPES_B if k in c}
    out["c_32x256"] = np.ones((32, 512), dtype=BF)
    out["c_1x128f"] = np.ones((1, 128), dtype=np.float32)
    return out


def emit_b(nc, pr, sems, ctx, dram):
    T, NQB, NT = (globals()["T"], globals()["NQB"], globals()["NT"])
    HS = 2048
    NHC = HS // 128
    RG = T // 8
    xi = dram["xab"]
    sw1, sw3, sw2 = dram["b_sw1"], dram["b_sw3"], dram["b_sw2"]
    w1, w2 = dram["b_w1"], dram["b_w2"]
    sb1c, sb3c, sb2c = dram["b_sb1c"], dram["b_sb3c"], dram["b_sb2c"]
    b1c, b2r = dram["b_b1c"], dram["b_b2r"]
    cin = {k: dram[k] for k in CONST_SHAPES_B}
    xo = dram["xbc"]
    temb = dram["b_temb"]
    if True:
        sb = lambda name, shape, dt=BF16: ctx.enter_context(nc.sbuf_tensor("b_" + name, shape, dt))
        ps = lambda name, shape, dt=F32: ctx.enter_context(nc.psum_tensor("b_" + name, shape, dt))
        arena = sb("arena", [128, 3 * NCH * HS])
        sw1_v = arena[:, 0:NCH * HS].rearrange("p (c d) -> p c d", c=NCH)
        sw3_v = arena[:, NCH * HS:2 * NCH * HS].rearrange("p (c d) -> p c d", c=NCH)
        sw2_v = arena[:, 2 * NCH * HS:2 * NCH * HS + NHC * D].rearrange(
            "p (c d) -> p c d", c=NHC)
        EW = NCH * D
        wexp = [None, None]
        for es in range(2):
            wexp[es] = (arena[:, es * 2 * EW:es * 2 * EW + EW].rearrange(
                "p (c d) -> p c d", c=NCH),
                arena[:, es * 2 * EW + EW:(es + 1) * 2 * EW].rearrange(
                "p (c d) -> p c d", c=NCH))
        x1_s = sb("x1_s", [128, NCH, T])
        facc = sb("facc", [128, NCH, T])
        h_s = sb("h_s", [128, NHC, QB])
        he_sb = sb("he_sb", [128, NCH, QB])
        silu_sb = sb("silu_sb", [128, 1, QB])
        v_blk = h_s[:, 8:16, :]
        t_blk = h_s[:, 0:8, :]
        r2f = sb("r2f", [128, QB])
        b2rep = sb("b2rep", [32, D])
        b2row = sb("b2row", [1, D])
        sb1_sb = sb("sb1_sb", [128, NHC], F32)
        sb3_sb = sb("sb3_sb", [128, NHC], F32)
        sb2_sb = sb("sb2_sb", [128, NCH], F32)
        sb2h = sb("sb2h", [128, NCH], F32)
        b1_sb = sb("b1_sb", [128, 2, NCH], F32)
        cs = {k: sb(k + "_s", sh, dt) for k, (sh, dt) in CONST_SHAPES_B.items()}
        epsB = sb("epsB", [1, 1], F32)
        row_mu = sb("row_mu", [1, QB], F32)
        rec_rowA = row_mu
        row_mu2 = sb("row_mu2", [1, QB], F32)
        den_row = row_mu2
        row_s2 = sb("row_s2", [1, QB], F32)
        rec_rowB = row_s2
        row_rstd = sb("row_rstd", [1, QB], F32)
        pA = ps("pA", [128, QB])
        pB = ps("pB", [128, QB])
        pC = ps("pC", [128, QB])
        pD = ps("pD", [128, QB])
        p_sum = ps("p_sum", [32, QB])
        p_sq = ps("p_sq", [32, QB])
        p_r1 = ps("p_r1", [128, QB])
        p_r2 = ps("p_r2", [128, QB])

        pe, sc, ve, dgp, dgx, dgs = (sems[k] for k in
                                     ("pe", "sc", "ve", "dgp", "dgx", "dgs"))
        dgw = [sems["dgw0"], sems["dgw1"]]
        allsems = list(sems.values())
        E = pr.emit
        E("gpsimd", lambda g: g.dma_start(out=sb("temb_sb", [1, 16], F32)[:],
                                          in_=temb[0:1, 0:16]), inc=dgp, inc_by=16)

        for dst, src_ in [(sb1_sb, sb1c), (sb3_sb, sb3c), (sb2_sb, sb2c)] + \
                         [(cs[k], cin[k]) for k in CONST_SHAPES_B]:
            E("gpsimd", lambda g, d=dst, s_=src_: g.dma_start(out=d[:], in_=s_),
              inc=dgp, inc_by=16)
        LDGP = dgp.n
        E("vector", lambda v: v.memset(epsB[:], 1e-5), inc=ve)
        t_sb2h = E("scalar", lambda c: c.mul(sb2h[:], sb2_sb[:], 1.0),
                   waits=[(dgp, LDGP)], inc=sc)

        for s in range(NSEQ):
            pr.barrier_all(allsems)
            for dst, src_ in [(sw1_v, sw1), (sw3_v, sw3), (sw2_v, sw2)]:
                E("gpsimd", lambda g, d=dst, s_=src_: g.dma_start(
                    out=d, in_=s_.rearrange("c p d -> p c d")), inc=dgp, inc_by=16)
            LDGP = dgp.n
            t_x1 = E("gpsimd", lambda g, s=s: g.dma_start(
                out=x1_s[:], in_=xi[s].rearrange("c p t -> p c t")),
                inc=dgx, inc_by=16)
            XRDY = (dgx, t_x1)

            t_bankG = [0, 0]
            t_bankGs = [0, 0]
            t_bankY = [0, 0]
            t_hs_cons = 0
            t_facc_st = 0
            for tb in range(NQB):
                t_h = [0] * NHC
                for hc in range(NHC):
                    bk = [pA, pB][hc % 2]
                    mm = None
                    for kc in range(NCH):
                        mm = E("tensor", lambda t, hc=hc, kc=kc, tb=tb, bk=bk: t.matmul(
                            bk[:], sw1_v[:, kc, hc * 128:(hc + 1) * 128],
                            x1_s[:, kc, tb * QB:(tb + 1) * QB],
                            start=(kc == 0), stop=(kc == NCH - 1)),
                            waits=[XRDY, (dgp, LDGP), (ve, t_bankG[hc % 2]),
                                   (sc, t_bankGs[hc % 2])], inc=pe)
                    t_sil = E("scalar", lambda sce, hc=hc, bk=bk: sce.activation(
                        silu_sb[:, 0, :], bk[:], AF.Silu,
                        bias=sb1_sb[:, hc:hc + 1], scale=1.0),
                        waits=[(pe, mm), (ve, max(t_bankG[0], t_bankG[1]))], inc=sc)
                    t_bankGs[hc % 2] = t_sil
                    mm = None
                    for kc in range(NCH):
                        mm = E("tensor", lambda t, hc=hc, kc=kc, tb=tb, bk=bk: t.matmul(
                            bk[:], sw3_v[:, kc, hc * 128:(hc + 1) * 128],
                            x1_s[:, kc, tb * QB:(tb + 1) * QB],
                            start=(kc == 0), stop=(kc == NCH - 1)),
                            waits=[(sc, t_sil)], inc=pe)
                    t_h[hc] = E("vector", lambda v, hc=hc, bk=bk: v.scalar_tensor_tensor(
                        h_s[:, hc, :], bk[:], sb3_sb[:, hc:hc + 1],
                        silu_sb[:, 0, :], ALU.add, ALU.mult),
                        waits=[(pe, mm), (sc, t_sil), (pe, t_hs_cons)], inc=ve)
                    t_bankG[hc % 2] = t_h[hc]
                for oc in range(NCH):
                    bk = [pC, pD][oc % 2]
                    mm = None
                    for hc in range(NHC):
                        mm = E("tensor", lambda t, oc=oc, hc=hc, bk=bk: t.matmul(
                            bk[:], sw2_v[:, hc, oc * 128:(oc + 1) * 128],
                            h_s[:, hc, :],
                            start=(hc == 0), stop=(hc == NHC - 1)),
                            waits=[(ve, max(t_h[hc], t_bankY[oc % 2]))], inc=pe)
                    t_bankY[oc % 2] = E("vector", lambda v, oc=oc, tb=tb, bk=bk: v.tensor_scalar(
                        facc[:, oc, tb * QB:(tb + 1) * QB], bk[:],
                        sb2h[:, oc:oc + 1], 0.5, ALU.add, ALU.mult),
                        waits=[(pe, mm), (sc, t_sb2h)], inc=ve)
                t_hs_cons = mm

            t_wld = [0, 0]
            t_wcons = [t_hs_cons, t_hs_cons]
            t_b2ld = [0, 0]
            t_he_cons = 0
            t_bankE = [0, 0]
            t_acc = 0
            t_b2rep_ev = 0
            t_b2rep_cons = 0
            t_b2row_cons = 0
            t_ywar = [t_bankY[0], t_bankY[1]]
            for e in range(8):
                es = e % 2
                E("gpsimd", lambda g, e=e, es=es: g.dma_start(
                    out=wexp[es][0], in_=w1[e].rearrange("c p d -> p c d")),
                    waits=[(pe, t_wcons[es])], inc=dgw[es], inc_by=16)
                E("gpsimd", lambda g, e=e, es=es: g.dma_start(
                    out=wexp[es][1], in_=w2[e].rearrange("c p d -> p c d")),
                    inc=dgw[es], inc_by=16)
                E("gpsimd", lambda g, e=e: g.dma_start(
                    out=b2row[:], in_=b2r[e]),
                    waits=[(pe, t_b2row_cons)], inc=dgw[es], inc_by=16)
                t_wall = E("gpsimd", lambda g, e=e, es=es: g.dma_start(
                    out=b1_sb[:, es, :], in_=b1c[e]),
                    waits=[(sc, t_bankE[es])], inc=dgw[es], inc_by=16)
                t_wld[es] = t_wall
                t_b2ld[es] = t_wall
                t_b1ld = t_wall
                t_bc = t_b2rep_ev
                for half in range(2):
                    mmb = E("tensor", lambda t, h=half: t.matmul(
                        p_r1[0:32, :], cs["c_1x32"][:], b2row[:, h * QB:(h + 1) * QB],
                        start=True, stop=True),
                        waits=[(dgw[es], t_b2ld[es]), (ve, t_bc),
                               (pe, t_b2rep_cons)], inc=pe)
                    t_b2row_cons = mmb
                    t_bc = E("vector", lambda v, h=half: v.tensor_scalar(
                        b2rep[:, h * QB:(h + 1) * QB], p_r1[0:32, :], 1.0 / 32.0,
                        None, ALU.mult), waits=[(pe, mmb)], inc=ve)
                t_b2rep_ev = t_bc
                lo, hi = sorted(((3 * e) % 8, (3 * e + 3) % 8))
                dr = hi - lo
                for hr in range(NCH):
                    bk = [pA, pB][hr % 2]
                    mm = None
                    for kc in range(NCH):
                        mm = E("tensor", lambda t, hr=hr, kc=kc, lo=lo, hi=hi, dr=dr,
                               es=es, bk=bk: t.matmul(
                            bk[:, 0:2 * RG].rearrange("p (n two) -> p n two", two=2),
                            wexp[es][0][:, kc, hr * 128:(hr + 1) * 128],
                            x1_s[:, kc, :].rearrange(
                                "p (n eight) -> p n eight", eight=8)[:, :, lo:hi + 1:dr],
                            start=(kc == 0), stop=(kc == NCH - 1)),
                            waits=[(dgw[es], t_wld[es]), XRDY,
                                   (sc, t_bankE[hr % 2]), (ve, t_bankG[hr % 2])],
                            inc=pe)
                    t_bankE[hr % 2] = E("scalar", lambda sce, hr=hr, es=es, bk=bk:
                                        sce.activation(
                        he_sb[:, hr, 0:2 * RG], bk[:, 0:2 * RG], AF.Gelu,
                        bias=b1_sb[:, es, hr:hr + 1], scale=1.0),
                        waits=[(pe, mm), (dgw[es], t_b1ld), (pe, t_he_cons)], inc=sc)
                    t_bankG[hr % 2] = 0
                for oc in range(NCH):
                    bk = [pC, pD][oc % 2]
                    mm = None
                    for hr in range(NCH):
                        mm = E("tensor", lambda t, oc=oc, hr=hr, es=es, bk=bk: t.matmul(
                            bk[:, 0:2 * RG],
                            wexp[es][1][:, hr, oc * 128:(oc + 1) * 128],
                            he_sb[:, hr, 0:2 * RG],
                            start=(hr == 0), stop=False),
                            waits=[(sc, t_bankE[hr % 2]), (ve, max(t_ywar[oc % 2], t_acc))],
                            inc=pe)
                    mm = E("tensor", lambda t, oc=oc, bk=bk: t.matmul(
                        bk[:, 0:2 * RG], b2rep[:, oc * 128:(oc + 1) * 128],
                        cs["c_32x256"][:, 0:2 * RG], start=False, stop=True),
                        waits=[(ve, t_b2rep_ev)], inc=pe)
                    t_b2rep_cons = mm
                    t_acc = E("vector", lambda v, oc=oc, lo=lo, hi=hi, dr=dr, bk=bk:
                              v.scalar_tensor_tensor(
                        facc[:, oc, :].rearrange(
                            "p (n eight) -> p n eight", eight=8)[:, :, lo:hi + 1:dr],
                        bk[:, 0:2 * RG].rearrange("p (n two) -> p n two", two=2), 0.25,
                        facc[:, oc, :].rearrange(
                            "p (n eight) -> p n eight", eight=8)[:, :, lo:hi + 1:dr],
                        ALU.mult, ALU.add),
                        waits=[(pe, mm), (ve, t_bankY[oc % 2])], inc=ve)
                    t_bankY[oc % 2] = 0
                    t_ywar[oc % 2] = 0
                    t_he_cons = mm
                t_wcons[es] = mm

            t_psum_free = (ve, 0)
            t_psq_free = (ve, 0)
            t_rows_free = (ve, 0)
            t_mubf_cons = 0
            t_rstdbf_cons = 0
            t_v_cons_pe = (pe, 0)
            t_v_cons_sc = (sc, 0)
            t_st = 0
            for tb in range(NQB):
                t_v = [0] * NCH
                for c in range(NCH):
                    t_v[c] = E("vector", lambda v, c=c, tb=tb: v.tensor_tensor(
                        v_blk[:, c, :], x1_s[:, c, tb * QB:(tb + 1) * QB],
                        facc[:, c, tb * QB:(tb + 1) * QB], ALU.add),
                        waits=[(ve, t_acc), t_v_cons_pe, t_v_cons_sc,
                               (dgs, t_st)], inc=ve)
                mm1 = None
                for c in range(NCH):
                    mm1 = E("tensor", lambda t, c=c: t.matmul(
                        p_sum[:], cs["c_128x32"][:], v_blk[:, c, :],
                        start=(c == 0), stop=(c == NCH - 1)),
                        waits=[(ve, t_v[c]), t_psum_free], inc=pe)
                mm2 = 0
                tsq = 0
                for c in range(NCH):
                    tsq = E("scalar", lambda sce, c=c: sce.activation(
                        r2f[:], v_blk[:, c, :], AF.Square),
                        waits=[(ve, t_v[c]), (pe, mm2)], inc=sc)
                    mm2 = E("tensor", lambda t, c=c: t.matmul(
                        p_sq[:], cs["c_128x32"][:], r2f[:],
                        start=(c == 0), stop=(c == NCH - 1)),
                        waits=[(sc, tsq), t_psq_free], inc=pe)
                a1 = E("scalar", lambda sce: sce.mul(row_mu[:], p_sum[0:1, :], 1.0 / D),
                       waits=[(pe, mm1), t_rows_free], inc=sc)
                a2 = E("scalar", lambda sce: sce.activation(row_mu2[:], row_mu[:], AF.Square),
                       waits=[(sc, a1)], inc=sc)
                a3 = E("vector", lambda v: v.scalar_tensor_tensor(
                    row_s2[:], p_sq[0:1, :], 1.0 / D, row_mu2[:], ALU.mult, ALU.subtract),
                    waits=[(pe, mm2), (sc, a2)], inc=ve)
                a4 = E("scalar", lambda sce: sce.activation(
                    row_mu2[:], row_s2[:], AF.Ln, bias=epsB[:], scale=1.0),
                    waits=[(ve, a3)], inc=sc)
                a5 = E("scalar", lambda sce: sce.activation(
                    row_rstd[:], row_mu2[:], AF.Exp, bias=0.0, scale=-0.5),
                    waits=[(sc, a4)], inc=sc)
                t_psum_free = (sc, a1)
                t_psq_free = (ve, a3)
                c3 = E("tensor", lambda t: t.matmul(
                    p_r1[:], cs["c_1x128f"][:], row_mu[:], start=True, stop=True),
                    waits=[(sc, a5)], inc=pe)
                c4 = E("tensor", lambda t: t.matmul(
                    p_r2[:], cs["c_1x128f"][:], row_rstd[:], start=True, stop=True),
                    waits=[(sc, a5)], inc=pe)
                t_rows_free = (pe, c4)
                t_p4 = [0] * NCH
                for c in range(NCH):
                    t_p4[c] = E("vector", lambda v, c=c: v.tensor_tensor(
                        t_blk[:, c, :], v_blk[:, c, :], p_r1[:], ALU.subtract),
                        waits=[(ve, t_v[c]), (pe, max(c3, mm2))], inc=ve)
                t_p5 = [0] * NCH
                for c in range(NCH):
                    t_p5[c] = E("vector", lambda v, c=c: v.tensor_tensor(
                        v_blk[:, c, :], t_blk[:, c, :], p_r2[:], ALU.mult),
                        waits=[(ve, t_p4[c]), (pe, c4)], inc=ve)
                t_v_cons_pe = (pe, mm1)
                t_v_cons_sc = (sc, tsq)
                t_st = E("gpsimd", lambda g, tb=tb, s=s: g.dma_start(
                    out=xo[s][:, :, tb * QB:(tb + 1) * QB].rearrange("c p t -> p c t"),
                    in_=v_blk[:]), waits=[(ve, t_p5[NCH - 1])], inc=dgs, inc_by=16)
                t_v_cons_ve_store = t_st
                t_v_cons_pe = (pe, mm1)
        pr.barrier_all(allsems)


def prep_b(inputs, n_cores=8, xin_override=None):
    sw1 = np.asarray(inputs["L0_ffn_sw1"])
    sw3 = np.asarray(inputs["L0_ffn_sw3"])
    sw2 = np.asarray(inputs["L0_ffn_sw2"])
    sb1 = np.asarray(inputs["L0_ffn_sb1"])
    sb3 = np.asarray(inputs["L0_ffn_sb3"])
    sb2 = np.asarray(inputs["L0_ffn_sb2"])
    w1 = np.asarray(inputs["L0_ffn_w1"])
    w2 = np.asarray(inputs["L0_ffn_w2"])
    b1 = np.asarray(inputs["L0_ffn_b1"])
    b2 = np.asarray(inputs["L0_ffn_b2"])
    HS = 2048
    shared = dict(
        sw1=np.ascontiguousarray(sw1.reshape(NCH, 128, HS)),
        sw3=np.ascontiguousarray(sw3.reshape(NCH, 128, HS)),
        sw2=np.ascontiguousarray(sw2.reshape(HS // 128, 128, D)),
        w1=np.ascontiguousarray(w1.reshape(8, NCH, 128, D)),
        w2=np.ascontiguousarray(w2.reshape(8, NCH, 128, D)),
        sb1c=np.ascontiguousarray(sb1.reshape(HS // 128, 128).T),
        sb3c=np.ascontiguousarray(sb3.reshape(HS // 128, 128).T),
        sb2c=np.ascontiguousarray(sb2.reshape(NCH, 128).T),
        b1c=np.ascontiguousarray(b1.reshape(8, NCH, 128).transpose(0, 2, 1)),
        b2r=np.ascontiguousarray(b2.reshape(8, 1, D)),
        **consts_b())
    in_maps = []
    for core in range(n_cores):
        m = dict(shared)
        m["xi"] = np.ascontiguousarray(xin_override[core])
        in_maps.append(m)
    return in_maps


CONST_SHAPES_C = {
    "c_128x32": ([128, 32], BF16), "c_1x128f": ([1, 128], F32),
    "c_tri128": ([128, 128], BF16),
    "c_E2": ([128, 2], BF16),
}


def consts_c():
    c = {}
    c["c_128x32"] = np.ones((128, 32), dtype=BF)
    c["c_1x128f"] = np.ones((1, 128), dtype=np.float32)
    k = np.arange(128)[:, None]
    j = np.arange(128)[None, :]
    c["c_tri128"] = (j >= k).astype(BF)
    e2 = np.zeros((128, 2), dtype=BF)
    e2[0:64, 0] = 1
    e2[64:128, 1] = 1
    c["c_E2"] = e2
    return c


def emit_c(nc, pr, sems, ctx, dram):
    T, NQB, NT = (globals()["T"], globals()["NQB"], globals()["NT"])
    NU = T // 128
    NCK = T // 64
    xi = dram["xbc"]
    wq, wk, wv, wo = dram["c_wq"], dram["c_wk"], dram["c_wv"], dram["c_wo"]
    bo = dram["c_bo"]
    cin = {k: dram[k] for k in CONST_SHAPES_C}
    xo = dram["xcd"]
    if True:
        sb = lambda name, shape, dt=BF16: ctx.enter_context(nc.sbuf_tensor("c_" + name, shape, dt))
        ps = lambda name, shape, dt=F32: ctx.enter_context(nc.psum_tensor("c_" + name, shape, dt))
        arena = sb("arena", [128, 2 * NCH * D])
        wslot = [arena[:, 0:NCH * D].rearrange("p (c d) -> p c d", c=NCH),
                 arena[:, NCH * D:2 * NCH * D].rearrange("p (c d) -> p c d", c=NCH)]
        x_tm = sb("x_tm", [128, D])
        x2b = sb("x2b", [128, NCH, QB])
        q_fm = sb("q_fm", [128, NCH, T])
        kl_fm = sb("kl_fm", [128, 2, T])
        kg_tm = sb("kg_tm", [128, NT, 768])
        v_tm = sb("v_tm", [128, NT * 1024])
        eq = sb("eq", [128, 6, T])
        q2f = eq[:].rearrange("p c t -> p (c t)")[:, 6 * T - QB:6 * T]
        elu2 = q2f
        attn_fm = sb("attn_fm", [128, NCH, T])
        ctx_sb = sb("ctx_sb", [128, 6, 64])
        rdm_sb = ctx_sb[0:64, 0:2, :].rearrange("p a b -> p (a b)")
        NCKl = T // 64
        scr_f32 = sb("scr_f32", [128, max(12 * NCKl, 384)], F32)
        ctx_acc1 = scr_f32[:, 0:384].rearrange("p (g c) -> p g c", g=6)
        ctx_acc = [ctx_acc1, ctx_acc1]
        bkc = scr_f32[:, 0:6 * NCKl].rearrange("p (g c) -> p g c", g=6)
        bkp = scr_f32[:, 6 * NCKl:12 * NCKl].rearrange("p (g c) -> p g c", g=6)
        rbk = sb("rbk", [128, 6, NCKl + 1])
        elu1 = sb("elu1", [128, QB])
        bacc = sb("bacc", [1, 64])
        rowb1 = sb("rowb1", [1, 1], F32)
        rowb2 = sb("rowb2", [1, 1], F32)
        rowb3 = sb("rowb3", [1, 1], F32)
        bias_loc = sb("bias_loc", [128, 1], F32)
        bias_glob = sb("bias_glob", [128, 1], F32)
        bo_sb = sb("bo_sb", [128, NCH], F32)

        cs = {k: sb(k + "_s", sh, dt) for k, (sh, dt) in CONST_SHAPES_C.items()}
        epsB = sb("epsB", [1, 1], F32)
        row_mu = sb("row_mu", [1, QB], F32)
        rec_rowA = row_mu
        row_mu2 = sb("row_mu2", [1, QB], F32)
        den_row = row_mu2
        row_s2 = sb("row_s2", [1, QB], F32)
        rec_rowB = row_s2
        row_rstd = sb("row_rstd", [1, QB], F32)
        r2f = elu1
        if T >= 2048:
            eqf = eq[:].rearrange("p c t -> p (c t)")
            r_blk = eqf[:, 0:NCH * QB].rearrange("p (c t) -> p c t", c=NCH)
            t_blk = eqf[:, NCH * QB:2 * NCH * QB].rearrange("p (c t) -> p c t", c=NCH)
        else:
            r_blk = sb("r_blk", [128, NCH, QB])
            t_blk = sb("t_blk", [128, NCH, QB])

        p_s = [ps("p_s0", [128, QB]), ps("p_s1", [128, QB])]
        p_big = ps("p_big", [128, 1024])
        p_ctx = p_big[:, 0:384]
        p_at = p_big[:, 512:896]
        p_bk = p_big[:, 896:908].rearrange("p (g two) -> p g two", two=2)
        p_loc = p_big[:, 0:128]
        p_rs = ps("p_rs", [128, QB])
        p_sum = ps("p_sum", [32, QB])
        p_sq = ps("p_sq", [32, QB])

        pe, sc, ve, dgp, dgs = (sems[k] for k in ("pe", "sc", "ve", "dgp", "dgs"))
        dgw = [sems["dgw0"], sems["dgw1"]]
        dgx = [sems["dgx0"], sems["dgx1"]]
        dst_ = [sems["ds0"], sems["ds1"]]
        allsems = list(sems.values())
        E = pr.emit

        for dst, src_ in [(bo_sb, bo)] + [(cs[k], cin[k]) for k in CONST_SHAPES_C]:
            E("gpsimd", lambda g, d=dst, s_=src_: g.dma_start(out=d[:], in_=s_),
              inc=dgp, inc_by=16)
        LDGP = dgp.n
        E("vector", lambda v: v.memset(epsB[:], 1e-5), inc=ve)
        t_bacc0 = E("vector", lambda v: v.memset(bacc[:], 0.0), inc=ve)

        t_w = [0, 0]
        t_wcons = [0, 0]
        t_tr_prev = [0, 0]
        t_ms_prev = [0, 0]

        def load_w(slot, mat):
            t_w[slot] = E("gpsimd", lambda g, sl=slot, m=mat: g.dma_start(
                out=wslot[sl], in_=m.rearrange("c p d -> p c d")),
                waits=[(pe, t_wcons[slot])], inc=dgw[slot], inc_by=16)
            return t_w[slot]

        def load_xb(s, tb, t_prev_cons):
            tl = E("gpsimd", lambda g, tb=tb, s=s: g.dma_start(
                out=x2b[:], in_=xi[s][:, :, tb * QB:(tb + 1) * QB].rearrange(
                    "c p t -> p c t")),
                waits=[t_prev_cons], inc=dgx[tb % 2], inc_by=16)
            return [(dgx[tb % 2], tl)]

        for s in range(NSEQ):
            pr.barrier_all(allsems)
            t_tr_prev = [0, 0]
            load_w(0, wq)
            load_w(1, wk)
            t_bank = [0, 0]
            t_bank_ve = [0, 0]
            t_xcons = (pe, 0)
            t_qcons = 0
            t_klcons = 0
            t_kgcons = 0
            t_vcons = 0
            for tb in range(NQB):
                wrdy = load_xb(s, tb, t_xcons)
                mmlast = None
                for oc in range(NCH):
                    bi = oc % 2
                    mm = None
                    for kc in range(NCH):
                        mm = E("tensor", lambda t, oc=oc, kc=kc, bi=bi: t.matmul(
                            p_s[bi][:], wslot[0][:, kc, oc * 128:(oc + 1) * 128],
                            x2b[:, kc, :], start=(kc == 0), stop=(kc == NCH - 1)),
                            waits=wrdy + [(dgw[0], t_w[0]), (sc, t_bank[bi]),
                                          (ve, t_bank_ve[bi])], inc=pe)
                    t_bank[bi] = E("scalar", lambda sce, oc=oc, tb=tb, bi=bi: sce.copy(
                        q_fm[:, oc, tb * QB:(tb + 1) * QB], p_s[bi][:]),
                        waits=[(pe, mm)], inc=sc)
                    t_lastq = mm
                for oc in range(2):
                    bi = oc % 2
                    mm = None
                    for kc in range(NCH):
                        mm = E("tensor", lambda t, oc=oc, kc=kc, bi=bi: t.matmul(
                            p_s[bi][:], wslot[1][:, kc, oc * 128:(oc + 1) * 128],
                            x2b[:, kc, :], start=(kc == 0), stop=(kc == NCH - 1)),
                            waits=[(dgw[1], t_w[1]), (sc, t_bank[bi]),
                                   (ve, t_bank_ve[bi])], inc=pe)
                    t_bank[bi] = E("scalar", lambda sce, oc=oc, tb=tb, bi=bi: sce.copy(
                        kl_fm[:, oc, tb * QB:(tb + 1) * QB], p_s[bi][:]),
                        waits=[(pe, mm)], inc=sc)
                for j in range(4):
                    i = tb * 4 + j
                    mm0 = mm1 = None
                    for kc in range(NCH):
                        mm0 = E("tensor", lambda t, j=j, kc=kc: t.matmul(
                            p_s[0][:], x2b[:, kc, j * 128:(j + 1) * 128],
                            wslot[1][:, kc, 256:768],
                            start=(kc == 0), stop=(kc == NCH - 1)),
                            waits=[(sc, t_bank[0]), (ve, t_bank_ve[0])], inc=pe)
                        mm1 = E("tensor", lambda t, j=j, kc=kc: t.matmul(
                            p_s[1][:, 0:256], x2b[:, kc, j * 128:(j + 1) * 128],
                            wslot[1][:, kc, 768:1024],
                            start=(kc == 0), stop=(kc == NCH - 1)),
                            waits=[(sc, t_bank[1]), (ve, t_bank_ve[1])], inc=pe)
                    for gsl, (bank, width, c0) in enumerate(
                            [(p_s[0], QB, 0), (p_s[1], 256, QB)]):
                        e1 = E("vector", lambda v, bank=bank, w=width: v.tensor_scalar(
                            elu1[:, 0:w], bank[:, 0:w], 0.0, None, ALU.min),
                            waits=[(pe, mm1)], inc=ve)
                        e2 = E("scalar", lambda sce, w=width: sce.activation(
                            elu2[:, 0:w], elu1[:, 0:w], AF.Exp, bias=0.0, scale=1.0),
                            waits=[(ve, e1)], inc=sc)
                        tok = E("vector", lambda v, i=i, bank=bank, w=width, c0=c0:
                                v.scalar_tensor_tensor(
                            kg_tm[:, i, c0:c0 + w], bank[:, 0:w], 0.0,
                            elu2[:, 0:w], ALU.max, ALU.add),
                            waits=[(sc, e2), (pe, t_kgcons)], inc=ve)
                        t_bank_ve[gsl] = tok
                        t_bank[gsl] = e2
                    mmlast = mm1
                t_xcons = (pe, mmlast if mmlast is not None else 0)
            t_wcons[0] = t_lastq
            t_wcons[1] = mmlast
            t_nq = 0
            idx = 0
            t_red = t_bacc0
            for srcbuf, nch_, tok_rdy in [(q_fm, NCH, None), (kl_fm, 2, None)]:
                for c in range(nch_):
                    for tb in range(NQB):
                        tsq = E("scalar", lambda sce, c=c, tb=tb, sbuf=srcbuf: sce.activation(
                            q2f[:], sbuf[:, c, tb * QB:(tb + 1) * QB], AF.Square),
                            waits=[(sc, t_bank[0]), (sc, t_bank[1]), (pe, t_nq),
                                   (ve, t_bank_ve[0]), (ve, t_bank_ve[1])], inc=sc)
                        mm = E("tensor", lambda t: t.matmul(
                            p_sum[:], cs["c_128x32"][:], q2f[:],
                            start=True, stop=True),
                            waits=[(sc, tsq), (ve, t_red)], inc=pe)
                        t_nq = mm
                        t_red = E("vector", lambda v, idx=idx: v.reduce_max(
                            bacc[0:1, idx:idx + 1], p_sum[0:1, :], axis=AX.X),
                            waits=[(pe, mm)], inc=ve)
                        idx += 1
            nq_total = idx
            b1 = E("vector", lambda v: v.reduce_max(
                rowb1[:], bacc[0:1, 0:NCH * NQB], axis=AX.X),
                waits=[(ve, t_red)], inc=ve)
            b2 = E("vector", lambda v: v.reduce_max(
                rowb2[:], bacc[0:1, NCH * NQB:nq_total], axis=AX.X),
                waits=[(ve, t_red)], inc=ve)
            l1 = E("scalar", lambda sce: sce.activation(rowb1[:], rowb1[:], AF.Ln)
                   if False else sce.activation(rowb3[:], rowb1[:], AF.Ln),
                   waits=[(ve, b1)], inc=sc)
            l2 = E("scalar", lambda sce: sce.activation(
                rec_rowB[0:1, 0:1], rowb2[:], AF.Ln), waits=[(ve, b2)], inc=sc)
            l3 = E("vector", lambda v: v.tensor_tensor(
                rowb1[:], rowb3[:], rec_rowB[0:1, 0:1], ALU.add),
                waits=[(sc, l1), (sc, l2)], inc=ve)
            l4 = E("scalar", lambda sce: sce.activation(rowb1[:], rowb1[:], AF.Exp)
                   if False else sce.activation(rowb2[:], rowb1[:], AF.Exp,
                                                bias=0.0, scale=0.5),
                   waits=[(ve, l3)], inc=sc)
            l5 = E("vector", lambda v: v.tensor_scalar(
                rowb2[:], rowb2[:], -0.125, None, ALU.mult)
                if False else v.tensor_scalar(rowb1[:], rowb2[:], -0.125, None, ALU.mult),
                waits=[(sc, l4)], inc=ve)
            gq = E("scalar", lambda sce: sce.activation(rowb2[:], rowb3[:], AF.Exp,
                                                        bias=0.0, scale=0.5),
                   waits=[(ve, l5)], inc=sc)
            gq2 = E("vector", lambda v: v.tensor_scalar(
                rowb3[:], rowb2[:], -1.0, None, ALU.mult),
                waits=[(sc, gq), (ve, l5)], inc=ve)
            bm1 = E("tensor", lambda t: t.matmul(
                p_rs[:, 0:1], cs["c_1x128f"][:], rowb1[:], start=True, stop=True),
                waits=[(ve, l5)], inc=pe)
            bl = E("vector", lambda v: v.tensor_copy(bias_loc[:], p_rs[:, 0:1]),
                   waits=[(pe, bm1)], inc=ve)
            bm2 = E("tensor", lambda t: t.matmul(
                p_rs[:, 0:1], cs["c_1x128f"][:], rowb3[:], start=True, stop=True),
                waits=[(ve, max(gq2, bl))], inc=pe)
            bg = E("vector", lambda v: v.tensor_copy(bias_glob[:], p_rs[:, 0:1]),
                   waits=[(pe, bm2)], inc=ve)
            t_prs_free = bg

            load_w(0, wv)
            t_vev = 0
            t_xcons2 = t_xcons
            for tb in range(NQB):
                wrdy = load_xb(s, tb, t_xcons2)
                for j in range(4):
                    i = tb * 4 + j
                    mm = None
                    for kc in range(NCH):
                        for half in range(2):
                            mm = E("tensor", lambda t, j=j, kc=kc, hf=half: t.matmul(
                                p_big[:, hf * QB:(hf + 1) * QB],
                                x2b[:, kc, j * 128:(j + 1) * 128],
                                wslot[0][:, kc, hf * QB:(hf + 1) * QB],
                                start=(kc == 0), stop=(kc == NCH - 1)),
                                waits=wrdy + [(dgw[0], t_w[0]), (sc, t_vev),
                                              (ve, 0)], inc=pe)
                    t_vev = E("scalar", lambda sce, i=i: sce.copy(
                        v_tm[:, i * 1024:(i + 1) * 1024], p_big[:]),
                        waits=[(pe, mm)], inc=sc)
                t_xcons2 = (pe, mm)
                t_wcons[0] = mm
            load_w(1, wo)

            t_praw_cons = 0
            t_pb_cons = 0
            t_elu2_cons = 0
            t_rdm_cons = 0
            t_attnw = 0
            for h in range(4):
                hb = (h % 2) * 64
                hcl = h // 2
                for u in range(NU):
                    parts = ([u - 1] if u > 0 else []) + [u]
                    olast = None
                    dlast = None
                    for pi, kt in enumerate(parts):
                        bi = kt % 2
                        smm = E("tensor", lambda t, hb=hb, hcl=hcl, kt=kt, u=u, bi=bi:
                                t.matmul(
                            p_s[bi][:, 0:128],
                            kl_fm[hb:hb + 64, hcl, kt * 128:(kt + 1) * 128],
                            q_fm[hb:hb + 64, hcl, u * 128:(u + 1) * 128],
                            start=True, stop=True),
                            waits=[(sc, t_bank[bi]), (ve, t_bank_ve[bi])], inc=pe)
                        t_qcons = smm
                        ex = E("scalar", lambda sce, bi=bi: sce.activation(
                            elu1[:, 0:128], p_s[bi][:, 0:128], AF.Exp,
                            bias=bias_loc[:], scale=0.125),
                            waits=[(pe, smm), (ve, max(t_praw_cons, bl))], inc=sc)
                        t_bank[bi] = ex
                        t_bank_ve[bi] = 0
                        if kt == u:
                            pm = E("vector", lambda v: v.tensor_tensor(
                                elu2[:, 0:128], elu1[:, 0:128], cs["c_tri128"][:],
                                ALU.mult), waits=[(sc, ex), (pe, t_elu2_cons)], inc=ve)
                            psrc = elu2
                            t_praw_cons = pm
                            pwait = [(ve, pm)]
                        else:
                            psrc = elu1
                            pwait = [(sc, ex)]
                        first, last = (pi == 0), (pi == len(parts) - 1)
                        dlast = E("tensor", lambda t, psrc=psrc, first=first,
                                  last=last: t.matmul(
                            p_big[0:32, 512:640], cs["c_128x32"][:],
                            psrc[:, 0:128], start=first, stop=last),
                            waits=pwait + [(sc, t_vev), (ve, t_pb_cons)], inc=pe)
                        olast = E("tensor", lambda t, h=h, kt=kt, psrc=psrc,
                                  first=first, last=last: t.matmul(
                            p_loc[0:64, :], v_tm[:, kt * 1024 + h * 64:
                                                 kt * 1024 + h * 64 + 64],
                            psrc[:, 0:128], start=first, stop=last), inc=pe)
                        if psrc is elu2:
                            t_elu2_cons = dlast
                    d1 = E("vector", lambda v: v.tensor_scalar(
                        den_row[0:1, 0:128], p_big[0:1, 512:640], 1e-30, None, ALU.add),
                        waits=[(pe, olast)], inc=ve)
                    d2 = E("vector", lambda v: v.reciprocal(
                        rec_rowA[0:1, 0:128], den_row[0:1, 0:128]),
                        waits=[(ve, d1)], inc=ve)
                    d4 = E("tensor", lambda t: t.matmul(
                        p_rs[0:64, 0:128], cs["c_1x128f"][:, 0:64], rec_rowA[0:1, 0:128],
                        start=True, stop=True),
                        waits=[(ve, d2), (ve, t_prs_free), (pe, t_rdm_cons)], inc=pe)
                    d5 = E("vector", lambda v: v.tensor_copy(
                        rdm_sb[:], p_rs[0:64, 0:128]), waits=[(pe, d4)], inc=ve)
                    t_rdm_cons = d4
                    t_attnw = E("vector", lambda v, hb=hb, hcl=hcl, u=u: v.tensor_tensor(
                        attn_fm[hb:hb + 64, hcl, u * 128:(u + 1) * 128],
                        p_loc[0:64, :], rdm_sb[:], ALU.mult),
                        waits=[(ve, d5)], inc=ve)
                    t_pb_cons = t_attnw
                    t_prs_free = d5

            t_eq = [0] * 6
            for pg in range(6):
                tk = 0
                for tb in range(NQB):
                    tk = E("scalar", lambda sce, pg=pg, tb=tb: sce.activation(
                        eq[:, pg, tb * QB:(tb + 1) * QB],
                        q_fm[:, 2 + pg, tb * QB:(tb + 1) * QB],
                        AF.Exp, bias=bias_glob[:], scale=1.0),
                        waits=[(ve, max(bg, t_attnw)), (pe, t_qcons),
                               (pe, t_elu2_cons)], inc=sc)
                t_eq[pg] = tk
            t_bkev = 0
            for i in range(NT):
                mm = None
                for pg in range(6):
                    for rr in range(2):
                        g = 4 + pg * 2 + rr
                        mm = E("tensor", lambda t, i=i, pg=pg, rr=rr, g=g: t.matmul(
                            p_bk[rr * 64:rr * 64 + 64, pg, :],
                            kg_tm[:, i, (g - 4) * 64:(g - 4) * 64 + 64],
                            cs["c_E2"][:], start=True, stop=True),
                            waits=[(ve, t_bank_ve[0]), (ve, t_bank_ve[1]),
                                   (ve, t_bkev)], inc=pe)
                t_bkev = E("vector", lambda v, i=i: v.tensor_copy(
                    bkc[:, :, 2 * i:2 * i + 2], p_bk[:]),
                    waits=[(pe, mm)], inc=ve)
            cur, oth = bkc, bkp
            t_cs = t_bkev
            sh = 1
            while sh < NCK:
                t1 = E("vector", lambda v, cur=cur, oth=oth, sh=sh: v.tensor_tensor(
                    oth[:, :, sh:NCK], cur[:, :, sh:NCK], cur[:, :, 0:NCK - sh],
                    ALU.add), waits=[(ve, t_cs)], inc=ve)
                t2 = E("vector", lambda v, cur=cur, oth=oth, sh=sh: v.tensor_copy(
                    oth[:, :, 0:sh], cur[:, :, 0:sh]),
                    waits=[(ve, t_cs)], inc=ve)
                t_cs = max(t1, t2)
                cur, oth = oth, cur
                sh *= 2
            tnv = E("vector", lambda v: v.memset(rbk[:, :, 0:1], 1.0), waits=[], inc=ve)
            scratch = oth
            ta = E("vector", lambda v, cur=cur, scratch=scratch: v.tensor_scalar(
                scratch[:, :, :], cur[:, :, :], 1e-6, None, ALU.add),
                waits=[(ve, t_cs)], inc=ve)
            tb_ = E("vector", lambda v, scratch=scratch, cur=cur: v.reciprocal(
                cur[:, :, :], scratch[:, :, :]), waits=[(ve, ta)], inc=ve)
            tb2_ = E("vector", lambda v, cur=cur: v.tensor_copy(
                rbk[:, :, 1:NCK + 1], cur[:, :, :]), waits=[(ve, tb_)], inc=ve)
            t_rbk = max(tnv, tb2_)
            t_qp = [0] * 6
            for pg in range(6):
                t_qp[pg] = E("vector", lambda v, pg=pg: v.tensor_tensor(
                    attn_fm[:, 2 + pg, :].rearrange("p (c n) -> p c n", n=64),
                    eq[:, pg, :].rearrange("p (c n) -> p c n", n=64),
                    rbk[:, pg, 0:NCK].broadcast_to([128, NCK, 64]), ALU.mult),
                    waits=[(ve, t_rbk), (sc, t_eq[pg]), (ve, t_attnw)], inc=ve)
            t_den_mm = 0
            t_qprime = [0] * 6
            for pg in range(6):
                tlast = 0
                for tb in range(NQB):
                    mmA = E("tensor", lambda t, pg=pg, tb=tb: t.matmul(
                        p_sum[:], cs["c_128x32"][0:64, :],
                        eq[0:64, pg, tb * QB:(tb + 1) * QB], start=True, stop=True),
                        waits=[(sc, t_eq[pg]), (ve, t_den_ve(0))] if False else
                              [(sc, t_eq[pg])], inc=pe)
                    mmB = E("tensor", lambda t, pg=pg, tb=tb: t.matmul(
                        p_sq[:], cs["c_128x32"][64:128, :],
                        eq[64:128, pg, tb * QB:(tb + 1) * QB], start=True, stop=True),
                        inc=pe)
                    dA = E("vector", lambda v: v.tensor_scalar(
                        den_row[:], p_sum[0:1, :], 1e-30, None, ALU.add),
                        waits=[(pe, mmA)], inc=ve)
                    rA = E("vector", lambda v: v.reciprocal(rec_rowA[:], den_row[:]),
                           waits=[(ve, dA)], inc=ve)
                    dB = E("vector", lambda v: v.tensor_scalar(
                        den_row[:], p_sq[0:1, :], 1e-30, None, ALU.add),
                        waits=[(pe, mmB), (ve, rA)], inc=ve)
                    rB = E("vector", lambda v: v.reciprocal(rec_rowB[:], den_row[:]),
                           waits=[(ve, dB)], inc=ve)
                    bA = E("tensor", lambda t: t.matmul(
                        p_rs[0:64, :], cs["c_1x128f"][:, 0:64], rec_rowA[:],
                        start=True, stop=True),
                        waits=[(ve, rA), (ve, t_prs_free), (pe, t_rdm_cons)], inc=pe)
                    bB = E("tensor", lambda t: t.matmul(
                        p_rs[64:128, :], cs["c_1x128f"][:, 0:64], rec_rowB[:],
                        start=True, stop=True), waits=[(ve, rB)], inc=pe)
                    t_rdm_cons = bB
                    tlast = E("vector", lambda v, pg=pg, tb=tb: v.tensor_tensor(
                        eq[:, pg, tb * QB:(tb + 1) * QB],
                        attn_fm[:, 2 + pg, tb * QB:(tb + 1) * QB],
                        p_rs[:], ALU.mult),
                        waits=[(pe, bB), (ve, t_qp[pg])], inc=ve)
                    t_prs_free = tlast
                    t_den_mm = mmB
                t_qprime[pg] = tlast
            t_ctx0 = E("vector", lambda v: v.memset(ctx_sb[:], 0.0),
                       waits=[(ve, t_attnw)], inc=ve)
            t_ctx1 = E("vector", lambda v: v.memset(ctx_acc[1][:], 0.0),
                       waits=[(ve, t_rbk)], inc=ve)
            t_snap = max(t_ctx0, t_ctx1)
            t_atev = 0
            t_ctxmm = 0
            t_accr = 0
            for c in range(NCK):
                atl = None
                for pg in range(6):
                    for rr in range(2):
                        gb = rr * 64
                        atl = E("tensor", lambda t, pg=pg, gb=gb, c=c: t.matmul(
                            p_at[gb:gb + 64, :].rearrange(
                                "p (g n) -> p g n", n=64)[:, pg, :],
                            ctx_sb[gb:gb + 64, pg, :],
                            eq[gb:gb + 64, pg, c * 64:(c + 1) * 64],
                            start=True, stop=True),
                            waits=[(ve, max(t_snap, t_qprime[pg], t_atev))], inc=pe)
                t_atev = E("vector", lambda v, c=c: v.tensor_copy(
                    attn_fm[:, 2:8, c * 64:(c + 1) * 64],
                    p_at[:].rearrange("p (g n) -> p g n", n=64)),
                    waits=[(pe, atl), (pe, t_ctxmm)], inc=ve)
                if c < NCK - 1:
                    mm = None
                    for pg in range(6):
                        for rr in range(2):
                            g = 4 + pg * 2 + rr
                            i, half = c // 2, (c % 2) * 64
                            mm = E("tensor", lambda t, pg=pg, rr=rr, g=g, i=i,
                                   half=half: t.matmul(
                                p_ctx[rr * 64:rr * 64 + 64, :].rearrange(
                                    "p (g n) -> p g n", n=64)[:, pg, :],
                                kg_tm[half:half + 64, i,
                                      (g - 4) * 64:(g - 4) * 64 + 64],
                                v_tm[half:half + 64, i * 1024 + g * 64:
                                     i * 1024 + g * 64 + 64],
                                start=True, stop=True),
                                waits=[(ve, t_accr), (sc, t_vev)], inc=pe)
                    t_ctxmm = mm
                    cur, prv = ctx_acc[c % 2], ctx_acc[(c + 1) % 2]
                    t_accr = E("vector", lambda v, cur=cur, prv=prv: v.tensor_tensor(
                        cur[:], prv[:], p_ctx[:].rearrange("p (g n) -> p g n", n=64),
                        ALU.add), waits=[(pe, mm), (ve, t_snap)], inc=ve)
                    t_snap = E("vector", lambda v, cur=cur: v.tensor_copy(
                        ctx_sb[:], cur[:]), waits=[(ve, t_accr)], inc=ve)
            t_attn_done = t_atev

            t_psum_free = (ve, 0)
            t_psq_free = (ve, 0)
            t_rows_free = (ve, 0)
            t_v_cons_pe = (pe, 0)
            t_v_cons_sc = (sc, 0)
            t_st = 0
            t_xc = t_xcons2
            for tb in range(NQB):
                wrdy = load_xb(s, tb, t_xc)
                t_r = [0] * NCH
                for oc in range(NCH):
                    bi = oc % 2
                    mm = None
                    for kc in range(NCH):
                        mm = E("tensor", lambda t, oc=oc, kc=kc, tb=tb, bi=bi: t.matmul(
                            p_s[bi][:], wslot[1][:, kc, oc * 128:(oc + 1) * 128],
                            attn_fm[:, kc, tb * QB:(tb + 1) * QB],
                            start=(kc == 0), stop=(kc == NCH - 1)),
                            waits=[(dgw[1], t_w[1]), (ve, t_attn_done),
                                   (sc, t_bank[bi]), (ve, t_bank_ve[bi]),
                                   (ve, t_rblk_war(0))] if False else
                                  [(dgw[1], t_w[1]), (ve, t_attn_done),
                                   (sc, t_bank[bi]), (ve, t_bank_ve[bi])], inc=pe)
                    t_r[oc] = E("vector", lambda v, oc=oc, bi=bi: v.scalar_tensor_tensor(
                        r_blk[:, oc, :], p_s[bi][:], bo_sb[:, oc:oc + 1],
                        x2b[:, oc, :], ALU.add, ALU.add),
                        waits=[(pe, mm), (dgp, LDGP), t_v_cons_pe, t_v_cons_sc,
                               (dgs, t_st)] + wrdy,
                        inc=ve)
                    t_bank_ve[bi] = t_r[oc]
                    t_bank[bi] = 0
                t_xc = (ve, t_r[NCH - 1])
                mm1 = None
                for c in range(NCH):
                    mm1 = E("tensor", lambda t, c=c: t.matmul(
                        p_sum[:], cs["c_128x32"][:], r_blk[:, c, :],
                        start=(c == 0), stop=(c == NCH - 1)),
                        waits=[(ve, t_r[c]), t_psum_free], inc=pe)
                mm2 = 0
                tsq = 0
                for c in range(NCH):
                    tsq = E("scalar", lambda sce, c=c: sce.activation(
                        r2f[:], r_blk[:, c, :], AF.Square),
                        waits=[(ve, t_r[c]), (pe, mm2)], inc=sc)
                    mm2 = E("tensor", lambda t, c=c: t.matmul(
                        p_sq[:], cs["c_128x32"][:], r2f[:],
                        start=(c == 0), stop=(c == NCH - 1)),
                        waits=[(sc, tsq), t_psq_free], inc=pe)
                a1 = E("scalar", lambda sce: sce.mul(row_mu[:], p_sum[0:1, :], 1.0 / D),
                       waits=[(pe, mm1), t_rows_free], inc=sc)
                a2 = E("scalar", lambda sce: sce.activation(row_mu2[:], row_mu[:],
                                                            AF.Square),
                       waits=[(sc, a1)], inc=sc)
                a3 = E("vector", lambda v: v.scalar_tensor_tensor(
                    row_s2[:], p_sq[0:1, :], 1.0 / D, row_mu2[:], ALU.mult,
                    ALU.subtract), waits=[(pe, mm2), (sc, a2)], inc=ve)
                a4 = E("scalar", lambda sce: sce.activation(
                    row_mu2[:], row_s2[:], AF.Ln, bias=epsB[:], scale=1.0),
                    waits=[(ve, a3)], inc=sc)
                a5 = E("scalar", lambda sce: sce.activation(
                    row_rstd[:], row_mu2[:], AF.Exp, bias=0.0, scale=-0.5),
                    waits=[(sc, a4)], inc=sc)
                t_psum_free = (sc, a1)
                t_psq_free = (ve, a3)
                c3 = E("tensor", lambda t: t.matmul(
                    p_rs[:], cs["c_1x128f"][:], row_mu[:], start=True, stop=True),
                    waits=[(sc, a5), (ve, t_prs_free), (pe, t_rdm_cons)], inc=pe)
                c4 = E("tensor", lambda t: t.matmul(
                    p_big[:, 0:QB], cs["c_1x128f"][:], row_rstd[:],
                    start=True, stop=True),
                    waits=[(sc, a5), (ve, t_snap), (ve, t_c4war(0))] if False else
                          [(sc, a5), (ve, t_snap)], inc=pe)
                t_rows_free = (pe, c4)
                t_p4 = [0] * NCH
                for c in range(NCH):
                    t_p4[c] = E("vector", lambda v, c=c: v.tensor_tensor(
                        t_blk[:, c, :], r_blk[:, c, :], p_rs[:], ALU.subtract),
                        waits=[(ve, t_r[c]), (pe, max(c3, mm2))], inc=ve)
                t_prs_free = t_p4[NCH - 1]
                t_p5 = [0] * NCH
                for c in range(NCH):
                    t_p5[c] = E("vector", lambda v, c=c: v.tensor_tensor(
                        r_blk[:, c, :], t_blk[:, c, :], p_big[:, 0:QB], ALU.mult),
                        waits=[(ve, t_p4[c]), (pe, c4), (dgs, t_st)], inc=ve)
                t_snap = t_p5[NCH - 1]
                t_v_cons_pe = (pe, mm1)
                t_v_cons_sc = (sc, tsq)
                t_st = E("gpsimd", lambda g, tb=tb, s=s: g.dma_start(
                    out=xo[s][:, :, tb * QB:(tb + 1) * QB].rearrange("c p t -> p c t"),
                    in_=r_blk[:]), waits=[(ve, t_p5[NCH - 1])], inc=dgs, inc_by=16)
        pr.barrier_all(allsems)


def t_pb_cons_v(x):
    return x


def t_den_cons_s(x):
    return x


def t_rblk_war(x):
    return x


def t_c4war(x):
    return x


def prep_c(inputs, n_cores=8, xin_override=None):
    Wq = np.asarray(inputs["L1_attn_Wq"])
    Wk = np.asarray(inputs["L1_attn_Wk"])
    Wv = np.asarray(inputs["L1_attn_Wv"])
    Wo = np.asarray(inputs["L1_attn_Wo"])
    bo = np.asarray(inputs["L1_attn_bo"])
    shared = dict(
        wq=np.ascontiguousarray(Wq.reshape(NCH, 128, D)),
        wk=np.ascontiguousarray(Wk.reshape(NCH, 128, D)),
        wv=np.ascontiguousarray(Wv.reshape(NCH, 128, D)),
        wo=np.ascontiguousarray(Wo.reshape(NCH, 128, D)),
        bo=np.ascontiguousarray(bo.reshape(NCH, 128).T),
        **consts_c())
    in_maps = []
    for core in range(n_cores):
        m = dict(shared)
        m["xi"] = np.ascontiguousarray(xin_override[core])
        in_maps.append(m)
    return in_maps


IH = 2816
NNC = IH // 128

CONST_SHAPES_D = {
    "c_128x32": ([128, 32], BF16), "c_1x128f": ([1, 128], F32),
}


def consts_d():
    return {"c_128x32": np.ones((128, 32), dtype=BF),
            "c_1x128f": np.ones((1, 128), dtype=np.float32)}


def emit_d(nc, pr, sems, ctx, dram):
    T, NQB, NT = (globals()["T"], globals()["NQB"], globals()["NT"])
    xi = dram["xcd"]
    wg, wu, wd = dram["d_wg"], dram["d_wu"], dram["d_wd"]
    cin = {k: dram[k] for k in CONST_SHAPES_D}
    gT, uT, dT = dram["d_gT"], dram["d_uT"], dram["d_dT"]
    xout = dram["xout"]
    NBW = [512] * 5 + [256]
    NBO = [0, 512, 1024, 1536, 2048, 2560]
    if True:
        sb = lambda name, shape, dt=BF16: ctx.enter_context(nc.sbuf_tensor("d_" + name, shape, dt))
        ps = lambda name, shape, dt=F32: ctx.enter_context(nc.psum_tensor("d_" + name, shape, dt))
        ctx0 = ExitStack()
        sb0 = lambda name, shape, dt=BF16: ctx0.enter_context(nc.sbuf_tensor("d_" + name, shape, dt))
        qsets = []
        for sid in range(3):
            qsets.append({
                "f1": sb("q%df1" % sid, [128, D], F32),
                "f2": sb("q%df2" % sid, [128, D], F32),
                "f3": sb("q%df3" % sid, [128, D], F32),
                "scl": sb("q%dscl" % sid, [128, D // 16], F32),
                "rscl": sb("q%drscl" % sid, [128, D // 16], F32),
                "rln": sb("q%drln" % sid, [128, D // 16], F32),
                "sgb": sb("q%dsgb" % sid, [128, D])})
        qi = sb("qi", [128, D], mybir.dt.int32)
        qb1 = sb("qb1", [128, D])
        qb2 = sb("qb2", [128, D])
        m4c = sb("m4c", [128, 1], F32)
        m6c = sb("m6c", [128, 1], F32)
        p2c = sb("p2c", [128, 1], F32)
        p4c = sb("p4c", [128, 1], F32)
        phc = sb("phc", [128, 1], F32)
        cs = {k: sb(k + "_s", sh, dt) for k, (sh, dt) in CONST_SHAPES_D.items()}
        epsB = sb("epsB", [1, 1], F32)
        wraw = sb0("wraw", [128, 2, D], F32)
        wqb = sb0("wqb", [128, 2, D])
        wqT = sb0("wqT", [128, 2, D])

        p_g = ps("p_g", [128, QB])
        p_u = ps("p_u", [128, QB])
        p_d = [ps("p_d0", [128, QB]), ps("p_d1", [128, QB])]
        p_sum = ps("p_sum", [32, QB])
        p_sq = ps("p_sq", [32, QB])
        p_r1 = ps("p_r1", [128, QB])
        p_r2 = ps("p_r2", [128, QB])

        pe, sc, ve, dgp, dgs = (sems[k] for k in ("pe", "sc", "ve", "dgp", "dgs"))
        gs = sems["gs"]
        dgw = [sems["dgw0"], sems["dgw1"]]
        dgx = [sems["dgx0"], sems["dgx1"]]
        dst_ = [sems["ds0"], sems["ds1"]]
        dgss = [sems["dgs"], sems["dgs1"]]
        allsems = list(sems.values())
        E = pr.emit
        _dma_last = {}

        def dch(sem):
            t = _dma_last.get(id(sem))
            return [(sem, t)] if t else []

        def dset(sem, tok):
            _dma_last[id(sem)] = tok
            return tok

        for dst, src_ in [(cs[k], cin[k]) for k in CONST_SHAPES_D]:
            E("gpsimd", lambda g, d=dst, s_=src_: g.dma_start(out=d[:], in_=s_),
              inc=dgp, inc_by=16)
        LDGP = dgp.n
        E("vector", lambda v: v.memset(epsB[:], 1e-5), inc=ve)
        E("vector", lambda v: v.memset(m4c[:], -4.0), inc=ve)
        E("vector", lambda v: v.memset(m6c[:], -6.0), inc=ve)
        E("vector", lambda v: v.memset(p2c[:], 2.0), inc=ve)
        E("vector", lambda v: v.memset(p4c[:], 4.0), inc=ve)
        t_cst = (ve, E("vector", lambda v: v.memset(phc[:], 0.5 - 2.0 ** -16),
                       inc=ve))

        qshared = {"qi": (sc, 0), "qb": (gs, 0)}
        for S in qsets:
            S.update({"f1_free": (gs, 0), "f2_free": (ve, 0), "f3_free": (ve, 0),
                      "scl_free": (gs, 0), "rscl_free": (gs, 0),
                      "rln_free": (sc, 0), "sgb_free": (gs, 0)})
        qp = {"p1": None, "p2": None, "n": 0}

        def _q_fha(inv):
            S = inv["S"]; W = inv["W"]; NB16 = W // 16
            f2, f3 = S["f2"], S["f3"]
            t0 = E("scalar", lambda c, src=inv["src"], f2=f2, W=W: c.activation(
                f2[:, 0:W], src, AF.Abs), waits=inv["wsrc"] + [S["f2_free"]], inc=sc)
            sgn = E("scalar", lambda c, src=inv["src"], sg_=S["sgb"], W=W:
                    c.activation(sg_[:, 0:W], src, AF.Sign),
                    waits=[S["sgb_free"]], inc=sc)
            cur_off, cur_n = 0, W
            tprev = (sc, t0)
            lev = 0
            while cur_n > NB16:
                nxt = cur_n // 2
                src_lo = (f2 if lev == 0 else f3)
                so = 0 if lev == 0 else cur_off
                do = 0 if lev % 2 == 0 else W // 2
                w = [tprev] + ([S["f3_free"]] if lev == 0 else [])
                tt = E("vector", lambda v, so=so, do=do, nxt=nxt, src_lo=src_lo,
                       f3=f3: v.tensor_tensor(
                    f3[:, do:do + nxt],
                    src_lo[:, so:so + 2 * nxt].rearrange(
                        "p (n two) -> p n two", two=2)[:, :, 0],
                    src_lo[:, so:so + 2 * nxt].rearrange(
                        "p (n two) -> p n two", two=2)[:, :, 1], ALU.max),
                    waits=w, inc=ve)
                tprev = (ve, tt)
                cur_off, cur_n = do, nxt
                lev += 1
            tsc = E("vector", lambda v, cur_off=cur_off, NB16=NB16, f3=f3,
                    sl_=S["scl"]: v.tensor_scalar(
                sl_[:, 0:NB16], f3[:, cur_off:cur_off + NB16],
                1.0 / 6.0, 1e-12, ALU.mult, ALU.max),
                waits=[tprev, S["scl_free"]], inc=ve)
            lnr = E("scalar", lambda c, NB16=NB16, sl_=S["scl"], rl_=S["rln"]:
                    c.activation(rl_[:, 0:NB16], sl_[:, 0:NB16], AF.Ln),
                    waits=[(ve, tsc), S["rln_free"]], inc=sc)
            exr = E("scalar", lambda c, NB16=NB16, rl_=S["rln"], rs_=S["rscl"]:
                    c.activation(rs_[:, 0:NB16], rl_[:, 0:NB16], AF.Exp,
                                 bias=0.0, scale=-1.0),
                    waits=[(sc, lnr), S["rscl_free"]], inc=sc)
            xnv = E("gpsimd", lambda g, W=W, NB16=NB16, f2=f2, f1=S["f1"],
                    rs_=S["rscl"]: g.tensor_tensor(
                f1[:, 0:W].rearrange("p (n b) -> p n b", b=16),
                f2[:, 0:W].rearrange("p (n b) -> p n b", b=16),
                rs_[:, 0:NB16].broadcast_to([128, NB16, 16]), ALU.mult),
                waits=[(sc, exr), S["f1_free"]], inc=gs)
            S["rscl_free"] = (gs, xnv)
            S["rln_free"] = (sc, exr)
            inv["tsc"], inv["xnv"] = tsc, xnv
            inv["src_done"] = (sc, sgn)

        def _q_fhb(inv):
            S, W = inv["S"], inv["W"]
            f1, f2, f3 = S["f1"], S["f2"], S["f3"]
            m1 = E("scalar", lambda c, W=W, f1=f1, f3=f3: c.activation(
                f3[:, 0:W], f1[:, 0:W], AF.Relu, bias=p2c[:], scale=1.0),
                waits=[(gs, inv["xnv"]), (ve, inv["tsc"]), t_cst], inc=sc)
            u1 = E("vector", lambda v, W=W, f1=f1, f2=f2, f3=f3:
                   v.scalar_tensor_tensor(
                f2[:, 0:W], f1[:, 0:W], 2.0, f3[:, 0:W], ALU.mult, ALU.min),
                waits=[(sc, m1)], inc=ve)
            inv["u1"] = u1

        def _q_sh1(inv):
            S, W = inv["S"], inv["W"]
            f1, f2, f3 = S["f1"], S["f2"], S["f3"]
            m2 = E("vector", lambda v, W=W, f1=f1, f3=f3: v.tensor_scalar(
                f3[:, 0:W], f1[:, 0:W], 0.5, 4.0, ALU.mult, ALU.add),
                waits=[(ve, inv["u1"])], inc=ve)
            u2 = E("vector", lambda v, W=W, f1=f1, f2=f2, f3=f3: v.tensor_tensor(
                f1[:, 0:W], f2[:, 0:W], f3[:, 0:W], ALU.min),
                waits=[(ve, m2)], inc=ve)
            tadd = E("vector", lambda v, W=W, f1=f1, f2=f2: v.tensor_scalar(
                f2[:, 0:W], f1[:, 0:W], 1.0, 0.5 - 2.0 ** -16, ALU.mult, ALU.add),
                waits=[(ve, u2)], inc=ve)
            ic = E("scalar", lambda c, W=W, f2=f2: c.activation(
                qi[:, 0:W], f2[:, 0:W], AF.Copy),
                waits=[(ve, tadd), qshared["qi"]], inc=sc)
            fi = E("scalar", lambda c, W=W, f3=f3: c.activation(
                f3[:, 0:W], qi[:, 0:W], AF.Copy), waits=[(sc, ic)], inc=sc)
            qshared["qi"] = (sc, fi)
            gt = E("vector", lambda v, W=W, f1=f1, f2=f2, f3=f3: v.tensor_tensor(
                f1[:, 0:W], f3[:, 0:W], f2[:, 0:W], ALU.is_gt),
                waits=[(sc, fi)], inc=ve)
            isub = E("vector", lambda v, W=W, f1=f1, f3=f3: v.tensor_tensor(
                f3[:, 0:W], f3[:, 0:W], f1[:, 0:W], ALU.subtract),
                waits=[(ve, gt)], inc=ve)
            inv["isub"] = isub

        def _q_sh2(inv):
            S, W = inv["S"], inv["W"]
            NB16 = W // 16
            f1, f2, f3 = S["f1"], S["f2"], S["f3"]
            r1 = E("scalar", lambda c, W=W, f1=f1, f3=f3: c.activation(
                f1[:, 0:W], f3[:, 0:W], AF.Relu, bias=m4c[:], scale=1.0),
                waits=[(ve, inv["isub"]), t_cst], inc=sc)
            r2 = E("scalar", lambda c, W=W, f2=f2, f3=f3: c.activation(
                f2[:, 0:W], f3[:, 0:W], AF.Relu, bias=m6c[:], scale=1.0),
                waits=[(sc, r1)], inc=sc)
            s1 = E("vector", lambda v, W=W, f1=f1, f2=f2: v.scalar_tensor_tensor(
                qb1[:, 0:W], f1[:, 0:W], 0.5, f2[:, 0:W], ALU.mult, ALU.add),
                waits=[(sc, r2), qshared["qb"]], inc=ve)
            s2 = E("vector", lambda v, W=W, f3=f3: v.scalar_tensor_tensor(
                qb2[:, 0:W], f3[:, 0:W], 0.5, qb1[:, 0:W], ALU.mult, ALU.add),
                waits=[(ve, s1)], inc=ve)
            t2 = E("gpsimd", lambda g, W=W, f1=f1, sg_=S["sgb"]: g.tensor_tensor(
                f1[:, 0:W], qb2[:, 0:W], sg_[:, 0:W], ALU.mult),
                waits=[(ve, s2)], inc=gs)
            qshared["qb"] = (gs, t2)
            qv = E("gpsimd", lambda g, W=W, NB16=NB16, dst=inv["dst"], f1=f1,
                   sl_=S["scl"]: g.tensor_tensor(
                dst.rearrange("p (n b) -> p n b", b=16),
                f1[:, 0:W].rearrange("p (n b) -> p n b", b=16),
                sl_[:, 0:NB16].broadcast_to([128, NB16, 16]), ALU.mult),
                waits=[(gs, t2)] + inv["wdst_fn"](), inc=gs)
            S["f1_free"] = (gs, qv)
            S["f2_free"] = (ve, s1)
            S["f3_free"] = (ve, s2)
            S["scl_free"] = (gs, qv)
            S["sgb_free"] = (gs, t2)
            inv["done"] = (gs, qv)

        def q_feed(src_ap, dst_ap, F, wsrc, wdst_fn, meta=None):
            inv = {"S": qsets[qp["n"] % 3], "src": src_ap, "dst": dst_ap, "W": F,
                   "wsrc": list(wsrc), "wdst_fn": wdst_fn, "meta": meta}
            qp["n"] += 1
            done = qp["p2"]
            if done is not None:
                _q_sh2(done)
            _q_fha(inv)
            if qp["p1"] is not None:
                _q_sh1(qp["p1"])
            _q_fhb(inv)
            qp["p2"] = qp["p1"]
            qp["p1"] = inv
            return done

        def q_flush(cb):
            if qp["p1"] is not None:
                _q_sh1(qp["p1"])
            if qp["p2"] is not None:
                _q_sh2(qp["p2"])
                cb(qp["p2"])
            if qp["p1"] is not None:
                _q_sh2(qp["p1"])
                cb(qp["p1"])
            qp["p1"] = qp["p2"] = None

        t_qbf_cons = [0, 0]
        t_trd_cons = [0, 0]
        winv = [None, None]
        chunks = []
        for ci in range(NNC):
            chunks.append(("g", wg[ci], gT, ci, D, 0))
        for ci in range(NNC):
            chunks.append(("u", wu[ci], uT, ci, D, 0))
        for ci in range(NCH):
            for seg in range(3):
                W = 1024 if seg < 2 else 768
                chunks.append(("d", wd[ci], dT, ci, W, seg * 1024))

        def emit_wtr(pi):
            kind, dstT, ci, W, soff, slw = pi["meta"]
            NKC = W // 128
            tt = dset(dst_[slw], E("sync", lambda sy, slw=slw, W=W, NKC=NKC:
                                    sy.dma_start(
                out=wqT[:, slw, 0:W].rearrange("p (c f) -> p c f", c=NKC),
                in_=wqb[:, slw, 0:W].rearrange("p (c f) -> p c f", c=NKC),
                transpose=True),
                waits=[pi["done"], (dgss[slw], t_trd_cons[slw])] + dch(dst_[slw]),
                inc=dst_[slw], inc_by=16))
            t_qbf_cons[slw] = tt
            if kind in ("g", "u"):
                st = dset(dgss[slw], E("gpsimd", lambda g, dstT=dstT, ci=ci,
                                       slw=slw: g.dma_start(
                    out=dstT[:, :, ci * 128:(ci + 1) * 128].rearrange(
                        "c k n -> k c n"),
                    in_=wqT[:, slw, :].rearrange("p (c f) -> p c f", c=NCH)),
                    waits=[(dst_[slw], tt)] + dch(dgss[slw]),
                    inc=dgss[slw], inc_by=16))
            else:
                st = dset(dgss[slw], E("gpsimd", lambda g, ci=ci, slw=slw,
                                       soff=soff, NKC=NKC: g.dma_start(
                    out=dT[soff // 128:soff // 128 + NKC, :,
                           ci * 128:(ci + 1) * 128].rearrange("c k n -> k c n"),
                    in_=wqT[:, slw, 0:NKC * 128].rearrange(
                        "p (c f) -> p c f", c=NKC)),
                    waits=[(dst_[slw], tt)] + dch(dgss[slw]),
                    inc=dgss[slw], inc_by=16))
            t_trd_cons[slw] = st

        for n, (kind, src_ap, dstT, ci, W, soff) in enumerate(chunks):
            slw = n % 2
            wld_waits = [winv[slw]["src_done"]] if winv[slw] else []
            tl = dset(dgw[slw], E("gpsimd", lambda g, src_ap=src_ap, slw=slw,
                                  W=W, soff=soff:
                   g.dma_start(out=wraw[:, slw, 0:W], in_=src_ap[:, soff:soff + W]),
                   waits=wld_waits + dch(dgw[slw]), inc=dgw[slw], inc_by=16))
            prev = q_feed(wraw[:, slw, 0:W], wqb[:, slw, 0:W], W,
                          [(dgw[slw], tl)],
                          (lambda slw=slw: [(dst_[slw], t_qbf_cons[slw])]),
                          meta=(kind, dstT, ci, W, soff, slw))
            winv[slw] = qp["p1"]
            if prev is not None:
                emit_wtr(prev)
        q_flush(emit_wtr)
        WSD0, WSD1 = dgss[0].n, dgss[1].n
        ctx0.close()
        x3q = sb("x3q", [128, NCH, T])
        xtm = sb("xtm", [128, 2, D])
        xqtm = sb("xqtm", [128, 1, D])
        wg_nb = sb("wg_nb", [128, 2, NCH, QB])
        wu_nb = sb("wu_nb", [128, 2, NCH, QB])
        wd_nb = sb("wd_nb", [128, 1, 4, D])
        h_tm = sb("h_tm", [128, 4 * QB])
        hq_tm = sb("hq_tm", [128, 4 * QB])
        hqf = sb("hq_fm", [128, max(4 * T, 2 * NCH * QB)])
        hq_fm = hqf[:, 0:4 * T].rearrange("p (c t) -> p c t", c=4)
        v_blk = hqf[:, 0:NCH * QB].rearrange("p (c t) -> p c t", c=NCH)
        t_blk = hqf[:, NCH * QB:2 * NCH * QB].rearrange("p (c t) -> p c t", c=NCH)
        sg_sb = sb("sg_sb", [128, 2, QB])
        facc = sb("facc", [128, NCH, T])
        xfmt = sb("xfmt", [128, 1, D])
        r2f = h_tm[:, 0:QB]
        row_mu = sb("row_mu", [1, QB], F32)
        row_mu2 = sb("row_mu2", [1, QB], F32)
        row_s2 = sb("row_s2", [1, QB], F32)
        row_rstd = sb("row_rstd", [1, QB], F32)
        x3b = wg_nb[:, 0]

        xinv = [None, None]
        hinv = [None, None]
        t_hqc = [[], []]
        t_trx = [0, 0]
        for s in range(NSEQ):
            pr.barrier_all(allsems)
            t_xq_cons = [0, 0]
            t_xfmt_last = (dst_[0], 0)

            def emit_xtr(pi):
                i = pi["meta"]
                sl = i % 2
                tt = dset(dst_[sl], E("sync", lambda sy, i=i: sy.dma_start(
                    out=x3q[:, :, i * 128:(i + 1) * 128],
                    in_=xqtm[:, 0, :].rearrange("p (c f) -> p c f", c=NCH),
                    transpose=True),
                    waits=[pi["done"]] + dch(dst_[sl]),
                    inc=dst_[sl], inc_by=16))
                t_xq_cons[sl] = tt
                t_trx[sl] = tt

            for i in range(NT):
                sl = i % 2
                tld = dset(dgx[sl], E("gpsimd", lambda g, i=i, s=s: g.dma_start(
                    out=xfmt[:, 0, :].rearrange("p (c f) -> p c f", c=NCH),
                    in_=xi[s][:, :, i * 128:(i + 1) * 128].rearrange(
                        "c p f -> p c f")),
                    waits=[t_xfmt_last] + dch(dgx[sl]), inc=dgx[sl], inc_by=16))
                xw = [xinv[sl]["src_done"]] if xinv[sl] else []
                tl = dset(dst_[sl], E("sync", lambda sy, i=i, sl=sl: sy.dma_start(
                    out=xtm[:, sl, :].rearrange("p (c f) -> p c f", c=NCH),
                    in_=xfmt[:, 0, :].rearrange("p (c f) -> p c f", c=NCH),
                    transpose=True),
                    waits=xw + [(dgx[sl], tld)] + dch(dst_[sl]),
                    inc=dst_[sl], inc_by=16))
                t_xfmt_last = (dst_[sl], tl)
                prev = q_feed(xtm[:, sl, :], xqtm[:, 0, :], D, [(dst_[sl], tl)],
                              (lambda: [(dst_[0], t_xq_cons[0]),
                                        (dst_[1], t_xq_cons[1])]), meta=i)
                xinv[sl] = qp["p1"]
                if prev is not None:
                    emit_xtr(prev)
            q_flush(emit_xtr)
            XQRDY = [(dst_[0], t_trx[0]), (dst_[1], t_trx[1])]

            t_wnb = [0, 0]
            t_wnb_cons = [0, 0]
            t_facc_cons = (ve, 0)
            t_hfm_cons = 0
            t_wd_cons = 0
            for nb in range(6):
                W = NBW[nb]
                NKC = W // 128
                slw = nb % 2
                ld1 = dset(dgw[slw], E("gpsimd", lambda g, nb=nb, slw=slw, W=W:
                                       g.dma_start(
                    out=wg_nb[:, slw, :, 0:W], in_=gT[:, :, NBO[nb]:NBO[nb] + W]
                    .rearrange("c k n -> k c n")),
                    waits=[(pe, t_wnb_cons[slw]), (dgss[0], WSD0), (dgss[1], WSD1)]
                    + dch(dgw[slw]), inc=dgw[slw], inc_by=16))
                ld2 = dset(dgw[slw], E("gpsimd", lambda g, nb=nb, slw=slw, W=W:
                                       g.dma_start(
                    out=wu_nb[:, slw, :, 0:W], in_=uT[:, :, NBO[nb]:NBO[nb] + W]
                    .rearrange("c k n -> k c n")), waits=dch(dgw[slw]),
                    inc=dgw[slw], inc_by=16))
                ld3 = dset(dgw[slw], E("gpsimd", lambda g, nb=nb, slw=slw, NKC=NKC:
                                       g.dma_start(
                    out=wd_nb[:, 0, 0:NKC, :],
                    in_=dT[NBO[nb] // 128:NBO[nb] // 128 + NKC].rearrange(
                        "c k n -> k c n")),
                    waits=[(pe, t_wd_cons)] + dch(dgw[slw]),
                    inc=dgw[slw], inc_by=16))
                t_wnb[slw] = ld3
                t_hq_war = (pe, t_hfm_cons)
                t_pg_free = (sc, 0)
                t_pu_free = (ve, 0)
                t_sg_free = [(ve, 0), (ve, 0)]

                def emit_htr(pi):
                    p_, W_, NKC_, hb_ = pi["meta"]
                    tts = []
                    for j in (0, 1):
                        i = 2 * p_ + j
                        tts.append(dset(dst_[j], E("sync", lambda sy, i=i, j=j,
                                        W_=W_, NKC_=NKC_, hb_=hb_: sy.dma_start(
                            out=hq_fm[:, 0:NKC_, i * 128:(i + 1) * 128],
                            in_=hq_tm[:, hb_ + j * W_:hb_ + (j + 1) * W_]
                            .rearrange("p (c f) -> p c f", c=NKC_),
                            transpose=True),
                            waits=[pi["done"], pi["hq_war"]] + dch(dst_[j]),
                            inc=dst_[j], inc_by=16)))
                    t_hqc[p_ % 2] = [(dst_[0], tts[0]), (dst_[1], tts[1])]
                    t_trx[0], t_trx[1] = tts[0], tts[1]

                for p in range(NT // 2):
                    par = p % 2
                    hb = par * 2 * QB
                    ths = []
                    for j in (0, 1):
                        i = 2 * p + j
                        mmg = mmu = None
                        for kc in range(NCH):
                            mmg = E("tensor", lambda t, i=i, kc=kc, slw=slw, W=W:
                                    t.matmul(
                                p_g[:, 0:W], x3q[:, kc, i * 128:(i + 1) * 128],
                                wg_nb[:, slw, kc, 0:W], start=(kc == 0),
                                stop=(kc == NCH - 1)),
                                waits=XQRDY + [(dgw[slw], t_wnb[slw]), t_pg_free,
                                               t_pu_free], inc=pe)
                            mmu = E("tensor", lambda t, i=i, kc=kc, slw=slw, W=W:
                                    t.matmul(
                                p_u[:, 0:W], x3q[:, kc, i * 128:(i + 1) * 128],
                                wu_nb[:, slw, kc, 0:W], start=(kc == 0),
                                stop=(kc == NCH - 1)), inc=pe)
                        tsg = E("scalar", lambda c, j=j, W=W: c.activation(
                            sg_sb[:, j, 0:W], p_g[:, 0:W], AF.Silu),
                            waits=[(pe, mmu), t_sg_free[j]], inc=sc)
                        t_pg_free = (sc, tsg)
                        hw = [hinv[par]["src_done"]] if hinv[par] else []
                        th = E("vector", lambda v, j=j, W=W, hb=hb: v.tensor_tensor(
                            h_tm[:, hb + j * W:hb + (j + 1) * W], sg_sb[:, j, 0:W],
                            p_u[:, 0:W], ALU.mult),
                            waits=[(sc, tsg)] + hw, inc=ve)
                        t_pu_free = (ve, th)
                        t_sg_free[j] = (ve, th)
                        ths.append(th)
                    prev = q_feed(h_tm[:, hb:hb + 2 * W], hq_tm[:, hb:hb + 2 * W],
                                  2 * W, [(ve, ths[0]), (ve, ths[1])],
                                  (lambda par=par: list(t_hqc[par])),
                                  meta=(p, W, NKC, hb))
                    qp["p1"]["hq_war"] = t_hq_war
                    hinv[par] = qp["p1"]
                    if prev is not None:
                        emit_htr(prev)
                q_flush(emit_htr)
                HQRDY = [(dst_[0], t_trx[0]), (dst_[1], t_trx[1])]
                first_nb = (nb == 0)
                for oc in range(NCH):
                    bi = oc % 2
                    for tb in range(NQB):
                        mm = None
                        for kk in range(NKC):
                            mm = E("tensor", lambda t, oc=oc, tb=tb, kk=kk, slw=slw,
                                   bi=bi, NKC=NKC: t.matmul(
                                p_d[bi][:], wd_nb[:, 0, kk, oc * 128:(oc + 1) * 128],
                                hq_fm[:, kk, tb * QB:(tb + 1) * QB],
                                start=(kk == 0), stop=(kk == NKC - 1)),
                                waits=HQRDY + [(dgw[slw], t_wnb[slw]),
                                               t_facc_cons], inc=pe)
                        if first_nb:
                            tf = E("vector", lambda v, oc=oc, tb=tb, bi=bi: v.tensor_scalar(
                                facc[:, oc, tb * QB:(tb + 1) * QB], p_d[bi][:],
                                1.0, None, ALU.mult),
                                waits=[(pe, mm)], inc=ve)
                        else:
                            tf = E("vector", lambda v, oc=oc, tb=tb, bi=bi:
                                   v.scalar_tensor_tensor(
                                facc[:, oc, tb * QB:(tb + 1) * QB], p_d[bi][:], 1.0,
                                facc[:, oc, tb * QB:(tb + 1) * QB], ALU.mult, ALU.add),
                                waits=[(pe, mm)], inc=ve)
                        t_facc_cons = (ve, tf)
                t_hfm_cons = mm
                t_wnb_cons[slw] = mm
                t_wd_cons = mm

            t_psum_free = (ve, 0)
            t_psq_free = (ve, 0)
            t_rows_free = (ve, 0)
            t_x3b_cons = (ve, 0)
            t_vcons_pe = (pe, 0)
            t_vcons_sc = (sc, 0)
            t_xtm2_cons = [0, 0]
            t_stf = 0
            for tb in range(NQB):
                tl = dset(dgx[tb % 2], E("gpsimd", lambda g, tb=tb, s=s:
                                         g.dma_start(
                    out=x3b[:], in_=xi[s][:, :, tb * QB:(tb + 1) * QB].rearrange(
                        "c p t -> p c t")),
                    waits=[t_x3b_cons] + dch(dgx[tb % 2]),
                    inc=dgx[tb % 2], inc_by=16))
                t_v = [0] * NCH
                for c in range(NCH):
                    t_v[c] = E("vector", lambda v, c=c, tb=tb: v.tensor_tensor(
                        v_blk[:, c, :], x3b[:, c, :],
                        facc[:, c, tb * QB:(tb + 1) * QB], ALU.add),
                        waits=[(dgx[tb % 2], tl), t_facc_cons, t_vcons_pe, t_vcons_sc,
                               (dgss[0], t_xtm2_cons[0]), (dgss[1], t_xtm2_cons[1])],
                        inc=ve)
                t_x3b_cons = (ve, t_v[NCH - 1])
                mm1 = None
                for c in range(NCH):
                    mm1 = E("tensor", lambda t, c=c: t.matmul(
                        p_sum[:], cs["c_128x32"][:], v_blk[:, c, :],
                        start=(c == 0), stop=(c == NCH - 1)),
                        waits=[(ve, t_v[c]), t_psum_free], inc=pe)
                mm2 = 0
                tsq = 0
                for c in range(NCH):
                    tsq = E("scalar", lambda sce, c=c: sce.activation(
                        r2f[:], v_blk[:, c, :], AF.Square),
                        waits=[(ve, t_v[c]), (pe, mm2),
                               hinv[0]["src_done"]], inc=sc)
                    mm2 = E("tensor", lambda t, c=c: t.matmul(
                        p_sq[:], cs["c_128x32"][:], r2f[:],
                        start=(c == 0), stop=(c == NCH - 1)),
                        waits=[(sc, tsq), t_psq_free], inc=pe)
                a1 = E("scalar", lambda sce: sce.mul(row_mu[:], p_sum[0:1, :], 1.0 / D),
                       waits=[(pe, mm1), t_rows_free], inc=sc)
                a2 = E("scalar", lambda sce: sce.activation(row_mu2[:], row_mu[:],
                                                            AF.Square),
                       waits=[(sc, a1)], inc=sc)
                a3 = E("vector", lambda v: v.scalar_tensor_tensor(
                    row_s2[:], p_sq[0:1, :], 1.0 / D, row_mu2[:], ALU.mult,
                    ALU.subtract), waits=[(pe, mm2), (sc, a2)], inc=ve)
                a4 = E("scalar", lambda sce: sce.activation(
                    row_mu2[:], row_s2[:], AF.Ln, bias=epsB[:], scale=1.0),
                    waits=[(ve, a3)], inc=sc)
                a5 = E("scalar", lambda sce: sce.activation(
                    row_rstd[:], row_mu2[:], AF.Exp, bias=0.0, scale=-0.5),
                    waits=[(sc, a4)], inc=sc)
                t_psum_free = (sc, a1)
                t_psq_free = (ve, a3)
                c3 = E("tensor", lambda t: t.matmul(
                    p_r1[:], cs["c_1x128f"][:], row_mu[:], start=True, stop=True),
                    waits=[(sc, a5), (ve, t_r1_cons(0))] if False else [(sc, a5)],
                    inc=pe)
                c4 = E("tensor", lambda t: t.matmul(
                    p_r2[:], cs["c_1x128f"][:], row_rstd[:], start=True, stop=True),
                    waits=[(sc, a5)], inc=pe)
                t_rows_free = (pe, c4)
                t_p4 = [0] * NCH
                for c in range(NCH):
                    t_p4[c] = E("vector", lambda v, c=c: v.tensor_tensor(
                        t_blk[:, c, :], v_blk[:, c, :], p_r1[:], ALU.subtract),
                        waits=[(ve, t_v[c]), (pe, max(c3, mm2))], inc=ve)
                t_p5 = [0] * NCH
                for c in range(NCH):
                    t_p5[c] = E("vector", lambda v, c=c: v.tensor_tensor(
                        v_blk[:, c, :], t_blk[:, c, :], p_r2[:], ALU.mult),
                        waits=[(ve, t_p4[c]), (pe, c4)], inc=ve)
                t_vcons_pe = (pe, mm1)
                t_vcons_sc = (sc, tsq)
                for j in range(4):
                    i = tb * 4 + j
                    sl = j % 2
                    ttr = None
                    for c in range(NCH):
                        ttr = dset(dst_[sl], E("sync", lambda sy, j=j, sl=sl,
                                               c=c: sy.dma_start(
                            out=xtm[:, sl, c * 128:(c + 1) * 128],
                            in_=v_blk[:, c, j * 128:(j + 1) * 128],
                            transpose=True),
                            waits=[(ve, t_p5[NCH - 1]), (dgss[sl], t_xtm2_cons[sl]),
                                   xinv[sl]["src_done"]] + dch(dst_[sl]),
                            inc=dst_[sl], inc_by=16))
                    t_trx[sl] = ttr
                    t_xtm2_cons[sl] = dset(dgss[sl], E("gpsimd", lambda g, i=i,
                                           s=s, sl=sl: g.dma_start(
                        out=xout[s, i * 128:(i + 1) * 128, :], in_=xtm[:, sl, :]),
                        waits=[(dst_[sl], ttr)] + dch(dgss[sl]),
                        inc=dgss[sl], inc_by=16))
        pr.barrier_all(allsems)
        E("gpsimd", None, waits=[(dgss[0], dgss[0].n), (dgss[1], dgss[1].n)])


def ds0_v(x, d):
    return x


def t_pg_cons(x):
    return x


def t_sg_cons(x):
    return x


def t_hqtr(a, b):
    return 0


def t_r1_cons(x):
    return x


def prep_d(inputs, n_cores=8, xin_override=None):
    w_gate = np.asarray(inputs["L1_ffn_w_gate"])
    w_up = np.asarray(inputs["L1_ffn_w_up"])
    w_down = np.asarray(inputs["L1_ffn_w_down"])
    shared = dict(
        wg=np.ascontiguousarray(w_gate.reshape(NNC, 128, D)),
        wu=np.ascontiguousarray(w_up.reshape(NNC, 128, D)),
        wd=np.ascontiguousarray(w_down.reshape(NCH, 128, IH)),
        **consts_d())
    in_maps = []
    for core in range(n_cores):
        m = dict(shared)
        m["xi"] = np.ascontiguousarray(xin_override[core])
        in_maps.append(m)
    return in_maps


ALL_SEMS = ("pe", "sc", "ve", "gs", "dgp", "dgw0", "dgw1", "dgx", "dgx0", "dgx1",
            "ds0", "ds1", "dgs", "dgs1")


def build_all():
    nc = bass.Bass("TRN2", target_bir_lowering=False, debug=False)
    T = globals()["T"]
    dram = {}

    def din(name, shape, dt):
        dram[name] = nc.dram_tensor(name, shape, dt, kind="ExternalInput").ap()

    def dint(name, shape, dt):
        dram[name] = nc.dram_tensor(name, shape, dt, kind="Internal").ap()

    din("a_xin", [NSEQ, T, D], F32)
    for n in ("a_wq", "a_wk", "a_wv"):
        din(n, [NCH, 128, D], F32)
    din("a_bq", [128, NCH], F32)
    din("a_bk", [128, NCH], F32)
    din("a_bv", [1, D], F32)
    din("a_gcol", [128, NCH], F32)
    din("a_gbrow", [2, D], F32)
    HS = 2048
    din("b_sw1", [NCH, 128, HS], F32)
    din("b_sw3", [NCH, 128, HS], F32)
    din("b_sw2", [HS // 128, 128, D], F32)
    din("b_w1", [8, NCH, 128, D], F32)
    din("b_w2", [8, NCH, 128, D], F32)
    din("b_sb1c", [128, HS // 128], F32)
    din("b_sb3c", [128, HS // 128], F32)
    din("b_sb2c", [128, NCH], F32)
    din("b_b1c", [8, 128, NCH], F32)
    din("b_b2r", [8, 1, D], F32)
    din("b_temb", [4, D], F32)
    for n in ("c_wq", "c_wk", "c_wv", "c_wo"):
        din(n, [NCH, 128, D], F32)
    din("c_bo", [128, NCH], F32)
    din("d_wg", [NNC, 128, D], F32)
    din("d_wu", [NNC, 128, D], F32)
    din("d_wd", [NCH, 128, IH], F32)
    seen = {}
    for shapes in (CONST_SHAPES_A, CONST_SHAPES_B, CONST_SHAPES_C, CONST_SHAPES_D):
        for k, (sh, dt) in shapes.items():
            if k not in seen:
                seen[k] = True
                din(k, sh, dt)
    dint("xab", [NSEQ, NCH, 128, T], BF16)
    dint("xbc", [NSEQ, NCH, 128, T], BF16)
    dint("xcd", [NSEQ, NCH, 128, T], BF16)
    dint("d_gT", [NCH, 128, IH], BF16)
    dint("d_uT", [NCH, 128, IH], BF16)
    dint("d_dT", [NNC, 128, D], BF16)
    dram["xout"] = nc.dram_tensor("xout", [NSEQ, T, D], F32,
                                  kind="ExternalOutput").ap()

    with ExitStack() as octx:
        sems = {n: Sem(octx.enter_context(nc.semaphore(n))) for n in ALL_SEMS}
        pr = Prog()
        with ExitStack() as c1:
            emit_a(nc, pr, sems, c1, dram)
        with ExitStack() as c2:
            emit_b(nc, pr, sems, c2, dram)
        with ExitStack() as c3:
            emit_c(nc, pr, sems, c3, dram)
        with ExitStack() as c4:
            emit_d(nc, pr, sems, c4, dram)
        with nc.Block() as block:
            pr.build(block)
    return nc


def const_union():
    c = {}
    c.update(consts_a())
    c.update(consts_b())
    c.update(consts_c())
    c.update(consts_d())
    return c


def prep_all(inputs, n_cores=8):
    x = np.asarray(inputs["L0_attn_x"])
    shared = {}
    shared["a_wq"] = np.ascontiguousarray(np.asarray(inputs["L0_attn_Wq"]).reshape(NCH, 128, D))
    shared["a_wk"] = np.ascontiguousarray(np.asarray(inputs["L0_attn_Wk"]).reshape(NCH, 128, D))
    shared["a_wv"] = np.ascontiguousarray(np.asarray(inputs["L0_attn_Wv"]).reshape(NCH, 128, D))
    shared["a_bq"] = np.ascontiguousarray(np.asarray(inputs["L0_attn_bq"]).reshape(NCH, 128).T)
    shared["a_bk"] = np.ascontiguousarray(np.asarray(inputs["L0_attn_bk"]).reshape(NCH, 128).T)
    shared["a_bv"] = np.asarray(inputs["L0_attn_bv"]).reshape(1, D)
    shared["a_gcol"] = np.ascontiguousarray(np.asarray(inputs["L0_attn_gamma"]).reshape(NCH, 128).T)
    shared["a_gbrow"] = np.stack([np.asarray(inputs["L0_attn_gamma"]),
                                  np.asarray(inputs["L0_attn_beta"])], axis=0)
    HS = 2048
    shared["b_sw1"] = np.ascontiguousarray(np.asarray(inputs["L0_ffn_sw1"]).reshape(NCH, 128, HS))
    shared["b_sw3"] = np.ascontiguousarray(np.asarray(inputs["L0_ffn_sw3"]).reshape(NCH, 128, HS))
    shared["b_sw2"] = np.ascontiguousarray(np.asarray(inputs["L0_ffn_sw2"]).reshape(HS // 128, 128, D))
    shared["b_w1"] = np.ascontiguousarray(np.asarray(inputs["L0_ffn_w1"]).reshape(8, NCH, 128, D))
    shared["b_w2"] = np.ascontiguousarray(np.asarray(inputs["L0_ffn_w2"]).reshape(8, NCH, 128, D))
    shared["b_sb1c"] = np.ascontiguousarray(np.asarray(inputs["L0_ffn_sb1"]).reshape(HS // 128, 128).T)
    shared["b_sb3c"] = np.ascontiguousarray(np.asarray(inputs["L0_ffn_sb3"]).reshape(HS // 128, 128).T)
    shared["b_sb2c"] = np.ascontiguousarray(np.asarray(inputs["L0_ffn_sb2"]).reshape(NCH, 128).T)
    shared["b_b1c"] = np.ascontiguousarray(np.asarray(inputs["L0_ffn_b1"]).reshape(8, NCH, 128).transpose(0, 2, 1))
    shared["b_b2r"] = np.ascontiguousarray(np.asarray(inputs["L0_ffn_b2"]).reshape(8, 1, D))
    shared["b_temb"] = np.ascontiguousarray(np.asarray(inputs["L0_ffn_t_emb"]))
    shared["c_wq"] = np.ascontiguousarray(np.asarray(inputs["L1_attn_Wq"]).reshape(NCH, 128, D))
    shared["c_wk"] = np.ascontiguousarray(np.asarray(inputs["L1_attn_Wk"]).reshape(NCH, 128, D))
    shared["c_wv"] = np.ascontiguousarray(np.asarray(inputs["L1_attn_Wv"]).reshape(NCH, 128, D))
    shared["c_wo"] = np.ascontiguousarray(np.asarray(inputs["L1_attn_Wo"]).reshape(NCH, 128, D))
    shared["c_bo"] = np.ascontiguousarray(np.asarray(inputs["L1_attn_bo"]).reshape(NCH, 128).T)
    shared["d_wg"] = np.ascontiguousarray(np.asarray(inputs["L1_ffn_w_gate"]).reshape(NNC, 128, D))
    shared["d_wu"] = np.ascontiguousarray(np.asarray(inputs["L1_ffn_w_up"]).reshape(NNC, 128, D))
    shared["d_wd"] = np.ascontiguousarray(np.asarray(inputs["L1_ffn_w_down"]).reshape(NCH, 128, IH))
    shared.update(const_union())
    in_maps = []
    for core in range(n_cores):
        m = dict(shared)
        m["a_xin"] = np.ascontiguousarray(x[core * NSEQ:(core + 1) * NSEQ])
        in_maps.append(m)
    return in_maps


def kernel(**inputs):
    n_cores = 8
    nc = build_all()
    in_maps = prep_all(inputs, n_cores=n_cores)
    res = run_bass_kernel_spmd(nc, in_maps, core_ids=list(range(n_cores)))
    out = np.empty((16, T, D), dtype=np.float32)
    for core in range(n_cores):
        xo = np.asarray(res.results[core]["xout"])
        out[core * 2] = xo[0]
        out[core * 2 + 1] = xo[1]
    return out
